# version 08
# baseline (speedup 1.0000x reference)
import numpy as np
from contextlib import ExitStack

import concourse.bass as bass
import concourse.mybir as mybir
from concourse.bass_utils import run_bass_kernel_spmd

F32 = mybir.dt.float32
BF16 = mybir.dt.bfloat16
I32 = mybir.dt.int32
AF = mybir.ActivationFunctionType
OP = mybir.AluOpType
AX = mybir.AxisListType

ENGS = ("sync", "scalar", "vector", "tensor", "gpsimd")
SCALE = 1.0 / np.sqrt(128.0)


class Sem:
    def __init__(self, h, name, dma=False):
        self.h = h
        self.name = name
        self.n = 0
        self.dma = dma


class Prog:
    def __init__(self, nc, es):
        self.nc = nc
        self.es = es
        self.items = {e: [] for e in ENGS}

    def sem(self, name, dma=False):
        return Sem(self.es.enter_context(self.nc.semaphore(name)), name, dma=dma)

    def wait(self, eng, tok):
        if tok is None:
            return
        sem, val = tok
        if sem.dma:
            val = sem.n
        self.items[eng].append(("w", sem, val))

    def waits(self, eng, toks):
        for t in toks:
            self.wait(eng, t)

    def op(self, eng, fn, sem, k=1):
        sem.n += k
        self.items[eng].append(("o", fn, [(sem, k)]))
        return (sem, sem.n)

    def finalize(self, block, strict=False):
        for ename in ENGS:
            items = self.items[ename]

            def body(eng, items=items):
                seen = {}
                for it in items:
                    if it[0] == "w":
                        eng.wait_ge(it[1].h, it[2])
                    else:
                        if strict:
                            for s, k in it[2]:
                                if not s.dma and s in seen:
                                    eng.wait_ge(s.h, seen[s])
                        ins = it[1](eng)
                        for s, k in it[2]:
                            seen[s] = seen.get(s, 0) + k
                            ins.then_inc(s.h, k)

            getattr(block, ename)(body)


class L0Builder:
    def __init__(self, cfg):
        self.cfg = cfg
        self.NB = cfg.get("NB", 2)
        self.L = cfg.get("L", 2048)
        self.TC = 256
        self.NCH = self.L // self.TC
        self.DM = 2048
        self.NDM = 16
        self.DI = 4096
        self.NDI = 32
        self.S = 16
        self.taps = dict(cfg.get("taps", []))

    def build(self, env=None):
        if env is None:
            nc = bass.Bass("TRN2", target_bir_lowering=False)
            es = ExitStack()
            p = Prog(nc, es)
        else:
            nc, es, p = env["nc"], env["es"], env["p"]
        self.env = env
        self.nc, self.es, self.p = nc, es, p
        NB, L, TC, NCH = self.NB, self.L, self.TC, self.NCH
        DM, NDM, DI, NDI, S = self.DM, self.NDM, self.DI, self.NDI, self.S

        self.x_bdt = nc.dram_tensor("x_bdt", [NB, DM, L], F32, kind="ExternalInput").ap()
        self.w_in = nc.dram_tensor("w_in", [DM, 2 * DI], F32, kind="ExternalInput").ap()
        self.conv_w_p = nc.dram_tensor("conv_w_p", [128, NDI * 3], F32, kind="ExternalInput").ap()
        self.conv_b_p = nc.dram_tensor("conv_b_p", [128, NDI], F32, kind="ExternalInput").ap()
        self.w_x = nc.dram_tensor("w_x", [DI, 160], F32, kind="ExternalInput").ap()
        self.w_dt = nc.dram_tensor("w_dt", [128, DI], F32, kind="ExternalInput").ap()
        self.b_dt_p = nc.dram_tensor("b_dt_p", [128, NDI], F32, kind="ExternalInput").ap()
        self.a_log_p = nc.dram_tensor("a_log_p", [128, NDI * S], F32, kind="ExternalInput").ap()
        self.d_p = nc.dram_tensor("d_p", [128, NDI], F32, kind="ExternalInput").ap()
        self.w_out = nc.dram_tensor("w_out", [DI, DM], F32, kind="ExternalInput").ap()
        self.l0w_t = nc.dram_tensor("l0w_t", [DM, DM], F32, kind="ExternalInput").ap()
        self.l0b_p = nc.dram_tensor("l0b_p", [128, NDM], F32, kind="ExternalInput").ap()
        if env is None:
            self.x1_bdt = nc.dram_tensor("x1_bdt", [NB, DM, L], F32, kind="ExternalOutput").ap()
        else:
            self.x1_bdt = env["x1_ap"]
        self.qt_dram = nc.dram_tensor("qt_scratch", [DM, DM], BF16).ap()
        self.win_bf = nc.dram_tensor("win_bf", [DM, 2 * DI], BF16).ap()
        self.wout_bf = nc.dram_tensor("wout_bf", [DI, DM], BF16).ap()
        self.bc_dram = nc.dram_tensor("bc_scratch", [2 * S, TC], F32).ap()
        self.xc_dram = nc.dram_tensor("xc_scratch", [4096, 256], F32).ap()
        self.tap_outs = {}
        for tname, tshape in self.taps.items():
            dt = F32
            self.tap_outs[tname] = nc.dram_tensor("tap_" + tname, list(tshape), dt, kind="ExternalOutput").ap()

        if env is None:
            def sb(name, shape, dt):
                return es.enter_context(nc.sbuf_tensor(name, shape, dt))
        else:
            sb = env["alloc"]

        self.x0buf = sb("x0buf", [128, NDM * 258], F32)
        self.xnbuf = sb("xnbuf", [128, NDM * 258], BF16)
        self.winbuf = sb("winbuf", [128, NDM * 256], BF16)
        self.xpslot = sb("xpslot", [128, 2 * 258], F32)
        self.sqslot = sb("sqslot", [128, 2 * 258], F32)
        self.big1 = sb("big1", [128, NDI * 258], F32)
        self.xcraw = self.big1[:, 0:NDI * 258]
        self.zraw = sb("zraw", [128, NDI * TC], F32)
        self.xcring = sb("xcring", [128, 2 * 256], F32)
        self.xcbf = sb("xcbf", [128, 2 * TC], BF16)
        self.statrow = sb("statrow", [1, 4 * 258], F32)
        self.dtinbf = sb("dtinbf", [128, TC], BF16)
        self.bcsb = sb("bcsb", [2 * S, TC], F32)
        self.Bb = self.big1[:, 0:S * TC]
        self.Cb = self.big1[:, S * TC:2 * S * TC]
        self.dtslot = sb("dtslot", [128, 2 * TC], F32)
        self.spslot = sb("spslot", [128, 2 * TC], F32)
        self.dtuslot = sb("dtuslot", [128, 2 * TC], F32)
        self.dAslot = sb("dAslot", [128, 4 * TC], F32)
        self.xsslot = sb("xsslot", [128, 4 * TC], F32)
        self.hbuf = sb("hbuf", [128, S * TC], F32)
        self.tmpslot = sb("tmpslot", [128, 4 * TC], BF16)
        self.ygbuf = sb("ygbuf", [128, NDI * TC], BF16)
        self.yg1slot = sb("yg1slot", [128, 2 * TC], F32)
        self.woutsl = sb("woutsl", [128, NDI * 128], BF16)
        self.qsl = sb("qsl", [128, NDM * 128], BF16)
        self.x1slot = sb("x1slot", [128, 2 * 256], F32)
        self.hstate = sb("hstate", [128, NB * 512], F32)
        self.wxbuf = sb("wxbuf", [128, NDI * 160], BF16)
        self.wdtbuf = sb("wdtbuf", [128, DI], BF16)
        self.Abuf = sb("Abuf", [128, NDI * S], F32)
        self.cwbuf = sb("cwbuf", [128, NDI * 3], F32)
        self.cbbuf = sb("cbbuf", [128, NDI], F32)
        self.bdtbuf = sb("bdtbuf", [128, NDI], F32)
        self.Dbuf = sb("Dbuf", [128, NDI], F32)
        self.l0bbuf = sb("l0bbuf", [128, NDM], F32)
        self.identbf = sb("identbf", [128, 128], BF16)
        self.pidx = sb("pidx", [128, 1], F32)
        self.fidx = sb("fidx", [128, 128], F32)
        self.ones_col = sb("ones_col", [128, 1], F32)
        self.ones_row = sb("ones_row", [1, 128], F32)
        self.epscol = sb("epscol", [128, 1], F32)
        self.onecol = sb("onecol", [128, 1], F32)
        self.qtile = self.zraw[:, 0:1024].bitcast(BF16)
        self.throw = self.zraw[0:1, 1024:3072]
        self.wtile = self.x0buf[:, 0:2048]
        self.awtile = self.x0buf[:, 2048:4096]
        self.sgtile = self.big1[:, 0:2048]
        self.thrbc = self.big1[:, 2048:4096]
        self.pb = env["pb"] if env is not None else [
            es.enter_context(nc.psum_tensor("pbk%d" % i, [128, 512], F32)) for i in range(8)]

        self.x0v = self.x0buf[:, :].rearrange("p (c t) -> p c t", t=258)
        self.xnv = self.xnbuf[:, :].rearrange("p (c t) -> p c t", t=258)
        self.winv = self.winbuf[:, :].rearrange("p (c o) -> p c o", o=256)
        self.xcv = self.xcraw.rearrange("p (c t) -> p c t", t=258)
        self.zv = self.zraw[:, :].rearrange("p (c t) -> p c t", t=TC)
        self.Bv = self.Bb.rearrange("p (s t) -> p s t", t=TC)
        self.Cv = self.Cb.rearrange("p (s t) -> p s t", t=TC)
        self.hvv = self.hbuf[:, :].rearrange("p (s t) -> p s t", t=TC)
        self.ygv = self.ygbuf[:, :].rearrange("p (c t) -> p c t", t=TC)
        self.wov = self.woutsl[:, :].rearrange("p (c o) -> p c o", o=128)
        self.qv = self.qsl[:, :].rearrange("p (c o) -> p c o", o=128)
        self.wxv = self.wxbuf[:, :].rearrange("p (c r) -> p c r", r=160)
        self.cwv = self.cwbuf[:, :].rearrange("p (c k) -> p c k", k=3)

        self.dsem = {n: p.sem("D_" + n, dma=True) for n in
                     ("x0", "res", "wtile", "qst", "q", "bc", "xcsp", "xcld0", "xcld1",
                      "x1st", "win", "wo", "tap", "wres", "wcst", "wci")}
        self.PE = p.sem("PE")
        self.DV = p.sem("DV")
        self.AC = p.sem("AC")
        self.PL = p.sem("PL")
        self.esem = {"scalar": self.AC, "vector": self.DV, "tensor": self.PE, "gpsimd": self.PL}

        self.evac_tok = {0: None, 1: None}
        self.resid_tok = {2: None, 3: None}
        self.dt_done = {4: None, 5: None}
        self.y_tok = {6: None, 7: None}
        self.x1_st = {0: None, 1: None}
        self.scan_tok = {}
        self.tmp_pe_tok = {}
        self.tmp_tok = {}
        self.st_prev = None
        self.wout_used = None
        self.q_used = None
        self.bb_used = None
        self.dtin_used = None

        self.stage = self.cfg.get("stage", "full")
        self.prologue()
        if self.stage != "prologue":
            nbl = self.cfg.get("NBL", NB)
            nchl = self.cfg.get("NCHL", NCH)
            for b in range(nbl):
                for k in range(nchl):
                    self.chunk(b, k)

        for sm in self.dsem.values():
            p.wait("sync", (sm, sm.n))
        p.wait("gpsimd", (self.dsem["tap"], self.dsem["tap"].n))

        if env is not None:
            return nc
        block = es.enter_context(nc.Block())
        with es:
            p.finalize(block, strict=self.cfg.get("strict", False))
        return nc

    def dma(self, q, stream, out, in_, **kw):
        sem = self.dsem[stream]
        if q == "gpsimd" and "max_dma_last_dim" not in kw:
            kw = dict(kw, max_dma_last_dim=4096)
        return self.p.op(q, lambda e, out=out, in_=in_, kw=kw: e.dma_start(out=out, in_=in_, **kw), sem, 16)

    def op(self, eng, fn):
        return self.p.op(eng, fn, self.esem[eng], 1)

    def tap(self, name, src_ap, after_toks):
        if name in self.tap_outs:
            self.p.waits("gpsimd", after_toks)
            self.dma("gpsimd", "tap", self.tap_outs[name][:, :], src_ap)

    def prologue(self):
        p, op, dma = self.p, self.op, self.dma
        NDM = self.NDM
        t_io = op("gpsimd", lambda e: e.iota(self.pidx[:, :], [[0, 1]], base=0, channel_multiplier=1,
                                             allow_small_or_imprecise_dtypes=True))
        t_io2 = op("gpsimd", lambda e: e.iota(self.fidx[:, :], [[1, 128]], base=0, channel_multiplier=0,
                                              allow_small_or_imprecise_dtypes=True))
        t_oc = op("gpsimd", lambda e: e.memset(self.ones_col[:, :], 1.0))
        t_or = op("gpsimd", lambda e: e.memset(self.ones_row[:, :], 1.0))
        t_ec = op("gpsimd", lambda e: e.memset(self.epscol[:, :], 1e-5))
        t_on = op("gpsimd", lambda e: e.memset(self.onecol[:, :], 1.0))
        t_hs = op("gpsimd", lambda e: e.memset(self.hstate[:, :], 0.0))
        t_bz = op("gpsimd", lambda e: e.memset(self.big1[:, :], 0.0))
        p.waits("vector", [t_io, t_io2])
        t_id = op("vector", lambda e: e.tensor_scalar(self.identbf[:, :], self.fidx[:, :], self.pidx[:, :],
                                                      None, op0=OP.is_equal))
        t_wx = dma("gpsimd", "wres", self.wxv[:, :, :], self.w_x[:, :].rearrange("(c p) r -> p c r", p=128))
        wcast = self.x0buf[:, :].bitcast(BF16)
        t_pc = None
        for c in range(16):
            if t_pc is not None:
                p.wait("gpsimd", t_pc)
            t_c = dma("gpsimd", "wci", wcast[:, 0:8192], self.w_in[c * 128:(c + 1) * 128, :])
            p.wait("sync", t_c)
            t_pc = dma("sync", "wcst", self.win_bf[c * 128:(c + 1) * 128, :], wcast[:, 0:8192])
        for c in range(32):
            p.wait("gpsimd", t_pc)
            t_c = dma("gpsimd", "wci", wcast[:, 0:2048], self.w_out[c * 128:(c + 1) * 128, :])
            p.wait("sync", t_c)
            t_pc = dma("sync", "wcst", self.wout_bf[c * 128:(c + 1) * 128, :], wcast[:, 0:2048])
        self.wcast_done = t_pc
        p.wait("sync", self.wcast_done)
        t_wdt = dma("gpsimd", "wres", self.wdtbuf[:, :], self.w_dt[:, :])
        t_cw = dma("sync", "res", self.cwbuf[:, :], self.conv_w_p[:, :])
        t_cb = dma("sync", "res", self.cbbuf[:, :], self.conv_b_p[:, :])
        t_bdt = dma("sync", "res", self.bdtbuf[:, :], self.b_dt_p[:, :])
        t_dp = dma("sync", "res", self.Dbuf[:, :], self.d_p[:, :])
        t_l0b = dma("sync", "res", self.l0bbuf[:, :], self.l0b_p[:, :])
        t_al = dma("sync", "res", self.Abuf[:, :], self.a_log_p[:, :])
        p.wait("scalar", t_al)
        t_a1 = op("scalar", lambda e: e.activation(out=self.Abuf[:, :], in_=self.Abuf[:, :], func=AF.Exp))
        p.wait("vector", t_a1)
        t_a2 = op("vector", lambda e: e.tensor_scalar(self.Abuf[:, :], self.Abuf[:, :], -1.0, None, op0=OP.mult))

        p.waits("scalar", [t_bz, t_hs])
        p.waits("vector", [t_bz])
        tok_mm = None
        tok_aw_reader = None
        prev_aw = None
        for i in range(NDM):
            if prev_aw is not None:
                p.wait("sync", prev_aw)
            t_w = dma("sync", "wtile", self.wtile, self.l0w_t[i * 128:(i + 1) * 128, :])
            p.wait("scalar", t_w)
            if tok_aw_reader is not None:
                p.wait("scalar", tok_aw_reader)
            t_aw = op("scalar", lambda e: e.activation(out=self.awtile, in_=self.wtile, func=AF.Abs))
            prev_aw = t_aw
            p.wait("tensor", t_aw)
            if i == 0:
                p.wait("tensor", t_oc)
            for g in range(4):
                tok_mm = op("tensor", lambda e, g=g, i=i: e.matmul(
                    self.pb[g][0:1, :], self.ones_col[:, :], self.awtile[:, g * 512:(g + 1) * 512],
                    start=(i == 0), stop=(i == NDM - 1)))
            tok_aw_reader = tok_mm
        p.wait("vector", tok_mm)
        for g in range(4):
            op("vector", lambda e, g=g: e.tensor_copy(self.throw[:, g * 512:(g + 1) * 512], self.pb[g][0:1, :]))
        op("vector", lambda e: e.tensor_scalar(self.throw, self.throw, 1.0 / self.DM, None, op0=OP.mult))
        t_tr = op("vector", lambda e: e.tensor_scalar(self.throw, self.throw, 0.7, None, op0=OP.mult))
        p.waits("tensor", [t_tr, t_or])
        t_bc = None
        for g in range(4):
            t_bc = op("tensor", lambda e, g=g: e.matmul(
                self.pb[g][:, :], self.ones_row[:, :], self.throw[:, g * 512:(g + 1) * 512], start=True, stop=True))
        p.wait("vector", t_bc)
        t_thbc = None
        for g in range(4):
            t_thbc = op("vector", lambda e, g=g: e.tensor_copy(self.thrbc[:, g * 512:(g + 1) * 512], self.pb[g][:, :]))
        t_qst = None
        p.waits("scalar", [tok_mm, t_bc])
        for i in range(NDM):
            p.wait("sync", prev_aw)
            t_w = dma("sync", "wtile", self.wtile, self.l0w_t[i * 128:(i + 1) * 128, :])
            p.wait("scalar", t_w)
            if t_qst is not None:
                p.wait("scalar", t_qst)
            t_sg = op("scalar", lambda e: e.activation(out=self.sgtile, in_=self.wtile, func=AF.Sign))
            t_aw = op("scalar", lambda e: e.activation(out=self.awtile, in_=self.wtile, func=AF.Abs))
            prev_aw = t_aw
            p.waits("vector", [t_sg, t_aw, t_thbc])
            op("vector", lambda e: e.tensor_tensor(self.awtile, self.awtile, self.thrbc, op=OP.is_gt))
            t_q = op("vector", lambda e: e.tensor_tensor(self.qtile, self.awtile, self.sgtile, op=OP.mult))
            self.p.wait("sync", t_q)
            t_qst = self.dma("sync", "qst", self.qt_dram[i * 128:(i + 1) * 128, :], self.qtile)
        self.q_prep_done = t_qst

        p.waits("vector", [t_wx, t_wdt, t_cw, t_cb, t_bdt, t_dp, t_l0b, t_a2, t_id, t_hs, t_bz])
        p.waits("scalar", [t_a2, t_bdt, t_cb, t_ec, t_on])
        p.waits("tensor", [t_id, t_wx, t_wdt])
        self.evac_tok[0] = self.evac_tok[1] = t_thbc
        self.resid_tok[2] = self.resid_tok[3] = t_thbc

    def chunk(self, b, k):
        p, op, dma = self.p, self.op, self.dma
        NDM, NDI, S, TC = self.NDM, self.NDI, self.S, self.TC
        pb = self.pb
        t0 = k * self.TC
        first, last = (k == 0), (k == self.NCH - 1)

        if self.x1_st[0] is not None:
            p.waits("sync", [self.x1_st[0], self.x1_st[1]])
        p.wait("sync", self.wcast_done)
        p.wait("scalar", (self.DV, self.DV.n))
        p.wait("vector", (self.AC, self.AC.n))
        p.wait("gpsimd", (self.DV, self.DV.n))
        p.wait("vector", (self.dsem["qst"], self.dsem["qst"].n))
        if first:
            t_ms = op("gpsimd", lambda e: e.memset(self.x0v[:, :, 0:1], 0.0))
            t_x0 = dma("sync", "x0", self.x0v[:, :, 1:258],
                       self.x_bdt[b, :, 0:257].rearrange("(c p) t -> p c t", p=128))
            x0toks = [t_ms, t_x0]
        elif last:
            t_ms = op("gpsimd", lambda e: e.memset(self.x0v[:, :, 257:258], 0.0))
            t_x0 = dma("sync", "x0", self.x0v[:, :, 0:257],
                       self.x_bdt[b, :, t0 - 1:t0 + 256].rearrange("(c p) t -> p c t", p=128))
            x0toks = [t_ms, t_x0]
        else:
            t_x0 = dma("sync", "x0", self.x0v[:, :, :],
                       self.x_bdt[b, :, t0 - 1:t0 + 257].rearrange("(c p) t -> p c t", p=128))
            x0toks = [t_x0]

        p.waits("scalar", x0toks)
        p.waits("tensor", x0toks + [self.resid_tok[2], self.resid_tok[3]])
        sq_tok = {}
        mm_x = mm_sq = None
        for c in range(NDM):
            sl = self.sqslot[:, (c % 2) * 258:(c % 2) * 258 + 258]
            if c >= 2:
                p.wait("scalar", sq_tok[c - 2])
            t_sq = op("scalar", lambda e, c=c, sl=sl: e.activation(out=sl, in_=self.x0v[:, c, :], func=AF.Square))
            p.wait("tensor", t_sq)
            mm_x = op("tensor", lambda e, c=c: e.matmul(pb[2][0:1, 0:258], self.ones_col[:, :], self.x0v[:, c, :],
                                                        start=(c == 0), stop=(c == NDM - 1)))
            mm_sq = op("tensor", lambda e, sl=sl, c=c: e.matmul(pb[3][0:1, 0:258], self.ones_col[:, :], sl,
                                                               start=(c == 0), stop=(c == NDM - 1)))
            sq_tok[c] = mm_sq
        p.waits("vector", [mm_x, mm_sq])
        mrow = self.statrow[0:1, 0:258]
        exrow = self.statrow[0:1, 258:516]
        varrow = self.statrow[0:1, 516:774]
        rrow = self.statrow[0:1, 774:1032]
        op("vector", lambda e: e.tensor_scalar(mrow, pb[2][0:1, 0:258], 1.0 / self.DM, None, op0=OP.mult))
        op("vector", lambda e: e.tensor_scalar(exrow, pb[3][0:1, 0:258], 1.0 / self.DM, None, op0=OP.mult))
        op("vector", lambda e: e.tensor_tensor(varrow, mrow, mrow, op=OP.mult))
        t_var = op("vector", lambda e: e.tensor_tensor(varrow, exrow, varrow, op=OP.subtract))
        p.wait("scalar", t_var)
        t_ln = op("scalar", lambda e: e.activation(out=rrow, in_=varrow, func=AF.Ln, bias=self.epscol[0:1, 0:1]))
        p.wait("scalar", t_ln)
        t_rstd = op("scalar", lambda e: e.activation(out=rrow, in_=rrow, func=AF.Exp, scale=-0.5))
        p.waits("tensor", [t_var, t_rstd])
        t_bm = op("tensor", lambda e: e.matmul(pb[2][:, 0:258], self.ones_row[:, :], mrow, start=True, stop=True))
        t_br = op("tensor", lambda e: e.matmul(pb[3][:, 0:258], self.ones_row[:, :], rrow, start=True, stop=True))
        p.waits("vector", [t_bm, t_br])
        xn_tok = None
        for c in range(NDM):
            sl = self.sqslot[:, (c % 2) * 258:(c % 2) * 258 + 258]
            op("vector", lambda e, c=c, sl=sl: e.tensor_tensor(sl, self.x0v[:, c, :], pb[2][:, 0:258], op=OP.subtract))
            xn_tok = op("vector", lambda e, c=c, sl=sl: e.tensor_tensor(self.xnv[:, c, :], sl, pb[3][:, 0:258], op=OP.mult))
        self.tap("xn_b%dk%d" % (b, k), self.xnbuf[:, :], [xn_tok])

        if self.stage == "ln":
            return
        win_used = None
        conv_last = None
        for og in range(16):
            if win_used is not None:
                p.wait("sync", win_used)
            t_win = dma("sync", "win", self.winv[:, :, :],
                        self.win_bf[:, og * 256:(og + 1) * 256].rearrange("(c p) o -> p c o", p=128))
            p.waits("tensor", [t_win, xn_tok])
            for oc in range(2):
                j = og * 2 + oc
                bk = j % 2
                p.wait("tensor", self.evac_tok[bk])
                mm = None
                for c in range(NDM):
                    mm = op("tensor", lambda e, c=c, oc=oc, bk=bk: e.matmul(
                        pb[bk][:, 0:258], self.winv[:, c, oc * 128:(oc + 1) * 128], self.xnv[:, c, :],
                        start=(c == 0), stop=(c == NDM - 1)))
                win_used = mm
                p.wait("scalar", mm)
                sl = self.xpslot[:, (j % 2) * 258:(j % 2) * 258 + 258]
                t_xpev = op("scalar", lambda e, sl=sl, bk=bk: e.activation(out=sl, in_=pb[bk][:, 0:258], func=AF.Copy))
                p.wait("vector", t_xpev)
                op("vector", lambda e, sl=sl, j=j: e.tensor_scalar(
                    self.xcv[:, j, 0:256], sl[:, 0:256], self.cwv[:, j, 0:1], self.cbbuf[:, j:j + 1],
                    op0=OP.mult, op1=OP.add))
                op("vector", lambda e, sl=sl, j=j: e.scalar_tensor_tensor(
                    self.xcv[:, j, 0:256], sl[:, 1:257], self.cwv[:, j, 1:2], self.xcv[:, j, 0:256],
                    op0=OP.mult, op1=OP.add))
                t_cv = op("vector", lambda e, sl=sl, j=j: e.scalar_tensor_tensor(
                    self.xcv[:, j, 0:256], sl[:, 2:258], self.cwv[:, j, 2:3], self.xcv[:, j, 0:256],
                    op0=OP.mult, op1=OP.add))
                self.evac_tok[bk] = t_cv
                conv_last = t_cv
        z_last = None
        for og in range(16, 32):
            p.wait("sync", win_used)
            t_win = dma("sync", "win", self.winv[:, :, :],
                        self.win_bf[:, og * 256:(og + 1) * 256].rearrange("(c p) o -> p c o", p=128))
            p.wait("tensor", t_win)
            for oc in range(2):
                j = (og - 16) * 2 + oc
                bk = j % 2
                p.wait("tensor", self.evac_tok[bk])
                mm = None
                for c in range(NDM):
                    mm = op("tensor", lambda e, c=c, oc=oc, bk=bk: e.matmul(
                        pb[bk][:, 0:256], self.winv[:, c, oc * 128:(oc + 1) * 128], self.xnv[:, c, 1:257],
                        start=(c == 0), stop=(c == NDM - 1)))
                win_used = mm
                p.wait("scalar", mm)
                t_z = op("scalar", lambda e, j=j, bk=bk: e.activation(out=self.zv[:, j, :], in_=pb[bk][:, 0:256], func=AF.Copy))
                self.evac_tok[bk] = t_z
                z_last = t_z
        if self.stage == "xz":
            self.x1_st = {0: None, 1: None}
            return
        p.waits("scalar", [conv_last, z_last])
        t_silu_xc = op("scalar", lambda e: e.activation(out=self.xcraw, in_=self.xcraw, func=AF.Silu))
        t_silu_z = op("scalar", lambda e: e.activation(out=self.zraw[:, :], in_=self.zraw[:, :], func=AF.Silu))
        self.tap("xc_b%dk%d" % (b, k), self.xcraw, [t_silu_xc])
        self.tap("z_b%dk%d" % (b, k), self.zraw[:, :], [t_silu_z])

        if self.stage == "silu":
            return
        p.wait("sync", t_silu_xc)
        t_xcsp = dma("sync", "xcsp", self.xc_dram[:, :].rearrange("(c p) t -> p c t", p=128),
                     self.xcv[:, :, 0:256])
        p.wait("sync", t_xcsp)
        xdbl_pe = {0: None, 1: None}
        mm2 = None
        xc_rd = {0: None, 1: None}
        if self.dtin_used is not None:
            p.waits("tensor", self.dtin_used)
        if self.dt_done[5] is not None:
            p.waits("tensor", self.dt_done[5])
        for j in range(NDI):
            csl = self.xcbf[:, (j % 2) * TC:(j % 2) * TC + TC]
            rsl = self.xcring[:, (j % 2) * 256:(j % 2) * 256 + 256]
            p.wait("sync", xc_rd[j % 2])
            t_ld = dma("sync", "xcld%d" % (j % 2), rsl, self.xc_dram[j * 128:(j + 1) * 128, :])
            p.wait("vector", xdbl_pe[j % 2])
            p.wait("vector", t_ld)
            t_c16 = op("vector", lambda e, csl=csl, rsl=rsl: e.tensor_copy(csl, rsl))
            xc_rd[j % 2] = t_c16
            p.wait("tensor", t_c16)
            op("tensor", lambda e, j=j, csl=csl: e.matmul(
                pb[4][:, 0:256], self.wxv[:, j, 0:128], csl, start=(j == 0), stop=(j == NDI - 1)))
            mm2 = op("tensor", lambda e, j=j, csl=csl: e.matmul(
                pb[5][0:32, 0:256], self.wxv[:, j, 128:160], csl, start=(j == 0), stop=(j == NDI - 1)))
            xdbl_pe[j % 2] = mm2
        p.wait("vector", mm2)
        t_dtin = op("vector", lambda e: e.tensor_copy(self.dtinbf[:, :], pb[4][:, 0:256]))
        t_bcs = op("vector", lambda e: e.tensor_copy(self.bcsb[:, :], pb[5][0:32, 0:256]))
        self.tap("bc_b%dk%d" % (b, k), self.bcsb[:, :], [t_bcs])
        p.waits("sync", [t_bcs, (self.DV, self.DV.n)])
        t_bcd = dma("sync", "bc", self.bc_dram[:, :], self.bcsb[:, :])
        p.wait("sync", t_bcd)
        tBC = None
        for s in range(S):
            dma("sync", "bc", self.Bv[:, s, :], self.bc_dram[s:s + 1, :].broadcast_to((128, TC)))
            tBC = dma("sync", "bc", self.Cv[:, s, :], self.bc_dram[S + s:S + s + 1, :].broadcast_to((128, TC)))

        if self.stage == "xdbl":
            return
        yg_tok = None
        xc_scan_rd = {0: xc_rd[0], 1: xc_rd[1]}
        for j in range(self.cfg.get("NJ", NDI)):
            dsl = self.dtslot[:, (j % 2) * TC:(j % 2) * TC + TC]
            ssl = self.spslot[:, (j % 2) * TC:(j % 2) * TC + TC]
            usl = self.dtuslot[:, (j % 2) * TC:(j % 2) * TC + TC]
            dbk = 5 if j % 2 == 0 else 4
            if j < 2:
                p.wait("tensor", t_dtin)
                p.wait("tensor", t_bcs)
            if self.dt_done[dbk] is not None:
                p.waits("tensor", self.dt_done[dbk])
            t_mmdt = op("tensor", lambda e, j=j, dbk=dbk: e.matmul(
                pb[dbk][:, 0:256], self.wdtbuf[:, j * 128:(j + 1) * 128], self.dtinbf[:, :], start=True, stop=True))
            njop = self.cfg.get("NJOP", "all")
            if njop == "mm":
                self.dt_done[dbk] = [t_mmdt]
                continue
            p.wait("scalar", t_mmdt)
            t_av = op("scalar", lambda e, j=j, dbk=dbk, ssl=ssl: e.activation(
                out=ssl, in_=pb[dbk][:, 0:256], func=AF.Abs, bias=self.bdtbuf[:, j:j + 1]))
            if njop == "av":
                self.dt_done[dbk] = [t_av]
                continue
            p.wait("scalar", t_av)
            t_e1 = op("scalar", lambda e, ssl=ssl: e.activation(out=ssl, in_=ssl, func=AF.Exp, scale=-1.0))
            if njop == "e1":
                self.dt_done[dbk] = [t_e1]
                continue
            p.wait("scalar", t_e1)
            t_l = op("scalar", lambda e, ssl=ssl: e.activation(out=ssl, in_=ssl, func=AF.Ln, bias=self.onecol[:, 0:1]))
            if njop == "l":
                self.dt_done[dbk] = [t_l]
                continue
            p.wait("vector", t_mmdt)
            p.wait("vector", t_av)
            t_m = op("vector", lambda e, j=j, dbk=dbk, dsl=dsl: e.tensor_scalar(
                dsl, pb[dbk][:, 0:256], self.bdtbuf[:, j:j + 1], 0.0, op0=OP.add, op1=OP.max))
            if njop == "m":
                self.dt_done[dbk] = [t_av, t_m]
                continue
            p.wait("vector", t_l)
            t_dt = op("vector", lambda e, dsl=dsl, ssl=ssl: e.tensor_tensor(dsl, dsl, ssl, op=OP.add))
            self.dt_done[dbk] = [t_av, t_m]
            if njop == "dt":
                continue
            rsl = self.xcring[:, (j % 2) * 256:(j % 2) * 256 + 256]
            if self.stage != "scan_sp":
                p.wait("sync", xc_scan_rd[j % 2])
                t_ldxc = dma("sync", "xcld%d" % (j % 2), rsl, self.xc_dram[j * 128:(j + 1) * 128, :])
                p.wait("vector", t_ldxc)
                t_dtu = op("vector", lambda e, usl=usl, rsl=rsl, dsl=dsl: e.tensor_tensor(
                    usl, dsl, rsl, op=OP.mult))
            if j == 0:
                p.wait("scalar", tBC)
                p.wait("vector", tBC)
            if j == 0 and b == 0 and k == 0:
                self.tap("dt_b0k0", dsl, [t_dt])
            if self.stage in ("scan_dt", "scan_sp"):
                continue
            ybk = 6 if j % 2 == 0 else 7
            p.wait("tensor", self.y_tok[ybk])
            t_y = None
            for s in range(S):
                asl = self.dAslot[:, (s % 4) * TC:(s % 4) * TC + TC]
                xsl = self.xsslot[:, (s % 4) * TC:(s % 4) * TC + TC]
                tsl = self.tmpslot[:, (s % 4) * TC:(s % 4) * TC + TC]
                prev = (s - 4) % 16
                if prev in self.scan_tok:
                    p.wait("scalar", self.scan_tok[prev])
                    p.wait("gpsimd", self.scan_tok[prev])
                    p.wait("gpsimd", self.tmp_pe_tok.get(prev))
                p.wait("scalar", t_dt)
                t_dA = op("scalar", lambda e, asl=asl, dsl=dsl, j=j, s=s: e.activation(
                    out=asl, in_=dsl, func=AF.Exp, scale=self.Abuf[:, j * S + s:j * S + s + 1]))
                p.waits("vector", [t_dA, t_dtu])
                t_xs = op("vector", lambda e, xsl=xsl, usl=usl, s=s: e.tensor_tensor(
                    xsl, usl, self.Bv[:, s, :], op=OP.mult))
                p.wait("vector", self.st_prev)
                if self.stage == "scan_noh":
                    continue
                p.wait("vector", t_xs)
                p.wait("vector", t_dA)
                p.wait("vector", self.tmp_tok.get(s))
                t_h = op("vector", lambda e, s=s, asl=asl, xsl=xsl, j=j, b=b: e.tensor_tensor_scan(
                    self.hvv[:, s, :], asl, xsl,
                    self.hstate[:, b * 512 + j * 16 + s:b * 512 + j * 16 + s + 1],
                    op0=OP.mult, op1=OP.add))
                self.scan_tok[s] = t_h
                if self.stage == "scan_noy":
                    continue
                p.wait("gpsimd", t_h)
                t_tmp = op("gpsimd", lambda e, s=s, tsl=tsl: e.tensor_tensor(
                    tsl, self.hvv[:, s, :], self.Cv[:, s, :], op=OP.mult))
                self.tmp_tok[s] = t_tmp
                p.wait("tensor", t_tmp)
                t_y = op("tensor", lambda e, s=s, tsl=tsl, ybk=ybk: e.matmul(
                    pb[ybk][:, 0:256], self.identbf[:, :], tsl, start=(s == 0), stop=(s == S - 1)))
                self.tmp_pe_tok[s] = t_y
            if self.stage in ("scan_noh", "scan_noy"):
                continue
            t_st = op("vector", lambda e, j=j, b=b: e.tensor_copy(
                self.hstate[:, b * 512 + j * 16:b * 512 + j * 16 + 16], self.hvv[:, :, 255:256]))
            self.st_prev = t_st
            p.waits("vector", [t_y, t_silu_z])
            gsl = self.yg1slot[:, (j % 2) * TC:(j % 2) * TC + TC]
            t_g1 = op("vector", lambda e, gsl=gsl, rsl=rsl, j=j, ybk=ybk: e.scalar_tensor_tensor(
                gsl, rsl, self.Dbuf[:, j:j + 1], pb[ybk][:, 0:256], op0=OP.mult, op1=OP.add))
            xc_scan_rd[j % 2] = t_g1
            yg_tok = op("vector", lambda e, gsl=gsl, j=j: e.tensor_tensor(
                self.ygv[:, j, :], gsl, self.zv[:, j, :], op=OP.mult))
            self.y_tok[ybk] = t_g1
            if j == 0 and b == 0 and k == 0:
                self.tap("y_b0k0", gsl, [t_g1])
        self.scan_done = yg_tok
        if self.stage in ("scan_dt", "scan_sp", "scan_noh", "scan_noy"):
            return

        if self.stage == "scan":
            return
        for dm in range(16):
            if self.wout_used is not None:
                p.wait("sync", self.wout_used)
            t_wo = dma("sync", "wo", self.wov[:, :, :],
                       self.wout_bf[:, dm * 128:(dm + 1) * 128].rearrange("(c p) o -> p c o", p=128))
            t_q = dma("sync", "q", self.qv[:, :, :],
                      self.qt_dram[:, dm * 128:(dm + 1) * 128].rearrange("(c p) o -> p c o", p=128))
            if dm == 0:
                p.wait("sync", self.q_prep_done)
            p.waits("tensor", [t_wo, t_q, yg_tok, xn_tok])
            rbk = 2 if dm % 2 == 0 else 3
            p.wait("tensor", self.resid_tok[rbk])
            mm = None
            for j in range(NDI):
                mm = op("tensor", lambda e, j=j, rbk=rbk: e.matmul(
                    pb[rbk][:, 0:256], self.wov[:, j, :], self.ygv[:, j, :],
                    start=(j == 0), stop=False))
            for i in range(NDM):
                mm = op("tensor", lambda e, i=i, rbk=rbk: e.matmul(
                    pb[rbk][:, 0:256], self.qv[:, i, :], self.xnv[:, i, 1:257],
                    start=False, stop=(i == NDM - 1)))
            p.wait("vector", mm)
            xsl = self.x1slot[:, (dm % 2) * 256:(dm % 2) * 256 + 256]
            p.wait("vector", self.x1_st[dm % 2])
            t_x1 = op("vector", lambda e, dm=dm, rbk=rbk, xsl=xsl: e.scalar_tensor_tensor(
                xsl, pb[rbk][:, 0:256], self.l0bbuf[:, dm:dm + 1], self.x0v[:, dm, 1:257],
                op0=OP.add, op1=OP.add))
            self.resid_tok[rbk] = t_x1
            p.wait("sync", t_x1)
            self.x1_st[dm % 2] = dma("sync", "x1st", self.x1_bdt[b, dm * 128:(dm + 1) * 128, t0:t0 + TC], xsl)
            self.wout_used = mm
        self.dtin_used = self.dt_done[4]


class L1Builder:
    def __init__(self, cfg):
        self.cfg = cfg
        self.NB = cfg.get("NB", 2)
        self.L = cfg.get("L", 2048)
        self.NT = self.L // 512
        self.NG = self.L // 512
        self.DM = 2048
        self.NDM = 16
        self.NH = 16
        self.NI = 64

    def build(self, env=None):
        if env is None:
            nc = bass.Bass("TRN2", target_bir_lowering=False)
            es = ExitStack()
            p = Prog(nc, es)
        else:
            nc, es, p = env["nc"], env["es"], env["p"]
        self.env = env
        self.nc, self.es, self.p = nc, es, p
        NB, L, NT, NDM, NH, NI = self.NB, self.L, self.NT, self.NDM, self.NH, self.NI

        if env is None:
            self.x1_bdt = nc.dram_tensor("x1_bdt", [NB, 2048, L], F32, kind="ExternalInput").ap()
        else:
            self.x1_bdt = env["x1_ap"]
        self.wq_t = nc.dram_tensor("wq_t", [NH, 2048, 128], F32, kind="ExternalInput").ap()
        self.wk_t = nc.dram_tensor("wk_t", [NH, 2048, 128], F32, kind="ExternalInput").ap()
        self.wv_t = nc.dram_tensor("wv_t", [NH, 2048, 128], F32, kind="ExternalInput").ap()
        self.wo_t = nc.dram_tensor("wo_t", [NH, 128, 2048], F32, kind="ExternalInput").ap()
        self.gate_wt = nc.dram_tensor("gate_wt", [2048, 8192], I32, kind="ExternalInput").ap()
        self.up_wt = nc.dram_tensor("up_wt", [2048, 8192], I32, kind="ExternalInput").ap()
        self.down_wt = nc.dram_tensor("down_wt", [8192, 2048], I32, kind="ExternalInput").ap()
        self.gsc_p = nc.dram_tensor("gsc_p", [128, NI], F32, kind="ExternalInput").ap()
        self.usc_p = nc.dram_tensor("usc_p", [128, NI], F32, kind="ExternalInput").ap()
        self.dsc_p = nc.dram_tensor("dsc_p", [128, NDM], F32, kind="ExternalInput").ap()
        self.out_bdt = nc.dram_tensor("out_bdt", [NB, 2048, L], F32, kind="ExternalOutput").ap()
        self.zt_dram = nc.dram_tensor("zt_scratch", [NH, 128, L], BF16).ap()
        if self.cfg.get("debug"):
            self.dbg_qt = nc.dram_tensor("dbg_qt", [128, L], F32, kind="ExternalOutput").ap()
            self.dbg_kt = nc.dram_tensor("dbg_kt", [128, L], F32, kind="ExternalOutput").ap()
            self.dbg_vt = nc.dram_tensor("dbg_vt", [128, 16 * 128], F32, kind="ExternalOutput").ap()
            self.dbg_p = nc.dram_tensor("dbg_p", [128, L], F32, kind="ExternalOutput").ap()
            self.dbg_pt = nc.dram_tensor("dbg_pt", [128, 16 * 512], F32, kind="ExternalOutput").ap()
        self.h_dram = nc.dram_tensor("h_scratch", [8192, L], BF16).ap()

        if env is None:
            def sb(name, shape, dt):
                return es.enter_context(nc.sbuf_tensor(name, shape, dt))
        else:
            sb = env["alloc"]

        self.R0 = sb("R0", [128, 16 * 2048], BF16)
        self.xn1 = self.R0[:, :]
        self.xn1v = self.xn1.rearrange("p (c t) -> p c t", t=2048)
        self.hbf = self.R0[:, :]
        self.hbv = self.hbf.rearrange("p (c t) -> p c t", t=2048)
        self.hbv2 = self.R0[:, :].rearrange("p (c t) -> p c t", t=512)
        self.R1 = sb("R1", [128, 16384], F32)
        r1b = self.R1[:, :].bitcast(BF16)
        self.x1ch = self.R1[:, 0:8192].rearrange("p (c t) -> p c t", t=512)
        self.sqsl = self.R1[:, 8192:9216]
        self.Sstrip = self.R1[:, 0:2048]
        self.VTf = self.R1[:, 2048:4096]
        self.QT = r1b[:, 8192:10240]
        self.KT = r1b[:, 10240:12288]
        self.VT = r1b[:, 12288:14336]
        self.Vtok = r1b[:, 14336:16384]
        self.Vtokv = self.Vtok.rearrange("p (c e) -> p c e", e=128)
        self.PT = r1b[:, 16384:24576]
        self.PTv = self.PT.rearrange("p (c t) -> p c t", t=512)
        self.zTtile = r1b[:, 24576:28672]
        self.zTl = r1b[:, 0:16384]
        self.zTlv = r1b[:, 0:8192].rearrange("p (c t) -> p c t", t=512)
        self.x1tile = self.R1[:, 8192:16384].rearrange("p (c t) -> p c t", t=512)

        self.R2 = sb("R2", [128, 8192], F32)
        r2b = self.R2[:, :].bitcast(BF16)
        self.wqb = r2b[:, 0:2048].rearrange("p (c e) -> p c e", e=128)
        self.wkb = r2b[:, 2048:4096].rearrange("p (c e) -> p c e", e=128)
        self.wvb = r2b[:, 4096:6144].rearrange("p (c e) -> p c e", e=128)
        self.gwb = r2b[:, 0:2048].rearrange("p (c e) -> p c e", e=128)
        self.uwb = r2b[:, 2048:4096].rearrange("p (c e) -> p c e", e=128)
        self.dwb = r2b[:, 4096:6144]
        self.wob = r2b[:, 6144:16384 + 0]
        self.wobv = r2b[:, 6144:10240].rearrange("p (c e) -> p c e", e=1024)

        self.gs = sb("gs", [128, 2048], BF16)
        self.up = sb("up", [128, 2048], BF16)
        self.outsl = sb("outsl", [128, 2 * 512], F32)
        self.statrow = sb("statrow1", [1, 4 * 512], F32)
        self.masks = sb("masks", [128, 4 * 512], F32)
        self.identbf = sb("identbf1", [128, 128], BF16)
        self.identf = sb("identf1", [128, 128], F32)
        self.pidx = sb("pidx1", [128, 1], F32)
        self.fidx = sb("fidx1", [128, 512], F32)
        self.vcol = sb("vcol1", [128, 4], F32)
        self.ones_col = sb("ones_col1", [128, 1], F32)
        self.ones_row = sb("ones_row1", [1, 128], F32)
        self.epscol = sb("epscol1", [128, 1], F32)
        self.nmax = sb("nmax1", [128, 1], F32)
        self.sums = sb("sums1", [128, 1], F32)
        self.recip = sb("recip1", [128, 1], F32)
        self.gscb = sb("gscb", [128, NI], F32)
        self.uscb = sb("uscb", [128, NI], F32)
        self.dscb = sb("dscb", [128, NDM], F32)

        self.pb = env["pb"] if env is not None else [
            es.enter_context(nc.psum_tensor("ps1_%d" % i, [128, 512], F32)) for i in range(8)]

        self.dsem = {n: p.sem("E_" + n, dma=True) for n in
                     ("x1", "wqkv", "wo", "gw", "dw0", "dw1", "zt", "h", "st", "res", "zl", "hl", "tapd")}
        self.PE = p.sem("F_PE")
        self.DV = p.sem("F_DV")
        self.AC = p.sem("F_AC")
        self.PL = p.sem("F_PL")
        self.esem = {"scalar": self.AC, "vector": self.DV, "tensor": self.PE, "gpsimd": self.PL}

        self.prologue()
        for b in range(NB):
            self.layer1_batch(b)

        for sm in self.dsem.values():
            p.wait("sync", (sm, sm.n))

        if env is not None:
            return nc
        block = es.enter_context(nc.Block())
        with es:
            p.finalize(block, strict=self.cfg.get("strict", False))
        return nc

    def dma(self, q, stream, out, in_, **kw):
        sem = self.dsem[stream]
        if q == "gpsimd" and "max_dma_last_dim" not in kw:
            kw = dict(kw, max_dma_last_dim=4096)
        return self.p.op(q, lambda e, out=out, in_=in_, kw=kw: e.dma_start(out=out, in_=in_, **kw), sem, 16)

    def op(self, eng, fn):
        return self.p.op(eng, fn, self.esem[eng], 1)

    def barrier(self, engs=("scalar", "vector", "tensor", "gpsimd", "sync")):
        toks = [(s, s.n) for s in (self.PE, self.DV, self.AC, self.PL)] + \
               [(s, s.n) for s in self.dsem.values()]
        for e in engs:
            self.p.waits(e, toks)

    def prologue(self):
        p, op, dma = self.p, self.op, self.dma
        t1 = op("gpsimd", lambda e: e.iota(self.pidx[:, :], [[0, 1]], base=0, channel_multiplier=1,
                                           allow_small_or_imprecise_dtypes=True))
        t2 = op("gpsimd", lambda e: e.iota(self.fidx[:, :], [[1, 512]], base=0, channel_multiplier=0,
                                           allow_small_or_imprecise_dtypes=True))
        op("gpsimd", lambda e: e.memset(self.ones_col[:, :], 1.0))
        op("gpsimd", lambda e: e.memset(self.ones_row[:, :], 1.0))
        t5 = op("gpsimd", lambda e: e.memset(self.epscol[:, :], 1e-5))
        p.waits("vector", [t1, t2, t5])
        op("vector", lambda e: e.tensor_scalar(self.identbf[:, :], self.fidx[:, 0:128], self.pidx[:, :],
                                               None, op0=OP.is_equal))
        op("vector", lambda e: e.tensor_scalar(self.identf[:, :], self.fidx[:, 0:128], self.pidx[:, :],
                                               None, op0=OP.is_equal))
        for v in range(4):
            t_vc = op("vector", lambda e, v=v: e.tensor_scalar(self.vcol[:, v:v + 1], self.pidx[:, :], float(v * 128),
                                                               None, op0=OP.add))
            p.wait("vector", t_vc)
            op("vector", lambda e, v=v: e.tensor_scalar(self.masks[:, v * 512:(v + 1) * 512], self.fidx[:, :],
                                                        self.vcol[:, v:v + 1], -1e9, op0=OP.is_gt, op1=OP.mult))
        dma("sync", "res", self.gscb[:, :], self.gsc_p[:, :])
        dma("sync", "res", self.uscb[:, :], self.usc_p[:, :])
        t = dma("sync", "res", self.dscb[:, :], self.dsc_p[:, :])
        p.waits("scalar", [t])
        p.waits("vector", [t])
        self.barrier()

    def ln(self, b):
        p, op, dma = self.p, self.op, self.dma
        pb = self.pb
        lnrd = None
        for ch in range(self.NT):
            t0 = ch * 512
            if lnrd is not None:
                p.wait("sync", lnrd)
            t_x = dma("sync", "x1", self.x1ch[:, :, :],
                      self.x1_bdt[b, :, t0:t0 + 512].rearrange("(c p) t -> p c t", p=128))
            p.waits("scalar", [t_x])
            p.waits("tensor", [t_x])
            if lnrd is not None:
                p.wait("tensor", lnrd)
            mm_x = mm_sq = None
            sq_tok = {}
            for c in range(16):
                sl = self.sqsl[:, (c % 2) * 512:(c % 2) * 512 + 512]
                if c >= 2:
                    p.wait("scalar", sq_tok[c - 2])
                t_sq = op("scalar", lambda e, c=c, sl=sl: e.activation(out=sl, in_=self.x1ch[:, c, :], func=AF.Square))
                p.wait("tensor", t_sq)
                mm_x = op("tensor", lambda e, c=c: e.matmul(pb[0][0:1, :], self.ones_col[:, :], self.x1ch[:, c, :],
                                                            start=(c == 0), stop=(c == 15)))
                mm_sq = op("tensor", lambda e, sl=sl, c=c: e.matmul(pb[1][0:1, :], self.ones_col[:, :], sl,
                                                                   start=(c == 0), stop=(c == 15)))
                sq_tok[c] = mm_sq
            p.waits("vector", [mm_x, mm_sq])
            mrow = self.statrow[0:1, 0:512]
            exrow = self.statrow[0:1, 512:1024]
            varrow = self.statrow[0:1, 1024:1536]
            rrow = self.statrow[0:1, 1536:2048]
            op("vector", lambda e: e.tensor_scalar(mrow, pb[0][0:1, :], 1.0 / 2048, None, op0=OP.mult))
            op("vector", lambda e: e.tensor_scalar(exrow, pb[1][0:1, :], 1.0 / 2048, None, op0=OP.mult))
            op("vector", lambda e: e.tensor_tensor(varrow, mrow, mrow, op=OP.mult))
            t_var = op("vector", lambda e: e.tensor_tensor(varrow, exrow, varrow, op=OP.subtract))
            p.wait("scalar", t_var)
            t_ln = op("scalar", lambda e: e.activation(out=rrow, in_=varrow, func=AF.Ln, bias=self.epscol[0:1, 0:1]))
            p.wait("scalar", t_ln)
            t_rstd = op("scalar", lambda e: e.activation(out=rrow, in_=rrow, func=AF.Exp, scale=-0.5))
            p.waits("tensor", [t_var, t_rstd])
            t_bm = op("tensor", lambda e: e.matmul(pb[0][:, :], self.ones_row[:, :], mrow, start=True, stop=True))
            t_br = op("tensor", lambda e: e.matmul(pb[1][:, :], self.ones_row[:, :], rrow, start=True, stop=True))
            p.waits("vector", [t_bm, t_br])
            for c in range(16):
                sl = self.sqsl[:, (c % 2) * 512:(c % 2) * 512 + 512]
                op("vector", lambda e, c=c, sl=sl: e.tensor_tensor(sl, self.x1ch[:, c, :], pb[0][:, :], op=OP.subtract))
                lnrd = op("vector", lambda e, c=c, sl=sl, t0=t0: e.tensor_tensor(
                    self.xn1v[:, c, t0:t0 + 512], sl, pb[1][:, :], op=OP.mult))
        self.barrier()

    def attn_head(self, b, h):
        p, op, dma = self.p, self.op, self.dma
        pb = self.pb
        self.strip_users = []
        NT, NG = self.NT, self.NG
        t_wq = dma("gpsimd", "wqkv", self.wqb[:, :, :],
                   self.wq_t[h].rearrange("(c p) e -> p c e", p=128))
        t_wk = dma("gpsimd", "wqkv", self.wkb[:, :, :],
                   self.wk_t[h].rearrange("(c p) e -> p c e", p=128))
        t_wv = dma("gpsimd", "wqkv", self.wvb[:, :, :],
                   self.wv_t[h].rearrange("(c p) e -> p c e", p=128))
        p.waits("tensor", [t_wq, t_wk, t_wv])
        for which, wbuf, dst in (("q", self.wqb, self.QT), ("k", self.wkb, self.KT)):
            for tt in range(NT):
                bk = tt % 2
                if self.pbk_tok[bk] is not None:
                    p.wait("tensor", self.pbk_tok[bk])
                mm = None
                for c in range(16):
                    mm = op("tensor", lambda e, c=c, tt=tt, bk=bk, wbuf=wbuf: e.matmul(
                        pb[bk][:, :], wbuf[:, c, :], self.xn1v[:, c, tt * 512:(tt + 1) * 512],
                        start=(c == 0), stop=(c == 15)))
                p.wait("vector", mm)
                t_ev = op("vector", lambda e, tt=tt, bk=bk, dst=dst: e.tensor_copy(
                    dst[:, tt * 512:(tt + 1) * 512], pb[bk][:, :]))
                self.pbk_tok[bk] = t_ev
        for tt in range(NT):
            bk = tt % 2
            p.wait("tensor", self.pbk_tok[bk])
            mm = None
            for c in range(16):
                mm = op("tensor", lambda e, c=c, tt=tt, bk=bk: e.matmul(
                    pb[bk][:, :], self.wvb[:, c, :], self.xn1v[:, c, tt * 512:(tt + 1) * 512],
                    start=(c == 0), stop=(c == 15)))
            p.wait("vector", mm)
            t_ev = op("vector", lambda e, tt=tt, bk=bk: e.tensor_copy(
                self.VTf[:, tt * 512:(tt + 1) * 512], pb[bk][:, :]))
            self.pbk_tok[bk] = t_ev
        vtf_done = (self.DV, self.DV.n)
        for g4 in range(NT):
            bk = 2 + (g4 % 2)
            if self.pbk_tok[bk] is not None:
                p.wait("tensor", self.pbk_tok[bk])
            p.wait("tensor", vtf_done)
            tr = None
            for q in range(4):
                ti = g4 * 4 + q
                tr = op("tensor", lambda e, ti=ti, q=q, bk=bk: e.transpose(
                    pb[bk][:, q * 128:(q + 1) * 128], self.VTf[:, ti * 128:(ti + 1) * 128], self.identf[:, :]))
            p.wait("vector", tr)
            t_ev = op("vector", lambda e, g4=g4, bk=bk: e.tensor_copy(
                self.Vtok[:, g4 * 512:(g4 + 1) * 512], pb[bk][:, :]))
            self.pbk_tok[bk] = t_ev
        for g in range(NG):
            zbk = 4 + (g % 2)
            if self.pbk_tok[zbk] is not None:
                p.wait("tensor", self.pbk_tok[zbk])
            nseg = g + 1
            for t1 in range(4):
                ti = g * 4 + t1
                p.waits("vector", self.strip_users)
                for seg in range(nseg):
                    bk = seg % 2
                    p.wait("tensor", self.pbk_tok[bk])
                    mm = op("tensor", lambda e, ti=ti, seg=seg, bk=bk: e.matmul(
                        pb[bk][:, :], self.QT[:, ti * 128:(ti + 1) * 128], self.KT[:, seg * 512:(seg + 1) * 512],
                        start=True, stop=True))
                    p.wait("vector", mm)
                    if seg == nseg - 1:
                        t_ev = op("vector", lambda e, seg=seg, bk=bk, t1=t1: e.scalar_tensor_tensor(
                            self.Sstrip[:, seg * 512:(seg + 1) * 512], pb[bk][:, :], SCALE,
                            self.masks[:, t1 * 512:(t1 + 1) * 512], op0=OP.mult, op1=OP.add))
                    else:
                        t_ev = op("vector", lambda e, seg=seg, bk=bk: e.tensor_scalar(
                            self.Sstrip[:, seg * 512:(seg + 1) * 512], pb[bk][:, :], SCALE, None, op0=OP.mult))
                    self.pbk_tok[bk] = t_ev
                Lq = nseg * 512
                t_mx = op("vector", lambda e, Lq=Lq: e.tensor_reduce(
                    self.nmax[:, :], self.Sstrip[:, 0:Lq], axis=AX.X, op=OP.max, negate=True))
                p.wait("scalar", t_mx)
                t_exp = op("scalar", lambda e, Lq=Lq: e.activation(
                    out=self.Sstrip[:, 0:Lq], in_=self.Sstrip[:, 0:Lq], func=AF.Exp,
                    bias=self.nmax[:, :], accum_out=self.sums[:, :]))
                p.wait("vector", t_exp)
                t_rc = op("vector", lambda e: e.reciprocal(self.recip[:, :], self.sums[:, :]))
                p.wait("vector", t_rc)
                t_p = op("vector", lambda e, Lq=Lq: e.tensor_scalar(
                    self.Sstrip[:, 0:Lq], self.Sstrip[:, 0:Lq], self.recip[:, :], None, op0=OP.mult))
                self.strip_users = [t_exp]
                for g4 in range(nseg):
                    bk = 2 + (g4 % 2)
                    p.wait("tensor", self.pbk_tok[bk])
                    p.wait("tensor", t_p)
                    tr = None
                    for q in range(4):
                        sg2 = g4 * 4 + q
                        tr = op("tensor", lambda e, sg2=sg2, q=q, bk=bk: e.transpose(
                            pb[bk][:, q * 128:(q + 1) * 128], self.Sstrip[:, sg2 * 128:(sg2 + 1) * 128],
                            self.identf[:, :]))
                    p.wait("vector", tr)
                    self.strip_users.append(tr)
                    t_ev = op("vector", lambda e, g4=g4, t1=t1, bk=bk: e.tensor_copy(
                        self.PTv[:, g4 * 4:g4 * 4 + 4, t1 * 128:(t1 + 1) * 128],
                        pb[bk][:, :].rearrange("p (c t) -> p c t", t=128)))
                    self.pbk_tok[bk] = t_ev
            p.wait("tensor", self.pbk_tok[2])
            p.wait("tensor", self.pbk_tok[3])
            mm = None
            for sg2 in range(nseg * 4):
                mm = op("tensor", lambda e, sg2=sg2, zbk=zbk, nseg=nseg: e.matmul(
                    pb[zbk][:, :], self.Vtokv[:, sg2, :], self.PTv[:, sg2, :],
                    start=(sg2 == 0), stop=(sg2 == nseg * 4 - 1)))
            p.wait("vector", mm)
            t_z = op("vector", lambda e, g=g, zbk=zbk: e.tensor_copy(
                self.zTtile[:, g * 512:(g + 1) * 512], pb[zbk][:, :]))
            self.pbk_tok[zbk] = t_z
        p.wait("sync", self.pbk_tok[4])
        p.wait("sync", self.pbk_tok[5])
        t_st = dma("sync", "zt", self.zt_dram[h][:, 0:self.L], self.zTtile[:, 0:self.L])
        if self.cfg.get("debug") and b == 0 and h == 0:
            self.barrier()
            self.dma("gpsimd", "tapd", self.dbg_qt[:, :], self.QT[:, :])
            self.dma("gpsimd", "tapd", self.dbg_kt[:, :], self.KT[:, :])
            self.dma("gpsimd", "tapd", self.dbg_vt[:, :], self.Vtok[:, :])
            self.dma("sync", "zt", self.dbg_p[:, :], self.Sstrip[:, :])
            self.dma("gpsimd", "tapd", self.dbg_pt[:, :], self.PT[:, :])
        return t_st

    def ffn_gateup(self, b):
        p, op, dma = self.p, self.op, self.dma
        pb = self.pb
        for i in range(self.NI):
            if self.gw_used is not None:
                p.wait("gpsimd", self.gw_used)
            t_g = dma("gpsimd", "gw", self.gwb[:, :, :],
                      self.gate_wt[:, i * 128:(i + 1) * 128].rearrange("(c p) e -> p c e", p=128))
            t_u = dma("gpsimd", "gw", self.uwb[:, :, :],
                      self.up_wt[:, i * 128:(i + 1) * 128].rearrange("(c p) e -> p c e", p=128))
            p.waits("vector", [t_g, t_u])
            t_sg = op("vector", lambda e: e.tensor_scalar(
                self.gwb[:, :, :], self.gwb[:, :, :], -128.0, None, op0=OP.add))
            t_su = op("vector", lambda e: e.tensor_scalar(
                self.uwb[:, :, :], self.uwb[:, :, :], -128.0, None, op0=OP.add))
            p.waits("tensor", [t_sg, t_su])
            for bk in range(8):
                if self.pbk_tok[bk] is not None:
                    p.wait("tensor", self.pbk_tok[bk])
            mmg = mmu = None
            for c in range(16):
                for tt in range(self.NT):
                    mmg = op("tensor", lambda e, c=c, tt=tt: e.matmul(
                        pb[tt][:, :], self.gwb[:, c, :], self.xn1v[:, c, tt * 512:(tt + 1) * 512],
                        start=(c == 0), stop=(c == 15)))
            for c in range(16):
                for tt in range(self.NT):
                    mmu = op("tensor", lambda e, c=c, tt=tt: e.matmul(
                        pb[4 + tt][:, :], self.uwb[:, c, :], self.xn1v[:, c, tt * 512:(tt + 1) * 512],
                        start=(c == 0), stop=(c == 15)))
            self.gw_used = mmu
            p.wait("scalar", mmg)
            if self.h_store is not None:
                p.wait("scalar", self.h_store)
            t_gs = None
            for tt in range(self.NT):
                t_gs = op("scalar", lambda e, tt=tt, i=i: e.activation(
                    out=self.gs[:, tt * 512:(tt + 1) * 512], in_=pb[tt][:, :], func=AF.Silu,
                    scale=self.gscb[:, i:i + 1]))
                self.pbk_tok[tt] = t_gs
            p.wait("vector", mmu)
            t_up = None
            for tt in range(self.NT):
                t_up = op("vector", lambda e, tt=tt, i=i: e.tensor_scalar(
                    self.up[:, tt * 512:(tt + 1) * 512], pb[4 + tt][:, :], self.uscb[:, i:i + 1],
                    None, op0=OP.mult))
                self.pbk_tok[4 + tt] = t_up
            p.waits("vector", [t_gs, t_up])
            if self.h_store is not None:
                p.wait("vector", self.h_store)
            t_h = op("vector", lambda e: e.tensor_tensor(self.gs[:, 0:self.L], self.gs[:, 0:self.L],
                                                         self.up[:, 0:self.L], op=OP.mult))
            p.wait("sync", t_h)
            self.h_store = dma("sync", "h", self.h_dram[i * 128:(i + 1) * 128, 0:self.L], self.gs[:, 0:self.L])
        self.barrier()

    def down_out(self, b):
        p, op, dma = self.p, self.op, self.dma
        pb = self.pb
        prev_mm = None
        prev_ev = None
        for tt in range(self.NT):
            if prev_mm is not None:
                p.wait("sync", prev_mm)
                p.wait("sync", prev_ev)
            t_hl = dma("sync", "hl", self.hbv2[:, :, :],
                       self.h_dram[:, tt * 512:(tt + 1) * 512].rearrange("(c p) t -> p c t", p=128))
            t_x1 = dma("sync", "x1", self.x1tile[:, :, :],
                       self.x1_bdt[b, :, tt * 512:(tt + 1) * 512].rearrange("(c p) t -> p c t", p=128))
            t_zl = dma("sync", "zl", self.zTlv[:, :, :],
                       self.zt_dram[:, :, tt * 512:(tt + 1) * 512].rearrange("c p t -> p c t"))
            p.waits("tensor", [t_hl, t_x1, t_zl])
            for half in range(2):
                dw_used = None
                for bk in range(8):
                    if self.pbk_tok[bk] is not None:
                        p.wait("tensor", self.pbk_tok[bk])
                mm = None
                for i in range(self.NI):
                    sl = self.dwb[:, (i % 2) * 1024:(i % 2) * 1024 + 1024]
                    if self.dw_used[i % 2] is not None:
                        p.wait("gpsimd", self.dw_used[i % 2])
                    t_dw = dma("gpsimd", "dw%d" % (i % 2), sl,
                               self.down_wt[i * 128:(i + 1) * 128, half * 1024:(half + 1) * 1024])
                    p.wait("vector", t_dw)
                    if self.dw_sub[i % 2] is not None:
                        p.wait("vector", self.dw_sub[i % 2])
                    t_sb = op("vector", lambda e, sl=sl: e.tensor_scalar(sl, sl, -128.0, None, op0=OP.add))
                    p.wait("tensor", t_sb)
                    for dmx in range(8):
                        mm = op("tensor", lambda e, i=i, dmx=dmx, sl=sl, tt=tt: e.matmul(
                            pb[dmx][:, :], sl[:, dmx * 128:(dmx + 1) * 128], self.hbv2[:, i, :],
                            start=(i == 0), stop=False))
                    self.dw_sub[i % 2] = mm
                    self.dw_used[i % 2] = mm
                for hb in range(4):
                    if self.wo_used is not None:
                        p.wait("gpsimd", self.wo_used)
                    t_wo = dma("gpsimd", "wo", self.wobv[:, :, :],
                               self.wo_t[hb * 4:(hb + 1) * 4, :, half * 1024:(half + 1) * 1024]
                               .rearrange("c p e -> p c e"))
                    p.wait("tensor", t_wo)
                    for hq in range(4):
                        h = hb * 4 + hq
                        for dmx in range(8):
                            last = (hb == 3 and hq == 3)
                            mm = op("tensor", lambda e, h=h, hq=hq, dmx=dmx, last=last: e.matmul(
                                pb[dmx][:, :], self.wobv[:, hq, dmx * 128:(dmx + 1) * 128],
                                self.zTlv[:, h, :], start=False, stop=last))
                    self.wo_used = mm
                p.wait("vector", mm)
                for dmx in range(8):
                    dm = half * 8 + dmx
                    osl = self.outsl[:, (dmx % 2) * 512:(dmx % 2) * 512 + 512]
                    p.wait("vector", self.out_st[dmx % 2])
                    t_o = op("vector", lambda e, dm=dm, dmx=dmx, osl=osl: e.scalar_tensor_tensor(
                        osl, pb[dmx][:, :], self.dscb[:, dm:dm + 1], self.x1tile[:, dm, :],
                        op0=OP.mult, op1=OP.add))
                    self.pbk_tok[dmx] = t_o
                    p.wait("sync", t_o)
                    self.out_st[dmx % 2] = dma("sync", "st",
                                               self.out_bdt[b, dm * 128:(dm + 1) * 128, tt * 512:(tt + 1) * 512],
                                               osl)
                prev_mm = mm
                prev_ev = t_o
        self.barrier()

    def layer1_batch(self, b):
        self.pbk_tok = {k: None for k in range(8)}
        self.gw_used = None
        self.h_store = None
        self.dw_used = {0: None, 1: None}
        self.dw_sub = {0: None, 1: None}
        self.wo_used = None
        self.out_st = {0: None, 1: None}
        self.ln(b)
        if self.cfg.get("stage") == "ln":
            return
        for h in range(self.NH):
            self.attn_head(b, h)
            self.barrier()
        if self.cfg.get("stage") == "attn":
            return
        self.ffn_gateup(b)
        if self.cfg.get("stage") == "gateup":
            return
        self.down_out(b)



def _vec_p(v, nch):
    return np.ascontiguousarray(np.asarray(v, np.float32).reshape(nch, 128).T)


_CACHE = {}

ARENA_F32 = 53200


def build_fused():
    nc = bass.Bass("TRN2", target_bir_lowering=False)
    es = ExitStack()
    p = Prog(nc, es)
    pb = [es.enter_context(nc.psum_tensor("pbk%d" % i, [128, 512], F32)) for i in range(8)]
    arena = es.enter_context(nc.sbuf_tensor("arena", [128, ARENA_F32], F32))
    arena_ap = arena[:, :]
    arena_bf = arena_ap.bitcast(BF16)
    x1_ap = nc.dram_tensor("x1_bdt", [2, 2048, 2048], F32).ap()
    off = [0]

    def alloc(name, shape, dt):
        nelem = shape[1]
        nf32 = (nelem + 1) // 2 if dt == BF16 else nelem
        a = off[0]
        off[0] = a + nf32 + ((-(a + nf32)) % 16)
        assert off[0] <= ARENA_F32, "arena overflow at %s (%d)" % (name, off[0])
        if dt == BF16:
            v = arena_bf[:, 2 * a:2 * a + nelem]
        else:
            v = arena_ap[:, a:a + nelem]
        if shape[0] != 128:
            v = v[0:shape[0], :]
        return v

    env = {"nc": nc, "es": es, "p": p, "pb": pb, "alloc": alloc, "x1_ap": x1_ap}
    b0 = L0Builder({"NB": 2, "L": 2048})
    b0.build(env)
    l0_toks = [(sm, sm.n) for sm in b0.dsem.values()] +               [(sm, sm.n) for sm in (b0.PE, b0.DV, b0.AC, b0.PL)]
    for e in ENGS:
        p.waits(e, l0_toks)
    off[0] = 0
    b1 = L1Builder({"NB": 2, "L": 2048})
    b1.build(env)
    block = es.enter_context(nc.Block())
    with es:
        p.finalize(block)
    return nc


def _get_prog():
    if "fused" not in _CACHE:
        _CACHE["fused"] = build_fused()
    return _CACHE["fused"]


def kernel(L0_attn_x, L0_attn_W_in, L0_attn_conv_w, L0_attn_conv_b, L0_attn_W_x,
           L0_attn_W_dt, L0_attn_b_dt, L0_attn_A_log, L0_attn_D, L0_attn_W_out,
           L0_ffn_weight, L0_ffn_bias, L1_attn_W_Q, L1_attn_W_K, L1_attn_W_V,
           L1_attn_W_O, L1_ffn_gate_w, L1_ffn_gate_scale, L1_ffn_up_w,
           L1_ffn_up_scale, L1_ffn_down_w, L1_ffn_down_scale):
    x = np.asarray(L0_attn_x, np.float32)
    NBC = 2
    n_cores = 8
    f32 = lambda a: np.ascontiguousarray(np.asarray(a, np.float32))
    i32 = lambda a: np.ascontiguousarray(np.asarray(a, np.int32))

    w_l0 = {
        "w_in": f32(L0_attn_W_in),
        "conv_w_p": np.ascontiguousarray(np.asarray(L0_attn_conv_w, np.float32)
                                         .reshape(32, 128, 3).transpose(1, 0, 2).reshape(128, 96)),
        "conv_b_p": _vec_p(L0_attn_conv_b, 32),
        "w_x": f32(L0_attn_W_x),
        "w_dt": f32(L0_attn_W_dt),
        "b_dt_p": _vec_p(L0_attn_b_dt, 32),
        "a_log_p": np.ascontiguousarray(np.asarray(L0_attn_A_log, np.float32)
                                        .reshape(32, 128, 16).transpose(1, 0, 2).reshape(128, 512)),
        "d_p": _vec_p(L0_attn_D, 32),
        "w_out": f32(L0_attn_W_out),
        "l0w_t": np.ascontiguousarray(np.asarray(L0_ffn_weight, np.float32).T),
        "l0b_p": _vec_p(L0_ffn_bias, 16),
    }
    w_l1 = {
        "wq_t": np.ascontiguousarray(np.asarray(L1_attn_W_Q, np.float32).transpose(0, 2, 1)),
        "wk_t": np.ascontiguousarray(np.asarray(L1_attn_W_K, np.float32).transpose(0, 2, 1)),
        "wv_t": np.ascontiguousarray(np.asarray(L1_attn_W_V, np.float32).transpose(0, 2, 1)),
        "wo_t": np.ascontiguousarray(np.asarray(L1_attn_W_O, np.float32).transpose(0, 2, 1)),
        "gate_wt": np.ascontiguousarray(np.asarray(L1_ffn_gate_w, np.int32).T),
        "up_wt": np.ascontiguousarray(np.asarray(L1_ffn_up_w, np.int32).T),
        "down_wt": np.ascontiguousarray(np.asarray(L1_ffn_down_w, np.int32).T),
        "gsc_p": _vec_p(L1_ffn_gate_scale, 64),
        "usc_p": _vec_p(L1_ffn_up_scale, 64),
        "dsc_p": _vec_p(L1_ffn_down_scale, 16),
    }

    ncf = _get_prog()
    in_maps = []
    for c in range(n_cores):
        xs = np.ascontiguousarray(x[c * NBC:(c + 1) * NBC].transpose(0, 2, 1))
        m = dict(w_l0, x_bdt=xs)
        m.update(w_l1)
        in_maps.append(m)
    res = run_bass_kernel_spmd(ncf, in_maps, core_ids=list(range(n_cores)))
    out = np.empty((16, 2048, 2048), np.float32)
    for c in range(n_cores):
        out[c * NBC:(c + 1) * NBC] = res.results[c]["out_bdt"].transpose(0, 2, 1)
    return out


# version 09
# speedup vs baseline: 1.0412x; 1.0412x over previous
import numpy as np
from contextlib import ExitStack

import concourse.bass as bass
import concourse.mybir as mybir
from concourse.bass_utils import run_bass_kernel_spmd

F32 = mybir.dt.float32
BF16 = mybir.dt.bfloat16
I32 = mybir.dt.int32
AF = mybir.ActivationFunctionType
OP = mybir.AluOpType
AX = mybir.AxisListType

ENGS = ("sync", "scalar", "vector", "tensor", "gpsimd")
SCALE = 1.0 / np.sqrt(128.0)


class Sem:
    def __init__(self, h, name, dma=False):
        self.h = h
        self.name = name
        self.n = 0
        self.dma = dma


class Prog:
    def __init__(self, nc, es):
        self.nc = nc
        self.es = es
        self.items = {e: [] for e in ENGS}

    def sem(self, name, dma=False):
        return Sem(self.es.enter_context(self.nc.semaphore(name)), name, dma=dma)

    def wait(self, eng, tok):
        if tok is None:
            return
        sem, val = tok
        if sem.dma:
            val = sem.n
        self.items[eng].append(("w", sem, val))

    def waits(self, eng, toks):
        for t in toks:
            self.wait(eng, t)

    def op(self, eng, fn, sem, k=1):
        sem.n += k
        self.items[eng].append(("o", fn, [(sem, k)]))
        return (sem, sem.n)

    def finalize(self, block, strict=False):
        for ename in ENGS:
            items = self.items[ename]

            def body(eng, items=items):
                seen = {}
                waited = {}
                for it in items:
                    if it[0] == "w":
                        if waited.get(it[1], -1) >= it[2]:
                            continue
                        waited[it[1]] = it[2]
                        eng.wait_ge(it[1].h, it[2])
                    else:
                        if strict:
                            for s, k in it[2]:
                                if not s.dma and s in seen:
                                    eng.wait_ge(s.h, seen[s])
                        ins = it[1](eng)
                        for s, k in it[2]:
                            seen[s] = seen.get(s, 0) + k
                            ins.then_inc(s.h, k)

            getattr(block, ename)(body)


class L0Builder:
    def __init__(self, cfg):
        self.cfg = cfg
        self.NB = cfg.get("NB", 2)
        self.L = cfg.get("L", 2048)
        self.TC = 256
        self.NCH = self.L // self.TC
        self.DM = 2048
        self.NDM = 16
        self.DI = 4096
        self.NDI = 32
        self.S = 16
        self.taps = dict(cfg.get("taps", []))

    def build(self, env=None):
        if env is None:
            nc = bass.Bass("TRN2", target_bir_lowering=False)
            es = ExitStack()
            p = Prog(nc, es)
        else:
            nc, es, p = env["nc"], env["es"], env["p"]
        self.env = env
        self.nc, self.es, self.p = nc, es, p
        NB, L, TC, NCH = self.NB, self.L, self.TC, self.NCH
        DM, NDM, DI, NDI, S = self.DM, self.NDM, self.DI, self.NDI, self.S

        self.x_bdt = nc.dram_tensor("x_bdt", [NB, DM, L], F32, kind="ExternalInput").ap()
        self.w_in = nc.dram_tensor("w_in", [DM, 2 * DI], F32, kind="ExternalInput").ap()
        self.conv_w_p = nc.dram_tensor("conv_w_p", [128, NDI * 3], F32, kind="ExternalInput").ap()
        self.conv_b_p = nc.dram_tensor("conv_b_p", [128, NDI], F32, kind="ExternalInput").ap()
        self.w_x = nc.dram_tensor("w_x", [DI, 160], F32, kind="ExternalInput").ap()
        self.w_dt = nc.dram_tensor("w_dt", [128, DI], F32, kind="ExternalInput").ap()
        self.b_dt_p = nc.dram_tensor("b_dt_p", [128, NDI], F32, kind="ExternalInput").ap()
        self.a_log_p = nc.dram_tensor("a_log_p", [128, NDI * S], F32, kind="ExternalInput").ap()
        self.d_p = nc.dram_tensor("d_p", [128, NDI], F32, kind="ExternalInput").ap()
        self.w_out = nc.dram_tensor("w_out", [DI, DM], F32, kind="ExternalInput").ap()
        self.l0w_t = nc.dram_tensor("l0w_t", [DM, DM], F32, kind="ExternalInput").ap()
        self.l0b_p = nc.dram_tensor("l0b_p", [128, NDM], F32, kind="ExternalInput").ap()
        if env is None:
            self.x1_bdt = nc.dram_tensor("x1_bdt", [NB, DM, L], F32, kind="ExternalOutput").ap()
        else:
            self.x1_bdt = env["x1_ap"]
        self.qt_dram = nc.dram_tensor("qt_scratch", [DM, DM], BF16).ap()
        self.win_bf = nc.dram_tensor("win_bf", [DM, 2 * DI], BF16).ap()
        self.wout_bf = nc.dram_tensor("wout_bf", [DI, DM], BF16).ap()
        self.bc_dram = nc.dram_tensor("bc_scratch", [2 * S, TC], F32).ap()
        self.xc_dram = nc.dram_tensor("xc_scratch", [4096, 256], F32).ap()
        self.tap_outs = {}
        for tname, tshape in self.taps.items():
            dt = F32
            self.tap_outs[tname] = nc.dram_tensor("tap_" + tname, list(tshape), dt, kind="ExternalOutput").ap()

        if env is None:
            def sb(name, shape, dt):
                return es.enter_context(nc.sbuf_tensor(name, shape, dt))
        else:
            sb = env["alloc"]

        self.x0buf = sb("x0buf", [128, NDM * 258], F32)
        self.xnbuf = sb("xnbuf", [128, NDM * 258], BF16)
        self.winbuf = sb("winbuf", [128, NDM * 256], BF16)
        self.xpslot = sb("xpslot", [128, 2 * 258], F32)
        self.sqslot = sb("sqslot", [128, 2 * 258], F32)
        self.big1 = sb("big1", [128, NDI * 258], F32)
        self.xcraw = self.big1[:, 0:NDI * 258]
        self.zraw = sb("zraw", [128, NDI * TC], F32)
        self.xcring = sb("xcring", [128, 2 * 256], F32)
        self.xcbf = sb("xcbf", [128, 2 * TC], BF16)
        self.statrow = sb("statrow", [1, 4 * 258], F32)
        self.dtinbf = sb("dtinbf", [128, TC], BF16)
        self.bcsb = sb("bcsb", [2 * S, TC], F32)
        self.Bb = self.big1[:, 0:S * TC]
        self.Cb = self.big1[:, S * TC:2 * S * TC]
        self.dtslot = sb("dtslot", [128, 2 * TC], F32)
        self.spslot = sb("spslot", [128, 2 * TC], F32)
        self.dtuslot = sb("dtuslot", [128, 2 * TC], F32)
        self.dAslot = sb("dAslot", [128, 4 * TC], F32)
        self.xsslot = sb("xsslot", [128, 4 * TC], F32)
        self.hbuf = sb("hbuf", [128, S * TC], F32)
        self.tmpslot = sb("tmpslot", [128, 4 * TC], BF16)
        self.ygbuf = sb("ygbuf", [128, NDI * TC], BF16)
        self.yg1slot = sb("yg1slot", [128, 2 * TC], F32)
        self.woutsl = sb("woutsl", [128, NDI * 128], BF16)
        self.qsl = sb("qsl", [128, NDM * 128], BF16)
        self.x1slot = sb("x1slot", [128, 2 * 256], F32)
        self.hstate = sb("hstate", [128, NB * 512], F32)
        self.wxbuf = sb("wxbuf", [128, NDI * 160], BF16)
        self.wdtbuf = sb("wdtbuf", [128, DI], BF16)
        self.Abuf = sb("Abuf", [128, NDI * S], F32)
        self.cwbuf = sb("cwbuf", [128, NDI * 3], F32)
        self.cbbuf = sb("cbbuf", [128, NDI], F32)
        self.bdtbuf = sb("bdtbuf", [128, NDI], F32)
        self.Dbuf = sb("Dbuf", [128, NDI], F32)
        self.l0bbuf = sb("l0bbuf", [128, NDM], F32)
        self.identbf = sb("identbf", [128, 128], BF16)
        self.pidx = sb("pidx", [128, 1], F32)
        self.fidx = sb("fidx", [128, 128], F32)
        self.ones_col = sb("ones_col", [128, 1], F32)
        self.ones_row = sb("ones_row", [1, 128], F32)
        self.epscol = sb("epscol", [128, 1], F32)
        self.onecol = sb("onecol", [128, 1], F32)
        self.qtile = self.zraw[:, 0:1024].bitcast(BF16)
        self.throw = self.zraw[0:1, 1024:3072]
        self.wtile = self.x0buf[:, 0:2048]
        self.awtile = self.x0buf[:, 2048:4096]
        self.sgtile = self.big1[:, 0:2048]
        self.thrbc = self.big1[:, 2048:4096]
        self.pb = env["pb"] if env is not None else [
            es.enter_context(nc.psum_tensor("pbk%d" % i, [128, 512], F32)) for i in range(8)]

        self.x0v = self.x0buf[:, :].rearrange("p (c t) -> p c t", t=258)
        self.xnv = self.xnbuf[:, :].rearrange("p (c t) -> p c t", t=258)
        self.winv = self.winbuf[:, :].rearrange("p (c o) -> p c o", o=256)
        self.xcv = self.xcraw.rearrange("p (c t) -> p c t", t=258)
        self.zv = self.zraw[:, :].rearrange("p (c t) -> p c t", t=TC)
        self.Bv = self.Bb.rearrange("p (s t) -> p s t", t=TC)
        self.Cv = self.Cb.rearrange("p (s t) -> p s t", t=TC)
        self.hvv = self.hbuf[:, :].rearrange("p (s t) -> p s t", t=TC)
        self.ygv = self.ygbuf[:, :].rearrange("p (c t) -> p c t", t=TC)
        self.wov = self.woutsl[:, :].rearrange("p (c o) -> p c o", o=128)
        self.qv = self.qsl[:, :].rearrange("p (c o) -> p c o", o=128)
        self.wxv = self.wxbuf[:, :].rearrange("p (c r) -> p c r", r=160)
        self.cwv = self.cwbuf[:, :].rearrange("p (c k) -> p c k", k=3)

        self.dsem = {n: p.sem("D_" + n, dma=True) for n in
                     ("x0", "res", "wtile", "qst", "q", "bc", "xcsp", "xcld0", "xcld1",
                      "x1st", "win", "wo", "tap", "wres", "wcst", "wci")}
        self.PE = p.sem("PE")
        self.DV = p.sem("DV")
        self.AC = p.sem("AC")
        self.PL = p.sem("PL")
        self.esem = {"scalar": self.AC, "vector": self.DV, "tensor": self.PE, "gpsimd": self.PL}

        self.evac_tok = {0: None, 1: None}
        self.resid_tok = {2: None, 3: None}
        self.dt_done = {4: None, 5: None}
        self.y_tok = {6: None, 7: None}
        self.x1_st = {0: None, 1: None}
        self.scan_tok = {}
        self.tmp_pe_tok = {}
        self.tmp_tok = {}
        self.st_prev = None
        self.wout_used = None
        self.q_used = None
        self.bb_used = None
        self.dtin_used = None

        self.stage = self.cfg.get("stage", "full")
        self.prologue()
        if self.stage != "prologue":
            nbl = self.cfg.get("NBL", NB)
            nchl = self.cfg.get("NCHL", NCH)
            for b in range(nbl):
                for k in range(nchl):
                    self.chunk(b, k)

        for sm in self.dsem.values():
            p.wait("sync", (sm, sm.n))
        p.wait("gpsimd", (self.dsem["tap"], self.dsem["tap"].n))

        if env is not None:
            return nc
        block = es.enter_context(nc.Block())
        with es:
            p.finalize(block, strict=self.cfg.get("strict", False))
        return nc

    def dma(self, q, stream, out, in_, **kw):
        sem = self.dsem[stream]
        if q == "gpsimd" and "max_dma_last_dim" not in kw:
            kw = dict(kw, max_dma_last_dim=4096)
        return self.p.op(q, lambda e, out=out, in_=in_, kw=kw: e.dma_start(out=out, in_=in_, **kw), sem, 16)

    def op(self, eng, fn):
        return self.p.op(eng, fn, self.esem[eng], 1)

    def tap(self, name, src_ap, after_toks):
        if name in self.tap_outs:
            self.p.waits("gpsimd", after_toks)
            self.dma("gpsimd", "tap", self.tap_outs[name][:, :], src_ap)

    def prologue(self):
        p, op, dma = self.p, self.op, self.dma
        NDM = self.NDM
        t_io = op("gpsimd", lambda e: e.iota(self.pidx[:, :], [[0, 1]], base=0, channel_multiplier=1,
                                             allow_small_or_imprecise_dtypes=True))
        t_io2 = op("gpsimd", lambda e: e.iota(self.fidx[:, :], [[1, 128]], base=0, channel_multiplier=0,
                                              allow_small_or_imprecise_dtypes=True))
        t_oc = op("gpsimd", lambda e: e.memset(self.ones_col[:, :], 1.0))
        t_or = op("gpsimd", lambda e: e.memset(self.ones_row[:, :], 1.0))
        t_ec = op("gpsimd", lambda e: e.memset(self.epscol[:, :], 1e-5))
        t_on = op("gpsimd", lambda e: e.memset(self.onecol[:, :], 1.0))
        t_hs = op("gpsimd", lambda e: e.memset(self.hstate[:, :], 0.0))
        t_bz = op("gpsimd", lambda e: e.memset(self.big1[:, :], 0.0))
        p.waits("vector", [t_io, t_io2])
        t_id = op("vector", lambda e: e.tensor_scalar(self.identbf[:, :], self.fidx[:, :], self.pidx[:, :],
                                                      None, op0=OP.is_equal))
        t_wx = dma("gpsimd", "wres", self.wxv[:, :, :], self.w_x[:, :].rearrange("(c p) r -> p c r", p=128))
        wcast = self.x0buf[:, :].bitcast(BF16)
        t_pc = None
        for c in range(16):
            if t_pc is not None:
                p.wait("gpsimd", t_pc)
            t_c = dma("gpsimd", "wci", wcast[:, 0:8192], self.w_in[c * 128:(c + 1) * 128, :])
            p.wait("sync", t_c)
            t_pc = dma("sync", "wcst", self.win_bf[c * 128:(c + 1) * 128, :], wcast[:, 0:8192])
        for c in range(32):
            p.wait("gpsimd", t_pc)
            t_c = dma("gpsimd", "wci", wcast[:, 0:2048], self.w_out[c * 128:(c + 1) * 128, :])
            p.wait("sync", t_c)
            t_pc = dma("sync", "wcst", self.wout_bf[c * 128:(c + 1) * 128, :], wcast[:, 0:2048])
        self.wcast_done = t_pc
        p.wait("sync", self.wcast_done)
        t_wdt = dma("gpsimd", "wres", self.wdtbuf[:, :], self.w_dt[:, :])
        t_cw = dma("sync", "res", self.cwbuf[:, :], self.conv_w_p[:, :])
        t_cb = dma("sync", "res", self.cbbuf[:, :], self.conv_b_p[:, :])
        t_bdt = dma("sync", "res", self.bdtbuf[:, :], self.b_dt_p[:, :])
        t_dp = dma("sync", "res", self.Dbuf[:, :], self.d_p[:, :])
        t_l0b = dma("sync", "res", self.l0bbuf[:, :], self.l0b_p[:, :])
        t_al = dma("sync", "res", self.Abuf[:, :], self.a_log_p[:, :])
        p.wait("scalar", t_al)
        t_a1 = op("scalar", lambda e: e.activation(out=self.Abuf[:, :], in_=self.Abuf[:, :], func=AF.Exp))
        p.wait("vector", t_a1)
        t_a2 = op("vector", lambda e: e.tensor_scalar(self.Abuf[:, :], self.Abuf[:, :], -1.0, None, op0=OP.mult))

        p.waits("scalar", [t_bz, t_hs])
        p.waits("vector", [t_bz])
        tok_mm = None
        tok_aw_reader = None
        prev_aw = None
        for i in range(NDM):
            if prev_aw is not None:
                p.wait("sync", prev_aw)
            t_w = dma("sync", "wtile", self.wtile, self.l0w_t[i * 128:(i + 1) * 128, :])
            p.wait("scalar", t_w)
            if tok_aw_reader is not None:
                p.wait("scalar", tok_aw_reader)
            t_aw = op("scalar", lambda e: e.activation(out=self.awtile, in_=self.wtile, func=AF.Abs))
            prev_aw = t_aw
            p.wait("tensor", t_aw)
            if i == 0:
                p.wait("tensor", t_oc)
            for g in range(4):
                tok_mm = op("tensor", lambda e, g=g, i=i: e.matmul(
                    self.pb[g][0:1, :], self.ones_col[:, :], self.awtile[:, g * 512:(g + 1) * 512],
                    start=(i == 0), stop=(i == NDM - 1)))
            tok_aw_reader = tok_mm
        p.wait("vector", tok_mm)
        for g in range(4):
            op("vector", lambda e, g=g: e.tensor_copy(self.throw[:, g * 512:(g + 1) * 512], self.pb[g][0:1, :]))
        op("vector", lambda e: e.tensor_scalar(self.throw, self.throw, 1.0 / self.DM, None, op0=OP.mult))
        t_tr = op("vector", lambda e: e.tensor_scalar(self.throw, self.throw, 0.7, None, op0=OP.mult))
        p.waits("tensor", [t_tr, t_or])
        t_bc = None
        for g in range(4):
            t_bc = op("tensor", lambda e, g=g: e.matmul(
                self.pb[g][:, :], self.ones_row[:, :], self.throw[:, g * 512:(g + 1) * 512], start=True, stop=True))
        p.wait("vector", t_bc)
        t_thbc = None
        for g in range(4):
            t_thbc = op("vector", lambda e, g=g: e.tensor_copy(self.thrbc[:, g * 512:(g + 1) * 512], self.pb[g][:, :]))
        t_qst = None
        p.waits("scalar", [tok_mm, t_bc])
        for i in range(NDM):
            p.wait("sync", prev_aw)
            t_w = dma("sync", "wtile", self.wtile, self.l0w_t[i * 128:(i + 1) * 128, :])
            p.wait("scalar", t_w)
            if t_qst is not None:
                p.wait("scalar", t_qst)
            t_sg = op("scalar", lambda e: e.activation(out=self.sgtile, in_=self.wtile, func=AF.Sign))
            t_aw = op("scalar", lambda e: e.activation(out=self.awtile, in_=self.wtile, func=AF.Abs))
            prev_aw = t_aw
            p.waits("vector", [t_sg, t_aw, t_thbc])
            op("vector", lambda e: e.tensor_tensor(self.awtile, self.awtile, self.thrbc, op=OP.is_gt))
            t_q = op("vector", lambda e: e.tensor_tensor(self.qtile, self.awtile, self.sgtile, op=OP.mult))
            self.p.wait("sync", t_q)
            t_qst = self.dma("sync", "qst", self.qt_dram[i * 128:(i + 1) * 128, :], self.qtile)
        self.q_prep_done = t_qst

        p.waits("vector", [t_wx, t_wdt, t_cw, t_cb, t_bdt, t_dp, t_l0b, t_a2, t_id, t_hs, t_bz])
        p.waits("scalar", [t_a2, t_bdt, t_cb, t_ec, t_on])
        p.waits("tensor", [t_id, t_wx, t_wdt])
        self.evac_tok[0] = self.evac_tok[1] = t_thbc
        self.resid_tok[2] = self.resid_tok[3] = t_thbc

    def chunk(self, b, k):
        p, op, dma = self.p, self.op, self.dma
        NDM, NDI, S, TC = self.NDM, self.NDI, self.S, self.TC
        pb = self.pb
        t0 = k * self.TC
        first, last = (k == 0), (k == self.NCH - 1)

        if self.x1_st[0] is not None:
            p.waits("sync", [self.x1_st[0], self.x1_st[1]])
        p.wait("sync", self.wcast_done)
        p.wait("scalar", (self.DV, self.DV.n))
        p.wait("vector", (self.AC, self.AC.n))
        p.wait("gpsimd", (self.DV, self.DV.n))
        p.wait("vector", (self.dsem["qst"], self.dsem["qst"].n))
        if first:
            t_ms = op("gpsimd", lambda e: e.memset(self.x0v[:, :, 0:1], 0.0))
            t_x0 = dma("sync", "x0", self.x0v[:, :, 1:258],
                       self.x_bdt[b, :, 0:257].rearrange("(c p) t -> p c t", p=128))
            x0toks = [t_ms, t_x0]
        elif last:
            t_ms = op("gpsimd", lambda e: e.memset(self.x0v[:, :, 257:258], 0.0))
            t_x0 = dma("sync", "x0", self.x0v[:, :, 0:257],
                       self.x_bdt[b, :, t0 - 1:t0 + 256].rearrange("(c p) t -> p c t", p=128))
            x0toks = [t_ms, t_x0]
        else:
            t_x0 = dma("sync", "x0", self.x0v[:, :, :],
                       self.x_bdt[b, :, t0 - 1:t0 + 257].rearrange("(c p) t -> p c t", p=128))
            x0toks = [t_x0]

        p.waits("scalar", x0toks)
        p.waits("tensor", x0toks + [self.resid_tok[2], self.resid_tok[3]])
        sq_tok = {}
        mm_x = mm_sq = None
        for c in range(NDM):
            sl = self.sqslot[:, (c % 2) * 258:(c % 2) * 258 + 258]
            if c >= 2:
                p.wait("scalar", sq_tok[c - 2])
            t_sq = op("scalar", lambda e, c=c, sl=sl: e.activation(out=sl, in_=self.x0v[:, c, :], func=AF.Square))
            p.wait("tensor", t_sq)
            mm_x = op("tensor", lambda e, c=c: e.matmul(pb[2][0:1, 0:258], self.ones_col[:, :], self.x0v[:, c, :],
                                                        start=(c == 0), stop=(c == NDM - 1)))
            mm_sq = op("tensor", lambda e, sl=sl, c=c: e.matmul(pb[3][0:1, 0:258], self.ones_col[:, :], sl,
                                                               start=(c == 0), stop=(c == NDM - 1)))
            sq_tok[c] = mm_sq
        p.waits("vector", [mm_x, mm_sq])
        mrow = self.statrow[0:1, 0:258]
        exrow = self.statrow[0:1, 258:516]
        varrow = self.statrow[0:1, 516:774]
        rrow = self.statrow[0:1, 774:1032]
        op("vector", lambda e: e.tensor_scalar(mrow, pb[2][0:1, 0:258], 1.0 / self.DM, None, op0=OP.mult))
        op("vector", lambda e: e.tensor_scalar(exrow, pb[3][0:1, 0:258], 1.0 / self.DM, None, op0=OP.mult))
        op("vector", lambda e: e.tensor_tensor(varrow, mrow, mrow, op=OP.mult))
        t_var = op("vector", lambda e: e.tensor_tensor(varrow, exrow, varrow, op=OP.subtract))
        p.wait("scalar", t_var)
        t_ln = op("scalar", lambda e: e.activation(out=rrow, in_=varrow, func=AF.Ln, bias=self.epscol[0:1, 0:1]))
        p.wait("scalar", t_ln)
        t_rstd = op("scalar", lambda e: e.activation(out=rrow, in_=rrow, func=AF.Exp, scale=-0.5))
        p.waits("tensor", [t_var, t_rstd])
        t_bm = op("tensor", lambda e: e.matmul(pb[2][:, 0:258], self.ones_row[:, :], mrow, start=True, stop=True))
        t_br = op("tensor", lambda e: e.matmul(pb[3][:, 0:258], self.ones_row[:, :], rrow, start=True, stop=True))
        p.waits("vector", [t_bm, t_br])
        xn_tok = None
        for c in range(NDM):
            sl = self.sqslot[:, (c % 2) * 258:(c % 2) * 258 + 258]
            op("vector", lambda e, c=c, sl=sl: e.tensor_tensor(sl, self.x0v[:, c, :], pb[2][:, 0:258], op=OP.subtract))
            xn_tok = op("vector", lambda e, c=c, sl=sl: e.tensor_tensor(self.xnv[:, c, :], sl, pb[3][:, 0:258], op=OP.mult))
        self.tap("xn_b%dk%d" % (b, k), self.xnbuf[:, :], [xn_tok])

        if self.stage == "ln":
            return
        win_used = None
        conv_last = None
        for og in range(16):
            if win_used is not None:
                p.wait("sync", win_used)
            t_win = dma("sync", "win", self.winv[:, :, :],
                        self.win_bf[:, og * 256:(og + 1) * 256].rearrange("(c p) o -> p c o", p=128))
            p.waits("tensor", [t_win, xn_tok])
            for oc in range(2):
                j = og * 2 + oc
                bk = j % 2
                p.wait("tensor", self.evac_tok[bk])
                mm = None
                for c in range(NDM):
                    mm = op("tensor", lambda e, c=c, oc=oc, bk=bk: e.matmul(
                        pb[bk][:, 0:258], self.winv[:, c, oc * 128:(oc + 1) * 128], self.xnv[:, c, :],
                        start=(c == 0), stop=(c == NDM - 1)))
                win_used = mm
                p.wait("scalar", mm)
                sl = self.xpslot[:, (j % 2) * 258:(j % 2) * 258 + 258]
                t_xpev = op("scalar", lambda e, sl=sl, bk=bk: e.activation(out=sl, in_=pb[bk][:, 0:258], func=AF.Copy))
                p.wait("vector", t_xpev)
                op("vector", lambda e, sl=sl, j=j: e.tensor_scalar(
                    self.xcv[:, j, 0:256], sl[:, 0:256], self.cwv[:, j, 0:1], self.cbbuf[:, j:j + 1],
                    op0=OP.mult, op1=OP.add))
                op("vector", lambda e, sl=sl, j=j: e.scalar_tensor_tensor(
                    self.xcv[:, j, 0:256], sl[:, 1:257], self.cwv[:, j, 1:2], self.xcv[:, j, 0:256],
                    op0=OP.mult, op1=OP.add))
                t_cv = op("vector", lambda e, sl=sl, j=j: e.scalar_tensor_tensor(
                    self.xcv[:, j, 0:256], sl[:, 2:258], self.cwv[:, j, 2:3], self.xcv[:, j, 0:256],
                    op0=OP.mult, op1=OP.add))
                self.evac_tok[bk] = t_cv
                conv_last = t_cv
        z_last = None
        for og in range(16, 32):
            p.wait("sync", win_used)
            t_win = dma("sync", "win", self.winv[:, :, :],
                        self.win_bf[:, og * 256:(og + 1) * 256].rearrange("(c p) o -> p c o", p=128))
            p.wait("tensor", t_win)
            for oc in range(2):
                j = (og - 16) * 2 + oc
                bk = j % 2
                p.wait("tensor", self.evac_tok[bk])
                mm = None
                for c in range(NDM):
                    mm = op("tensor", lambda e, c=c, oc=oc, bk=bk: e.matmul(
                        pb[bk][:, 0:256], self.winv[:, c, oc * 128:(oc + 1) * 128], self.xnv[:, c, 1:257],
                        start=(c == 0), stop=(c == NDM - 1)))
                win_used = mm
                p.wait("scalar", mm)
                t_z = op("scalar", lambda e, j=j, bk=bk: e.activation(out=self.zv[:, j, :], in_=pb[bk][:, 0:256], func=AF.Copy))
                self.evac_tok[bk] = t_z
                z_last = t_z
        if self.stage == "xz":
            self.x1_st = {0: None, 1: None}
            return
        p.waits("scalar", [conv_last, z_last])
        t_silu_xc = op("scalar", lambda e: e.activation(out=self.xcraw, in_=self.xcraw, func=AF.Silu))
        t_silu_z = op("scalar", lambda e: e.activation(out=self.zraw[:, :], in_=self.zraw[:, :], func=AF.Silu))
        self.tap("xc_b%dk%d" % (b, k), self.xcraw, [t_silu_xc])
        self.tap("z_b%dk%d" % (b, k), self.zraw[:, :], [t_silu_z])

        if self.stage == "silu":
            return
        p.wait("sync", t_silu_xc)
        t_xcsp = dma("sync", "xcsp", self.xc_dram[:, :].rearrange("(c p) t -> p c t", p=128),
                     self.xcv[:, :, 0:256])
        p.wait("sync", t_xcsp)
        xdbl_pe = {0: None, 1: None}
        mm2 = None
        xc_rd = {0: None, 1: None}
        if self.dtin_used is not None:
            p.waits("tensor", self.dtin_used)
        if self.dt_done[5] is not None:
            p.waits("tensor", self.dt_done[5])
        for j in range(NDI):
            csl = self.xcbf[:, (j % 2) * TC:(j % 2) * TC + TC]
            rsl = self.xcring[:, (j % 2) * 256:(j % 2) * 256 + 256]
            p.wait("sync", xc_rd[j % 2])
            t_ld = dma("sync", "xcld%d" % (j % 2), rsl, self.xc_dram[j * 128:(j + 1) * 128, :])
            p.wait("vector", xdbl_pe[j % 2])
            p.wait("vector", t_ld)
            t_c16 = op("vector", lambda e, csl=csl, rsl=rsl: e.tensor_copy(csl, rsl))
            xc_rd[j % 2] = t_c16
            p.wait("tensor", t_c16)
            op("tensor", lambda e, j=j, csl=csl: e.matmul(
                pb[4][:, 0:256], self.wxv[:, j, 0:128], csl, start=(j == 0), stop=(j == NDI - 1)))
            mm2 = op("tensor", lambda e, j=j, csl=csl: e.matmul(
                pb[5][0:32, 0:256], self.wxv[:, j, 128:160], csl, start=(j == 0), stop=(j == NDI - 1)))
            xdbl_pe[j % 2] = mm2
        p.wait("vector", mm2)
        t_dtin = op("vector", lambda e: e.tensor_copy(self.dtinbf[:, :], pb[4][:, 0:256]))
        t_bcs = op("vector", lambda e: e.tensor_copy(self.bcsb[:, :], pb[5][0:32, 0:256]))
        self.tap("bc_b%dk%d" % (b, k), self.bcsb[:, :], [t_bcs])
        p.waits("sync", [t_bcs, (self.DV, self.DV.n)])
        t_bcd = dma("sync", "bc", self.bc_dram[:, :], self.bcsb[:, :])
        p.wait("sync", t_bcd)
        tBC = None
        for s in range(S):
            dma("sync", "bc", self.Bv[:, s, :], self.bc_dram[s:s + 1, :].broadcast_to((128, TC)))
            tBC = dma("sync", "bc", self.Cv[:, s, :], self.bc_dram[S + s:S + s + 1, :].broadcast_to((128, TC)))

        if self.stage == "xdbl":
            return
        yg_tok = None
        xc_scan_rd = {0: xc_rd[0], 1: xc_rd[1]}
        for j in range(self.cfg.get("NJ", NDI)):
            dsl = self.dtslot[:, (j % 2) * TC:(j % 2) * TC + TC]
            ssl = self.spslot[:, (j % 2) * TC:(j % 2) * TC + TC]
            usl = self.dtuslot[:, (j % 2) * TC:(j % 2) * TC + TC]
            dbk = 5 if j % 2 == 0 else 4
            if j < 2:
                p.wait("tensor", t_dtin)
                p.wait("tensor", t_bcs)
            if self.dt_done[dbk] is not None:
                p.waits("tensor", self.dt_done[dbk])
            t_mmdt = op("tensor", lambda e, j=j, dbk=dbk: e.matmul(
                pb[dbk][:, 0:256], self.wdtbuf[:, j * 128:(j + 1) * 128], self.dtinbf[:, :], start=True, stop=True))
            njop = self.cfg.get("NJOP", "all")
            if njop == "mm":
                self.dt_done[dbk] = [t_mmdt]
                continue
            p.wait("scalar", t_mmdt)
            t_av = op("scalar", lambda e, j=j, dbk=dbk, ssl=ssl: e.activation(
                out=ssl, in_=pb[dbk][:, 0:256], func=AF.Abs, bias=self.bdtbuf[:, j:j + 1]))
            if njop == "av":
                self.dt_done[dbk] = [t_av]
                continue
            p.wait("scalar", t_av)
            t_e1 = op("scalar", lambda e, ssl=ssl: e.activation(out=ssl, in_=ssl, func=AF.Exp, scale=-1.0))
            if njop == "e1":
                self.dt_done[dbk] = [t_e1]
                continue
            p.wait("scalar", t_e1)
            t_l = op("scalar", lambda e, ssl=ssl: e.activation(out=ssl, in_=ssl, func=AF.Ln, bias=self.onecol[:, 0:1]))
            if njop == "l":
                self.dt_done[dbk] = [t_l]
                continue
            p.wait("vector", t_mmdt)
            p.wait("vector", t_av)
            t_m = op("vector", lambda e, j=j, dbk=dbk, dsl=dsl: e.tensor_scalar(
                dsl, pb[dbk][:, 0:256], self.bdtbuf[:, j:j + 1], 0.0, op0=OP.add, op1=OP.max))
            if njop == "m":
                self.dt_done[dbk] = [t_av, t_m]
                continue
            p.wait("vector", t_l)
            t_dt = op("vector", lambda e, dsl=dsl, ssl=ssl: e.tensor_tensor(dsl, dsl, ssl, op=OP.add))
            self.dt_done[dbk] = [t_av, t_m]
            if njop == "dt":
                continue
            rsl = self.xcring[:, (j % 2) * 256:(j % 2) * 256 + 256]
            if self.stage != "scan_sp":
                p.wait("sync", xc_scan_rd[j % 2])
                t_ldxc = dma("sync", "xcld%d" % (j % 2), rsl, self.xc_dram[j * 128:(j + 1) * 128, :])
                p.wait("vector", t_ldxc)
                t_dtu = op("vector", lambda e, usl=usl, rsl=rsl, dsl=dsl: e.tensor_tensor(
                    usl, dsl, rsl, op=OP.mult))
            if j == 0:
                p.wait("scalar", tBC)
                p.wait("vector", tBC)
            if j == 0 and b == 0 and k == 0:
                self.tap("dt_b0k0", dsl, [t_dt])
            if self.stage in ("scan_dt", "scan_sp"):
                continue
            ybk = 6 if j % 2 == 0 else 7
            p.wait("tensor", self.y_tok[ybk])
            t_y = None
            for s in range(S):
                asl = self.dAslot[:, (s % 4) * TC:(s % 4) * TC + TC]
                xsl = self.xsslot[:, (s % 4) * TC:(s % 4) * TC + TC]
                tsl = self.tmpslot[:, (s % 4) * TC:(s % 4) * TC + TC]
                prev = (s - 4) % 16
                if prev in self.scan_tok:
                    p.wait("scalar", self.scan_tok[prev])
                    p.wait("vector", self.tmp_pe_tok.get(prev))
                p.wait("scalar", t_dt)
                t_dA = op("scalar", lambda e, asl=asl, dsl=dsl, j=j, s=s: e.activation(
                    out=asl, in_=dsl, func=AF.Exp, scale=self.Abuf[:, j * S + s:j * S + s + 1]))
                p.waits("vector", [t_dA, t_dtu])
                t_xs = op("vector", lambda e, xsl=xsl, usl=usl, s=s: e.tensor_tensor(
                    xsl, usl, self.Bv[:, s, :], op=OP.mult))
                p.wait("vector", self.st_prev)
                if self.stage == "scan_noh":
                    continue
                p.wait("vector", t_dA)
                t_h = op("vector", lambda e, s=s, asl=asl, xsl=xsl, j=j, b=b: e.tensor_tensor_scan(
                    self.hvv[:, s, :], asl, xsl,
                    self.hstate[:, b * 512 + j * 16 + s:b * 512 + j * 16 + s + 1],
                    op0=OP.mult, op1=OP.add))
                self.scan_tok[s] = t_h
                if self.stage == "scan_noy":
                    continue
                t_tmp = op("vector", lambda e, s=s, tsl=tsl: e.tensor_tensor(
                    tsl, self.hvv[:, s, :], self.Cv[:, s, :], op=OP.mult))
                p.wait("tensor", t_tmp)
                t_y = op("tensor", lambda e, s=s, tsl=tsl, ybk=ybk: e.matmul(
                    pb[ybk][:, 0:256], self.identbf[:, :], tsl, start=(s == 0), stop=(s == S - 1)))
                self.tmp_pe_tok[s] = t_y
            if self.stage in ("scan_noh", "scan_noy"):
                continue
            t_st = op("vector", lambda e, j=j, b=b: e.tensor_copy(
                self.hstate[:, b * 512 + j * 16:b * 512 + j * 16 + 16], self.hvv[:, :, 255:256]))
            self.st_prev = t_st
            p.waits("vector", [t_y, t_silu_z])
            gsl = self.yg1slot[:, (j % 2) * TC:(j % 2) * TC + TC]
            t_g1 = op("vector", lambda e, gsl=gsl, rsl=rsl, j=j, ybk=ybk: e.scalar_tensor_tensor(
                gsl, rsl, self.Dbuf[:, j:j + 1], pb[ybk][:, 0:256], op0=OP.mult, op1=OP.add))
            xc_scan_rd[j % 2] = t_g1
            yg_tok = op("vector", lambda e, gsl=gsl, j=j: e.tensor_tensor(
                self.ygv[:, j, :], gsl, self.zv[:, j, :], op=OP.mult))
            self.y_tok[ybk] = t_g1
            if j == 0 and b == 0 and k == 0:
                self.tap("y_b0k0", gsl, [t_g1])
        self.scan_done = yg_tok
        if self.stage in ("scan_dt", "scan_sp", "scan_noh", "scan_noy"):
            return

        if self.stage == "scan":
            return
        for dm in range(16):
            if self.wout_used is not None:
                p.wait("sync", self.wout_used)
            t_wo = dma("sync", "wo", self.wov[:, :, :],
                       self.wout_bf[:, dm * 128:(dm + 1) * 128].rearrange("(c p) o -> p c o", p=128))
            t_q = dma("sync", "q", self.qv[:, :, :],
                      self.qt_dram[:, dm * 128:(dm + 1) * 128].rearrange("(c p) o -> p c o", p=128))
            if dm == 0:
                p.wait("sync", self.q_prep_done)
            p.waits("tensor", [t_wo, t_q, yg_tok, xn_tok])
            rbk = 2 if dm % 2 == 0 else 3
            p.wait("tensor", self.resid_tok[rbk])
            mm = None
            for j in range(NDI):
                mm = op("tensor", lambda e, j=j, rbk=rbk: e.matmul(
                    pb[rbk][:, 0:256], self.wov[:, j, :], self.ygv[:, j, :],
                    start=(j == 0), stop=False))
            for i in range(NDM):
                mm = op("tensor", lambda e, i=i, rbk=rbk: e.matmul(
                    pb[rbk][:, 0:256], self.qv[:, i, :], self.xnv[:, i, 1:257],
                    start=False, stop=(i == NDM - 1)))
            p.wait("vector", mm)
            xsl = self.x1slot[:, (dm % 2) * 256:(dm % 2) * 256 + 256]
            p.wait("vector", self.x1_st[dm % 2])
            t_x1 = op("vector", lambda e, dm=dm, rbk=rbk, xsl=xsl: e.scalar_tensor_tensor(
                xsl, pb[rbk][:, 0:256], self.l0bbuf[:, dm:dm + 1], self.x0v[:, dm, 1:257],
                op0=OP.add, op1=OP.add))
            self.resid_tok[rbk] = t_x1
            p.wait("sync", t_x1)
            self.x1_st[dm % 2] = dma("sync", "x1st", self.x1_bdt[b, dm * 128:(dm + 1) * 128, t0:t0 + TC], xsl)
            self.wout_used = mm
        self.dtin_used = self.dt_done[4]


class L1Builder:
    def __init__(self, cfg):
        self.cfg = cfg
        self.NB = cfg.get("NB", 2)
        self.L = cfg.get("L", 2048)
        self.NT = self.L // 512
        self.NG = self.L // 512
        self.DM = 2048
        self.NDM = 16
        self.NH = 16
        self.NI = 64

    def build(self, env=None):
        if env is None:
            nc = bass.Bass("TRN2", target_bir_lowering=False)
            es = ExitStack()
            p = Prog(nc, es)
        else:
            nc, es, p = env["nc"], env["es"], env["p"]
        self.env = env
        self.nc, self.es, self.p = nc, es, p
        NB, L, NT, NDM, NH, NI = self.NB, self.L, self.NT, self.NDM, self.NH, self.NI

        if env is None:
            self.x1_bdt = nc.dram_tensor("x1_bdt", [NB, 2048, L], F32, kind="ExternalInput").ap()
        else:
            self.x1_bdt = env["x1_ap"]
        self.wq_t = nc.dram_tensor("wq_t", [NH, 2048, 128], F32, kind="ExternalInput").ap()
        self.wk_t = nc.dram_tensor("wk_t", [NH, 2048, 128], F32, kind="ExternalInput").ap()
        self.wv_t = nc.dram_tensor("wv_t", [NH, 2048, 128], F32, kind="ExternalInput").ap()
        self.wo_t = nc.dram_tensor("wo_t", [NH, 128, 2048], F32, kind="ExternalInput").ap()
        self.gate_wt = nc.dram_tensor("gate_wt", [2048, 8192], I32, kind="ExternalInput").ap()
        self.up_wt = nc.dram_tensor("up_wt", [2048, 8192], I32, kind="ExternalInput").ap()
        self.down_wt = nc.dram_tensor("down_wt", [8192, 2048], I32, kind="ExternalInput").ap()
        self.gsc_p = nc.dram_tensor("gsc_p", [128, NI], F32, kind="ExternalInput").ap()
        self.usc_p = nc.dram_tensor("usc_p", [128, NI], F32, kind="ExternalInput").ap()
        self.dsc_p = nc.dram_tensor("dsc_p", [128, NDM], F32, kind="ExternalInput").ap()
        self.out_bdt = nc.dram_tensor("out_bdt", [NB, 2048, L], F32, kind="ExternalOutput").ap()
        self.zt_dram = nc.dram_tensor("zt_scratch", [NH, 128, L], BF16).ap()
        if self.cfg.get("debug"):
            self.dbg_qt = nc.dram_tensor("dbg_qt", [128, L], F32, kind="ExternalOutput").ap()
            self.dbg_kt = nc.dram_tensor("dbg_kt", [128, L], F32, kind="ExternalOutput").ap()
            self.dbg_vt = nc.dram_tensor("dbg_vt", [128, 16 * 128], F32, kind="ExternalOutput").ap()
            self.dbg_p = nc.dram_tensor("dbg_p", [128, L], F32, kind="ExternalOutput").ap()
            self.dbg_pt = nc.dram_tensor("dbg_pt", [128, 16 * 512], F32, kind="ExternalOutput").ap()
        self.h_dram = nc.dram_tensor("h_scratch", [8192, L], BF16).ap()

        if env is None:
            def sb(name, shape, dt):
                return es.enter_context(nc.sbuf_tensor(name, shape, dt))
        else:
            sb = env["alloc"]

        self.R0 = sb("R0", [128, 16 * 2048], BF16)
        self.xn1 = self.R0[:, :]
        self.xn1v = self.xn1.rearrange("p (c t) -> p c t", t=2048)
        self.hbf = self.R0[:, :]
        self.hbv = self.hbf.rearrange("p (c t) -> p c t", t=2048)
        self.hbv2 = self.R0[:, :].rearrange("p (c t) -> p c t", t=512)
        self.R1 = sb("R1", [128, 16384], F32)
        r1b = self.R1[:, :].bitcast(BF16)
        self.x1ch = self.R1[:, 0:8192].rearrange("p (c t) -> p c t", t=512)
        self.sqsl = self.R1[:, 8192:9216]
        self.Sstrip = self.R1[:, 0:2048]
        self.VTf = self.R1[:, 2048:4096]
        self.QT = r1b[:, 8192:10240]
        self.KT = r1b[:, 10240:12288]
        self.VT = r1b[:, 12288:14336]
        self.Vtok = r1b[:, 14336:16384]
        self.Vtokv = self.Vtok.rearrange("p (c e) -> p c e", e=128)
        self.PT = r1b[:, 16384:24576]
        self.PTv = self.PT.rearrange("p (c t) -> p c t", t=512)
        self.zTtile = r1b[:, 24576:28672]
        self.zTl = r1b[:, 0:16384]
        self.zTlv = r1b[:, 0:8192].rearrange("p (c t) -> p c t", t=512)
        self.x1tile = self.R1[:, 8192:16384].rearrange("p (c t) -> p c t", t=512)

        self.R2 = sb("R2", [128, 8192], F32)
        r2b = self.R2[:, :].bitcast(BF16)
        self.wqb = r2b[:, 0:2048].rearrange("p (c e) -> p c e", e=128)
        self.wkb = r2b[:, 2048:4096].rearrange("p (c e) -> p c e", e=128)
        self.wvb = r2b[:, 4096:6144].rearrange("p (c e) -> p c e", e=128)
        self.gwb = r2b[:, 0:2048].rearrange("p (c e) -> p c e", e=128)
        self.uwb = r2b[:, 2048:4096].rearrange("p (c e) -> p c e", e=128)
        self.dwb = r2b[:, 4096:6144]
        self.wob = r2b[:, 6144:16384 + 0]
        self.wobv = r2b[:, 6144:10240].rearrange("p (c e) -> p c e", e=1024)

        self.gs = sb("gs", [128, 2048], BF16)
        self.up = sb("up", [128, 2048], BF16)
        self.outsl = sb("outsl", [128, 2 * 512], F32)
        self.statrow = sb("statrow1", [1, 4 * 512], F32)
        self.masks = sb("masks", [128, 4 * 512], F32)
        self.identbf = sb("identbf1", [128, 128], BF16)
        self.identf = sb("identf1", [128, 128], F32)
        self.pidx = sb("pidx1", [128, 1], F32)
        self.fidx = sb("fidx1", [128, 512], F32)
        self.vcol = sb("vcol1", [128, 4], F32)
        self.ones_col = sb("ones_col1", [128, 1], F32)
        self.ones_row = sb("ones_row1", [1, 128], F32)
        self.epscol = sb("epscol1", [128, 1], F32)
        self.nmax = sb("nmax1", [128, 1], F32)
        self.sums = sb("sums1", [128, 1], F32)
        self.recip = sb("recip1", [128, 1], F32)
        self.gscb = sb("gscb", [128, NI], F32)
        self.uscb = sb("uscb", [128, NI], F32)
        self.dscb = sb("dscb", [128, NDM], F32)

        self.pb = env["pb"] if env is not None else [
            es.enter_context(nc.psum_tensor("ps1_%d" % i, [128, 512], F32)) for i in range(8)]

        self.dsem = {n: p.sem("E_" + n, dma=True) for n in
                     ("x1", "wqkv", "wo", "gw", "dw0", "dw1", "zt", "h", "st", "res", "zl", "hl", "tapd")}
        self.PE = p.sem("F_PE")
        self.DV = p.sem("F_DV")
        self.AC = p.sem("F_AC")
        self.PL = p.sem("F_PL")
        self.esem = {"scalar": self.AC, "vector": self.DV, "tensor": self.PE, "gpsimd": self.PL}

        self.prologue()
        for b in range(NB):
            self.layer1_batch(b)

        for sm in self.dsem.values():
            p.wait("sync", (sm, sm.n))

        if env is not None:
            return nc
        block = es.enter_context(nc.Block())
        with es:
            p.finalize(block, strict=self.cfg.get("strict", False))
        return nc

    def dma(self, q, stream, out, in_, **kw):
        sem = self.dsem[stream]
        if q == "gpsimd" and "max_dma_last_dim" not in kw:
            kw = dict(kw, max_dma_last_dim=4096)
        return self.p.op(q, lambda e, out=out, in_=in_, kw=kw: e.dma_start(out=out, in_=in_, **kw), sem, 16)

    def op(self, eng, fn):
        return self.p.op(eng, fn, self.esem[eng], 1)

    def barrier(self, engs=("scalar", "vector", "tensor", "gpsimd", "sync")):
        toks = [(s, s.n) for s in (self.PE, self.DV, self.AC, self.PL)] + \
               [(s, s.n) for s in self.dsem.values()]
        for e in engs:
            self.p.waits(e, toks)

    def prologue(self):
        p, op, dma = self.p, self.op, self.dma
        t1 = op("gpsimd", lambda e: e.iota(self.pidx[:, :], [[0, 1]], base=0, channel_multiplier=1,
                                           allow_small_or_imprecise_dtypes=True))
        t2 = op("gpsimd", lambda e: e.iota(self.fidx[:, :], [[1, 512]], base=0, channel_multiplier=0,
                                           allow_small_or_imprecise_dtypes=True))
        op("gpsimd", lambda e: e.memset(self.ones_col[:, :], 1.0))
        op("gpsimd", lambda e: e.memset(self.ones_row[:, :], 1.0))
        t5 = op("gpsimd", lambda e: e.memset(self.epscol[:, :], 1e-5))
        p.waits("vector", [t1, t2, t5])
        op("vector", lambda e: e.tensor_scalar(self.identbf[:, :], self.fidx[:, 0:128], self.pidx[:, :],
                                               None, op0=OP.is_equal))
        op("vector", lambda e: e.tensor_scalar(self.identf[:, :], self.fidx[:, 0:128], self.pidx[:, :],
                                               None, op0=OP.is_equal))
        for v in range(4):
            t_vc = op("vector", lambda e, v=v: e.tensor_scalar(self.vcol[:, v:v + 1], self.pidx[:, :], float(v * 128),
                                                               None, op0=OP.add))
            p.wait("vector", t_vc)
            op("vector", lambda e, v=v: e.tensor_scalar(self.masks[:, v * 512:(v + 1) * 512], self.fidx[:, :],
                                                        self.vcol[:, v:v + 1], -1e9, op0=OP.is_gt, op1=OP.mult))
        dma("sync", "res", self.gscb[:, :], self.gsc_p[:, :])
        dma("sync", "res", self.uscb[:, :], self.usc_p[:, :])
        t = dma("sync", "res", self.dscb[:, :], self.dsc_p[:, :])
        p.waits("scalar", [t])
        p.waits("vector", [t])
        self.barrier()

    def ln(self, b):
        p, op, dma = self.p, self.op, self.dma
        pb = self.pb
        lnrd = None
        for ch in range(self.NT):
            t0 = ch * 512
            if lnrd is not None:
                p.wait("sync", lnrd)
            t_x = dma("sync", "x1", self.x1ch[:, :, :],
                      self.x1_bdt[b, :, t0:t0 + 512].rearrange("(c p) t -> p c t", p=128))
            p.waits("scalar", [t_x])
            p.waits("tensor", [t_x])
            if lnrd is not None:
                p.wait("tensor", lnrd)
            mm_x = mm_sq = None
            sq_tok = {}
            for c in range(16):
                sl = self.sqsl[:, (c % 2) * 512:(c % 2) * 512 + 512]
                if c >= 2:
                    p.wait("scalar", sq_tok[c - 2])
                t_sq = op("scalar", lambda e, c=c, sl=sl: e.activation(out=sl, in_=self.x1ch[:, c, :], func=AF.Square))
                p.wait("tensor", t_sq)
                mm_x = op("tensor", lambda e, c=c: e.matmul(pb[0][0:1, :], self.ones_col[:, :], self.x1ch[:, c, :],
                                                            start=(c == 0), stop=(c == 15)))
                mm_sq = op("tensor", lambda e, sl=sl, c=c: e.matmul(pb[1][0:1, :], self.ones_col[:, :], sl,
                                                                   start=(c == 0), stop=(c == 15)))
                sq_tok[c] = mm_sq
            p.waits("vector", [mm_x, mm_sq])
            mrow = self.statrow[0:1, 0:512]
            exrow = self.statrow[0:1, 512:1024]
            varrow = self.statrow[0:1, 1024:1536]
            rrow = self.statrow[0:1, 1536:2048]
            op("vector", lambda e: e.tensor_scalar(mrow, pb[0][0:1, :], 1.0 / 2048, None, op0=OP.mult))
            op("vector", lambda e: e.tensor_scalar(exrow, pb[1][0:1, :], 1.0 / 2048, None, op0=OP.mult))
            op("vector", lambda e: e.tensor_tensor(varrow, mrow, mrow, op=OP.mult))
            t_var = op("vector", lambda e: e.tensor_tensor(varrow, exrow, varrow, op=OP.subtract))
            p.wait("scalar", t_var)
            t_ln = op("scalar", lambda e: e.activation(out=rrow, in_=varrow, func=AF.Ln, bias=self.epscol[0:1, 0:1]))
            p.wait("scalar", t_ln)
            t_rstd = op("scalar", lambda e: e.activation(out=rrow, in_=rrow, func=AF.Exp, scale=-0.5))
            p.waits("tensor", [t_var, t_rstd])
            t_bm = op("tensor", lambda e: e.matmul(pb[0][:, :], self.ones_row[:, :], mrow, start=True, stop=True))
            t_br = op("tensor", lambda e: e.matmul(pb[1][:, :], self.ones_row[:, :], rrow, start=True, stop=True))
            p.waits("vector", [t_bm, t_br])
            for c in range(16):
                sl = self.sqsl[:, (c % 2) * 512:(c % 2) * 512 + 512]
                op("vector", lambda e, c=c, sl=sl: e.tensor_tensor(sl, self.x1ch[:, c, :], pb[0][:, :], op=OP.subtract))
                lnrd = op("vector", lambda e, c=c, sl=sl, t0=t0: e.tensor_tensor(
                    self.xn1v[:, c, t0:t0 + 512], sl, pb[1][:, :], op=OP.mult))
        self.barrier()

    def attn_head(self, b, h):
        p, op, dma = self.p, self.op, self.dma
        pb = self.pb
        self.strip_users = []
        NT, NG = self.NT, self.NG
        t_wq = dma("gpsimd", "wqkv", self.wqb[:, :, :],
                   self.wq_t[h].rearrange("(c p) e -> p c e", p=128))
        t_wk = dma("gpsimd", "wqkv", self.wkb[:, :, :],
                   self.wk_t[h].rearrange("(c p) e -> p c e", p=128))
        t_wv = dma("gpsimd", "wqkv", self.wvb[:, :, :],
                   self.wv_t[h].rearrange("(c p) e -> p c e", p=128))
        p.waits("tensor", [t_wq, t_wk, t_wv])
        for which, wbuf, dst in (("q", self.wqb, self.QT), ("k", self.wkb, self.KT)):
            for tt in range(NT):
                bk = tt % 2
                if self.pbk_tok[bk] is not None:
                    p.wait("tensor", self.pbk_tok[bk])
                mm = None
                for c in range(16):
                    mm = op("tensor", lambda e, c=c, tt=tt, bk=bk, wbuf=wbuf: e.matmul(
                        pb[bk][:, :], wbuf[:, c, :], self.xn1v[:, c, tt * 512:(tt + 1) * 512],
                        start=(c == 0), stop=(c == 15)))
                p.wait("vector", mm)
                t_ev = op("vector", lambda e, tt=tt, bk=bk, dst=dst: e.tensor_copy(
                    dst[:, tt * 512:(tt + 1) * 512], pb[bk][:, :]))
                self.pbk_tok[bk] = t_ev
        for tt in range(NT):
            bk = tt % 2
            p.wait("tensor", self.pbk_tok[bk])
            mm = None
            for c in range(16):
                mm = op("tensor", lambda e, c=c, tt=tt, bk=bk: e.matmul(
                    pb[bk][:, :], self.wvb[:, c, :], self.xn1v[:, c, tt * 512:(tt + 1) * 512],
                    start=(c == 0), stop=(c == 15)))
            p.wait("vector", mm)
            t_ev = op("vector", lambda e, tt=tt, bk=bk: e.tensor_copy(
                self.VTf[:, tt * 512:(tt + 1) * 512], pb[bk][:, :]))
            self.pbk_tok[bk] = t_ev
        vtf_done = (self.DV, self.DV.n)
        for g4 in range(NT):
            bk = 2 + (g4 % 2)
            if self.pbk_tok[bk] is not None:
                p.wait("tensor", self.pbk_tok[bk])
            p.wait("tensor", vtf_done)
            tr = None
            for q in range(4):
                ti = g4 * 4 + q
                tr = op("tensor", lambda e, ti=ti, q=q, bk=bk: e.transpose(
                    pb[bk][:, q * 128:(q + 1) * 128], self.VTf[:, ti * 128:(ti + 1) * 128], self.identf[:, :]))
            p.wait("vector", tr)
            t_ev = op("vector", lambda e, g4=g4, bk=bk: e.tensor_copy(
                self.Vtok[:, g4 * 512:(g4 + 1) * 512], pb[bk][:, :]))
            self.pbk_tok[bk] = t_ev
        for g in range(NG):
            zbk = 4 + (g % 2)
            if self.pbk_tok[zbk] is not None:
                p.wait("tensor", self.pbk_tok[zbk])
            nseg = g + 1
            for t1 in range(4):
                ti = g * 4 + t1
                p.waits("vector", self.strip_users)
                for seg in range(nseg):
                    bk = seg % 2
                    p.wait("tensor", self.pbk_tok[bk])
                    mm = op("tensor", lambda e, ti=ti, seg=seg, bk=bk: e.matmul(
                        pb[bk][:, :], self.QT[:, ti * 128:(ti + 1) * 128], self.KT[:, seg * 512:(seg + 1) * 512],
                        start=True, stop=True))
                    p.wait("vector", mm)
                    if seg == nseg - 1:
                        t_ev = op("vector", lambda e, seg=seg, bk=bk, t1=t1: e.scalar_tensor_tensor(
                            self.Sstrip[:, seg * 512:(seg + 1) * 512], pb[bk][:, :], SCALE,
                            self.masks[:, t1 * 512:(t1 + 1) * 512], op0=OP.mult, op1=OP.add))
                    else:
                        t_ev = op("vector", lambda e, seg=seg, bk=bk: e.tensor_scalar(
                            self.Sstrip[:, seg * 512:(seg + 1) * 512], pb[bk][:, :], SCALE, None, op0=OP.mult))
                    self.pbk_tok[bk] = t_ev
                Lq = nseg * 512
                t_mx = op("vector", lambda e, Lq=Lq: e.tensor_reduce(
                    self.nmax[:, :], self.Sstrip[:, 0:Lq], axis=AX.X, op=OP.max, negate=True))
                p.wait("scalar", t_mx)
                t_exp = op("scalar", lambda e, Lq=Lq: e.activation(
                    out=self.Sstrip[:, 0:Lq], in_=self.Sstrip[:, 0:Lq], func=AF.Exp,
                    bias=self.nmax[:, :], accum_out=self.sums[:, :]))
                p.wait("vector", t_exp)
                t_rc = op("vector", lambda e: e.reciprocal(self.recip[:, :], self.sums[:, :]))
                p.wait("vector", t_rc)
                t_p = op("vector", lambda e, Lq=Lq: e.tensor_scalar(
                    self.Sstrip[:, 0:Lq], self.Sstrip[:, 0:Lq], self.recip[:, :], None, op0=OP.mult))
                self.strip_users = [t_exp]
                for g4 in range(nseg):
                    bk = 2 + (g4 % 2)
                    p.wait("tensor", self.pbk_tok[bk])
                    p.wait("tensor", t_p)
                    tr = None
                    for q in range(4):
                        sg2 = g4 * 4 + q
                        tr = op("tensor", lambda e, sg2=sg2, q=q, bk=bk: e.transpose(
                            pb[bk][:, q * 128:(q + 1) * 128], self.Sstrip[:, sg2 * 128:(sg2 + 1) * 128],
                            self.identf[:, :]))
                    p.wait("vector", tr)
                    self.strip_users.append(tr)
                    t_ev = op("vector", lambda e, g4=g4, t1=t1, bk=bk: e.tensor_copy(
                        self.PTv[:, g4 * 4:g4 * 4 + 4, t1 * 128:(t1 + 1) * 128],
                        pb[bk][:, :].rearrange("p (c t) -> p c t", t=128)))
                    self.pbk_tok[bk] = t_ev
            p.wait("tensor", self.pbk_tok[2])
            p.wait("tensor", self.pbk_tok[3])
            mm = None
            for sg2 in range(nseg * 4):
                mm = op("tensor", lambda e, sg2=sg2, zbk=zbk, nseg=nseg: e.matmul(
                    pb[zbk][:, :], self.Vtokv[:, sg2, :], self.PTv[:, sg2, :],
                    start=(sg2 == 0), stop=(sg2 == nseg * 4 - 1)))
            p.wait("vector", mm)
            t_z = op("vector", lambda e, g=g, zbk=zbk: e.tensor_copy(
                self.zTtile[:, g * 512:(g + 1) * 512], pb[zbk][:, :]))
            self.pbk_tok[zbk] = t_z
        p.wait("sync", self.pbk_tok[4])
        p.wait("sync", self.pbk_tok[5])
        t_st = dma("sync", "zt", self.zt_dram[h][:, 0:self.L], self.zTtile[:, 0:self.L])
        if self.cfg.get("debug") and b == 0 and h == 0:
            self.barrier()
            self.dma("gpsimd", "tapd", self.dbg_qt[:, :], self.QT[:, :])
            self.dma("gpsimd", "tapd", self.dbg_kt[:, :], self.KT[:, :])
            self.dma("gpsimd", "tapd", self.dbg_vt[:, :], self.Vtok[:, :])
            self.dma("sync", "zt", self.dbg_p[:, :], self.Sstrip[:, :])
            self.dma("gpsimd", "tapd", self.dbg_pt[:, :], self.PT[:, :])
        return t_st

    def ffn_gateup(self, b):
        p, op, dma = self.p, self.op, self.dma
        pb = self.pb
        for i in range(self.NI):
            if self.gw_used is not None:
                p.wait("gpsimd", self.gw_used)
            t_g = dma("gpsimd", "gw", self.gwb[:, :, :],
                      self.gate_wt[:, i * 128:(i + 1) * 128].rearrange("(c p) e -> p c e", p=128))
            t_u = dma("gpsimd", "gw", self.uwb[:, :, :],
                      self.up_wt[:, i * 128:(i + 1) * 128].rearrange("(c p) e -> p c e", p=128))
            p.waits("vector", [t_g, t_u])
            t_sg = op("vector", lambda e: e.tensor_scalar(
                self.gwb[:, :, :], self.gwb[:, :, :], -128.0, None, op0=OP.add))
            t_su = op("vector", lambda e: e.tensor_scalar(
                self.uwb[:, :, :], self.uwb[:, :, :], -128.0, None, op0=OP.add))
            p.waits("tensor", [t_sg, t_su])
            for bk in range(8):
                if self.pbk_tok[bk] is not None:
                    p.wait("tensor", self.pbk_tok[bk])
            mmg = mmu = None
            for c in range(16):
                for tt in range(self.NT):
                    mmg = op("tensor", lambda e, c=c, tt=tt: e.matmul(
                        pb[tt][:, :], self.gwb[:, c, :], self.xn1v[:, c, tt * 512:(tt + 1) * 512],
                        start=(c == 0), stop=(c == 15)))
            for c in range(16):
                for tt in range(self.NT):
                    mmu = op("tensor", lambda e, c=c, tt=tt: e.matmul(
                        pb[4 + tt][:, :], self.uwb[:, c, :], self.xn1v[:, c, tt * 512:(tt + 1) * 512],
                        start=(c == 0), stop=(c == 15)))
            self.gw_used = mmu
            p.wait("scalar", mmg)
            if self.h_store is not None:
                p.wait("scalar", self.h_store)
            t_gs = None
            for tt in range(self.NT):
                t_gs = op("scalar", lambda e, tt=tt, i=i: e.activation(
                    out=self.gs[:, tt * 512:(tt + 1) * 512], in_=pb[tt][:, :], func=AF.Silu,
                    scale=self.gscb[:, i:i + 1]))
                self.pbk_tok[tt] = t_gs
            p.wait("vector", mmu)
            t_up = None
            for tt in range(self.NT):
                t_up = op("vector", lambda e, tt=tt, i=i: e.tensor_scalar(
                    self.up[:, tt * 512:(tt + 1) * 512], pb[4 + tt][:, :], self.uscb[:, i:i + 1],
                    None, op0=OP.mult))
                self.pbk_tok[4 + tt] = t_up
            p.waits("vector", [t_gs, t_up])
            if self.h_store is not None:
                p.wait("vector", self.h_store)
            t_h = op("vector", lambda e: e.tensor_tensor(self.gs[:, 0:self.L], self.gs[:, 0:self.L],
                                                         self.up[:, 0:self.L], op=OP.mult))
            p.wait("sync", t_h)
            self.h_store = dma("sync", "h", self.h_dram[i * 128:(i + 1) * 128, 0:self.L], self.gs[:, 0:self.L])
        self.barrier()

    def down_out(self, b):
        p, op, dma = self.p, self.op, self.dma
        pb = self.pb
        prev_mm = None
        prev_ev = None
        for tt in range(self.NT):
            if prev_mm is not None:
                p.wait("sync", prev_mm)
                p.wait("sync", prev_ev)
            t_hl = dma("sync", "hl", self.hbv2[:, :, :],
                       self.h_dram[:, tt * 512:(tt + 1) * 512].rearrange("(c p) t -> p c t", p=128))
            t_x1 = dma("sync", "x1", self.x1tile[:, :, :],
                       self.x1_bdt[b, :, tt * 512:(tt + 1) * 512].rearrange("(c p) t -> p c t", p=128))
            t_zl = dma("sync", "zl", self.zTlv[:, :, :],
                       self.zt_dram[:, :, tt * 512:(tt + 1) * 512].rearrange("c p t -> p c t"))
            p.waits("tensor", [t_hl, t_x1, t_zl])
            for half in range(2):
                dw_used = None
                for bk in range(8):
                    if self.pbk_tok[bk] is not None:
                        p.wait("tensor", self.pbk_tok[bk])
                mm = None
                for i in range(self.NI):
                    sl = self.dwb[:, (i % 2) * 1024:(i % 2) * 1024 + 1024]
                    if self.dw_used[i % 2] is not None:
                        p.wait("gpsimd", self.dw_used[i % 2])
                    t_dw = dma("gpsimd", "dw%d" % (i % 2), sl,
                               self.down_wt[i * 128:(i + 1) * 128, half * 1024:(half + 1) * 1024])
                    p.wait("vector", t_dw)
                    if self.dw_sub[i % 2] is not None:
                        p.wait("vector", self.dw_sub[i % 2])
                    t_sb = op("vector", lambda e, sl=sl: e.tensor_scalar(sl, sl, -128.0, None, op0=OP.add))
                    p.wait("tensor", t_sb)
                    for dmx in range(8):
                        mm = op("tensor", lambda e, i=i, dmx=dmx, sl=sl, tt=tt: e.matmul(
                            pb[dmx][:, :], sl[:, dmx * 128:(dmx + 1) * 128], self.hbv2[:, i, :],
                            start=(i == 0), stop=False))
                    self.dw_sub[i % 2] = mm
                    self.dw_used[i % 2] = mm
                for hb in range(4):
                    if self.wo_used is not None:
                        p.wait("gpsimd", self.wo_used)
                    t_wo = dma("gpsimd", "wo", self.wobv[:, :, :],
                               self.wo_t[hb * 4:(hb + 1) * 4, :, half * 1024:(half + 1) * 1024]
                               .rearrange("c p e -> p c e"))
                    p.wait("tensor", t_wo)
                    for hq in range(4):
                        h = hb * 4 + hq
                        for dmx in range(8):
                            last = (hb == 3 and hq == 3)
                            mm = op("tensor", lambda e, h=h, hq=hq, dmx=dmx, last=last: e.matmul(
                                pb[dmx][:, :], self.wobv[:, hq, dmx * 128:(dmx + 1) * 128],
                                self.zTlv[:, h, :], start=False, stop=last))
                    self.wo_used = mm
                p.wait("vector", mm)
                for dmx in range(8):
                    dm = half * 8 + dmx
                    osl = self.outsl[:, (dmx % 2) * 512:(dmx % 2) * 512 + 512]
                    p.wait("vector", self.out_st[dmx % 2])
                    t_o = op("vector", lambda e, dm=dm, dmx=dmx, osl=osl: e.scalar_tensor_tensor(
                        osl, pb[dmx][:, :], self.dscb[:, dm:dm + 1], self.x1tile[:, dm, :],
                        op0=OP.mult, op1=OP.add))
                    self.pbk_tok[dmx] = t_o
                    p.wait("sync", t_o)
                    self.out_st[dmx % 2] = dma("sync", "st",
                                               self.out_bdt[b, dm * 128:(dm + 1) * 128, tt * 512:(tt + 1) * 512],
                                               osl)
                prev_mm = mm
                prev_ev = t_o
        self.barrier()

    def layer1_batch(self, b):
        self.pbk_tok = {k: None for k in range(8)}
        self.gw_used = None
        self.h_store = None
        self.dw_used = {0: None, 1: None}
        self.dw_sub = {0: None, 1: None}
        self.wo_used = None
        self.out_st = {0: None, 1: None}
        self.ln(b)
        if self.cfg.get("stage") == "ln":
            return
        for h in range(self.NH):
            self.attn_head(b, h)
            self.barrier()
        if self.cfg.get("stage") == "attn":
            return
        self.ffn_gateup(b)
        if self.cfg.get("stage") == "gateup":
            return
        self.down_out(b)



def _vec_p(v, nch):
    return np.ascontiguousarray(np.asarray(v, np.float32).reshape(nch, 128).T)


_CACHE = {}

ARENA_F32 = 53200


def build_fused():
    nc = bass.Bass("TRN2", target_bir_lowering=False)
    es = ExitStack()
    p = Prog(nc, es)
    pb = [es.enter_context(nc.psum_tensor("pbk%d" % i, [128, 512], F32)) for i in range(8)]
    arena = es.enter_context(nc.sbuf_tensor("arena", [128, ARENA_F32], F32))
    arena_ap = arena[:, :]
    arena_bf = arena_ap.bitcast(BF16)
    x1_ap = nc.dram_tensor("x1_bdt", [2, 2048, 2048], F32).ap()
    off = [0]

    def alloc(name, shape, dt):
        nelem = shape[1]
        nf32 = (nelem + 1) // 2 if dt == BF16 else nelem
        a = off[0]
        off[0] = a + nf32 + ((-(a + nf32)) % 16)
        assert off[0] <= ARENA_F32, "arena overflow at %s (%d)" % (name, off[0])
        if dt == BF16:
            v = arena_bf[:, 2 * a:2 * a + nelem]
        else:
            v = arena_ap[:, a:a + nelem]
        if shape[0] != 128:
            v = v[0:shape[0], :]
        return v

    env = {"nc": nc, "es": es, "p": p, "pb": pb, "alloc": alloc, "x1_ap": x1_ap}
    b0 = L0Builder({"NB": 2, "L": 2048})
    b0.build(env)
    l0_toks = [(sm, sm.n) for sm in b0.dsem.values()] +               [(sm, sm.n) for sm in (b0.PE, b0.DV, b0.AC, b0.PL)]
    for e in ENGS:
        p.waits(e, l0_toks)
    off[0] = 0
    b1 = L1Builder({"NB": 2, "L": 2048})
    b1.build(env)
    block = es.enter_context(nc.Block())
    with es:
        p.finalize(block)
    return nc


def _get_prog():
    if "fused" not in _CACHE:
        _CACHE["fused"] = build_fused()
    return _CACHE["fused"]


def kernel(L0_attn_x, L0_attn_W_in, L0_attn_conv_w, L0_attn_conv_b, L0_attn_W_x,
           L0_attn_W_dt, L0_attn_b_dt, L0_attn_A_log, L0_attn_D, L0_attn_W_out,
           L0_ffn_weight, L0_ffn_bias, L1_attn_W_Q, L1_attn_W_K, L1_attn_W_V,
           L1_attn_W_O, L1_ffn_gate_w, L1_ffn_gate_scale, L1_ffn_up_w,
           L1_ffn_up_scale, L1_ffn_down_w, L1_ffn_down_scale):
    x = np.asarray(L0_attn_x, np.float32)
    NBC = 2
    n_cores = 8
    f32 = lambda a: np.ascontiguousarray(np.asarray(a, np.float32))
    i32 = lambda a: np.ascontiguousarray(np.asarray(a, np.int32))

    w_l0 = {
        "w_in": f32(L0_attn_W_in),
        "conv_w_p": np.ascontiguousarray(np.asarray(L0_attn_conv_w, np.float32)
                                         .reshape(32, 128, 3).transpose(1, 0, 2).reshape(128, 96)),
        "conv_b_p": _vec_p(L0_attn_conv_b, 32),
        "w_x": f32(L0_attn_W_x),
        "w_dt": f32(L0_attn_W_dt),
        "b_dt_p": _vec_p(L0_attn_b_dt, 32),
        "a_log_p": np.ascontiguousarray(np.asarray(L0_attn_A_log, np.float32)
                                        .reshape(32, 128, 16).transpose(1, 0, 2).reshape(128, 512)),
        "d_p": _vec_p(L0_attn_D, 32),
        "w_out": f32(L0_attn_W_out),
        "l0w_t": np.ascontiguousarray(np.asarray(L0_ffn_weight, np.float32).T),
        "l0b_p": _vec_p(L0_ffn_bias, 16),
    }
    w_l1 = {
        "wq_t": np.ascontiguousarray(np.asarray(L1_attn_W_Q, np.float32).transpose(0, 2, 1)),
        "wk_t": np.ascontiguousarray(np.asarray(L1_attn_W_K, np.float32).transpose(0, 2, 1)),
        "wv_t": np.ascontiguousarray(np.asarray(L1_attn_W_V, np.float32).transpose(0, 2, 1)),
        "wo_t": np.ascontiguousarray(np.asarray(L1_attn_W_O, np.float32).transpose(0, 2, 1)),
        "gate_wt": np.ascontiguousarray(np.asarray(L1_ffn_gate_w, np.int32).T),
        "up_wt": np.ascontiguousarray(np.asarray(L1_ffn_up_w, np.int32).T),
        "down_wt": np.ascontiguousarray(np.asarray(L1_ffn_down_w, np.int32).T),
        "gsc_p": _vec_p(L1_ffn_gate_scale, 64),
        "usc_p": _vec_p(L1_ffn_up_scale, 64),
        "dsc_p": _vec_p(L1_ffn_down_scale, 16),
    }

    ncf = _get_prog()
    in_maps = []
    for c in range(n_cores):
        xs = np.ascontiguousarray(x[c * NBC:(c + 1) * NBC].transpose(0, 2, 1))
        m = dict(w_l0, x_bdt=xs)
        m.update(w_l1)
        in_maps.append(m)
    res = run_bass_kernel_spmd(ncf, in_maps, core_ids=list(range(n_cores)))
    out = np.empty((16, 2048, 2048), np.float32)
    for c in range(n_cores):
        out[c * NBC:(c + 1) * NBC] = res.results[c]["out_bdt"].transpose(0, 2, 1)
    return out


# version 10 (final)
# speedup vs baseline: 1.0439x; 1.0026x over previous
import numpy as np
from contextlib import ExitStack

import concourse.bass as bass
import concourse.mybir as mybir
from concourse.bass_utils import run_bass_kernel_spmd

F32 = mybir.dt.float32
BF16 = mybir.dt.bfloat16
I32 = mybir.dt.int32
AF = mybir.ActivationFunctionType
OP = mybir.AluOpType
AX = mybir.AxisListType

ENGS = ("sync", "scalar", "vector", "tensor", "gpsimd")
SCALE = 1.0 / np.sqrt(128.0)


class Sem:
    def __init__(self, h, name, dma=False):
        self.h = h
        self.name = name
        self.n = 0
        self.dma = dma


class Prog:
    def __init__(self, nc, es):
        self.nc = nc
        self.es = es
        self.items = {e: [] for e in ENGS}

    def sem(self, name, dma=False):
        return Sem(self.es.enter_context(self.nc.semaphore(name)), name, dma=dma)

    def wait(self, eng, tok):
        if tok is None:
            return
        sem, val = tok
        if sem.dma:
            val = sem.n
        self.items[eng].append(("w", sem, val))

    def waits(self, eng, toks):
        for t in toks:
            self.wait(eng, t)

    def op(self, eng, fn, sem, k=1):
        sem.n += k
        self.items[eng].append(("o", fn, [(sem, k)]))
        return (sem, sem.n)

    def finalize(self, block, strict=False):
        for ename in ENGS:
            items = self.items[ename]

            def body(eng, items=items):
                seen = {}
                waited = {}
                for it in items:
                    if it[0] == "w":
                        if waited.get(it[1], -1) >= it[2]:
                            continue
                        waited[it[1]] = it[2]
                        eng.wait_ge(it[1].h, it[2])
                    else:
                        if strict:
                            for s, k in it[2]:
                                if not s.dma and s in seen:
                                    eng.wait_ge(s.h, seen[s])
                        ins = it[1](eng)
                        for s, k in it[2]:
                            seen[s] = seen.get(s, 0) + k
                            ins.then_inc(s.h, k)

            getattr(block, ename)(body)


class L0Builder:
    def __init__(self, cfg):
        self.cfg = cfg
        self.NB = cfg.get("NB", 2)
        self.L = cfg.get("L", 2048)
        self.TC = 256
        self.NCH = self.L // self.TC
        self.DM = 2048
        self.NDM = 16
        self.DI = 4096
        self.NDI = 32
        self.S = 16
        self.taps = dict(cfg.get("taps", []))

    def build(self, env=None):
        if env is None:
            nc = bass.Bass("TRN2", target_bir_lowering=False)
            es = ExitStack()
            p = Prog(nc, es)
        else:
            nc, es, p = env["nc"], env["es"], env["p"]
        self.env = env
        self.nc, self.es, self.p = nc, es, p
        NB, L, TC, NCH = self.NB, self.L, self.TC, self.NCH
        DM, NDM, DI, NDI, S = self.DM, self.NDM, self.DI, self.NDI, self.S

        self.x_bdt = nc.dram_tensor("x_bdt", [NB, DM, L], F32, kind="ExternalInput").ap()
        self.w_in = nc.dram_tensor("w_in", [DM, 2 * DI], F32, kind="ExternalInput").ap()
        self.conv_w_p = nc.dram_tensor("conv_w_p", [128, NDI * 3], F32, kind="ExternalInput").ap()
        self.conv_b_p = nc.dram_tensor("conv_b_p", [128, NDI], F32, kind="ExternalInput").ap()
        self.w_x = nc.dram_tensor("w_x", [DI, 160], F32, kind="ExternalInput").ap()
        self.w_dt = nc.dram_tensor("w_dt", [128, DI], F32, kind="ExternalInput").ap()
        self.b_dt_p = nc.dram_tensor("b_dt_p", [128, NDI], F32, kind="ExternalInput").ap()
        self.a_log_p = nc.dram_tensor("a_log_p", [128, NDI * S], F32, kind="ExternalInput").ap()
        self.d_p = nc.dram_tensor("d_p", [128, NDI], F32, kind="ExternalInput").ap()
        self.w_out = nc.dram_tensor("w_out", [DI, DM], F32, kind="ExternalInput").ap()
        self.l0w_t = nc.dram_tensor("l0w_t", [DM, DM], F32, kind="ExternalInput").ap()
        self.l0b_p = nc.dram_tensor("l0b_p", [128, NDM], F32, kind="ExternalInput").ap()
        if env is None:
            self.x1_bdt = nc.dram_tensor("x1_bdt", [NB, DM, L], F32, kind="ExternalOutput").ap()
        else:
            self.x1_bdt = env["x1_ap"]
        self.qt_dram = nc.dram_tensor("qt_scratch", [DM, DM], BF16).ap()
        self.win_bf = nc.dram_tensor("win_bf", [DM, 2 * DI], BF16).ap()
        self.wout_bf = nc.dram_tensor("wout_bf", [DI, DM], BF16).ap()
        self.bc_dram = nc.dram_tensor("bc_scratch", [2 * S, TC], F32).ap()
        self.xc_dram = nc.dram_tensor("xc_scratch", [4096, 256], F32).ap()
        self.tap_outs = {}
        for tname, tshape in self.taps.items():
            dt = F32
            self.tap_outs[tname] = nc.dram_tensor("tap_" + tname, list(tshape), dt, kind="ExternalOutput").ap()

        if env is None:
            def sb(name, shape, dt):
                return es.enter_context(nc.sbuf_tensor(name, shape, dt))
        else:
            sb = env["alloc"]

        self.x0buf = sb("x0buf", [128, NDM * 258], F32)
        self.xnbuf = sb("xnbuf", [128, NDM * 258], BF16)
        self.winbuf = sb("winbuf", [128, NDM * 256], BF16)
        self.xpslot = sb("xpslot", [128, 2 * 258], F32)
        self.sqslot = sb("sqslot", [128, 2 * 258], F32)
        self.big1 = sb("big1", [128, NDI * 258], F32)
        self.xcraw = self.big1[:, 0:NDI * 258]
        self.zraw = sb("zraw", [128, NDI * TC], F32)
        self.xcring = sb("xcring", [128, 2 * 256], F32)
        self.xcbf = sb("xcbf", [128, 2 * TC], BF16)
        self.statrow = sb("statrow", [1, 4 * 258], F32)
        self.dtinbf = sb("dtinbf", [128, TC], BF16)
        self.bcsb = sb("bcsb", [2 * S, TC], F32)
        self.Bb = self.big1[:, 0:S * TC]
        self.Cb = self.big1[:, S * TC:2 * S * TC]
        self.dtslot = sb("dtslot", [128, 2 * TC], F32)
        self.spslot = sb("spslot", [128, 2 * TC], F32)
        self.dtuslot = sb("dtuslot", [128, 2 * TC], F32)
        self.dAslot = sb("dAslot", [128, 4 * TC], F32)
        self.xsslot = sb("xsslot", [128, 4 * TC], F32)
        self.hbuf = sb("hbuf", [128, S * TC], F32)
        self.tmpslot = sb("tmpslot", [128, 4 * TC], BF16)
        self.ygbuf = sb("ygbuf", [128, NDI * TC], BF16)
        self.yg1slot = sb("yg1slot", [128, 2 * TC], F32)
        self.woutsl = sb("woutsl", [128, NDI * 128], BF16)
        self.qsl = sb("qsl", [128, NDM * 128], BF16)
        self.x1slot = sb("x1slot", [128, 2 * 256], F32)
        self.hstate = sb("hstate", [128, NB * 512], F32)
        self.wxbuf = sb("wxbuf", [128, NDI * 160], BF16)
        self.wdtbuf = sb("wdtbuf", [128, DI], BF16)
        self.Abuf = sb("Abuf", [128, NDI * S], F32)
        self.cwbuf = sb("cwbuf", [128, NDI * 3], F32)
        self.cbbuf = sb("cbbuf", [128, NDI], F32)
        self.bdtbuf = sb("bdtbuf", [128, NDI], F32)
        self.Dbuf = sb("Dbuf", [128, NDI], F32)
        self.l0bbuf = sb("l0bbuf", [128, NDM], F32)
        self.identbf = sb("identbf", [128, 128], BF16)
        self.pidx = sb("pidx", [128, 1], F32)
        self.fidx = sb("fidx", [128, 128], F32)
        self.ones_col = sb("ones_col", [128, 1], F32)
        self.ones_row = sb("ones_row", [1, 128], F32)
        self.epscol = sb("epscol", [128, 1], F32)
        self.onecol = sb("onecol", [128, 1], F32)
        self.qtile = self.zraw[:, 0:1024].bitcast(BF16)
        self.throw = self.zraw[0:1, 1024:3072]
        self.wtile = self.x0buf[:, 0:2048]
        self.awtile = self.x0buf[:, 2048:4096]
        self.sgtile = self.big1[:, 0:2048]
        self.thrbc = self.big1[:, 2048:4096]
        self.pb = env["pb"] if env is not None else [
            es.enter_context(nc.psum_tensor("pbk%d" % i, [128, 512], F32)) for i in range(8)]

        self.x0v = self.x0buf[:, :].rearrange("p (c t) -> p c t", t=258)
        self.xnv = self.xnbuf[:, :].rearrange("p (c t) -> p c t", t=258)
        self.winv = self.winbuf[:, :].rearrange("p (c o) -> p c o", o=256)
        self.xcv = self.xcraw.rearrange("p (c t) -> p c t", t=258)
        self.zv = self.zraw[:, :].rearrange("p (c t) -> p c t", t=TC)
        self.Bv = self.Bb.rearrange("p (s t) -> p s t", t=TC)
        self.Cv = self.Cb.rearrange("p (s t) -> p s t", t=TC)
        self.hvv = self.hbuf[:, :].rearrange("p (s t) -> p s t", t=TC)
        self.ygv = self.ygbuf[:, :].rearrange("p (c t) -> p c t", t=TC)
        self.wov = self.woutsl[:, :].rearrange("p (c o) -> p c o", o=128)
        self.qv = self.qsl[:, :].rearrange("p (c o) -> p c o", o=128)
        self.wxv = self.wxbuf[:, :].rearrange("p (c r) -> p c r", r=160)
        self.cwv = self.cwbuf[:, :].rearrange("p (c k) -> p c k", k=3)

        self.dsem = {n: p.sem("D_" + n, dma=True) for n in
                     ("x0", "res", "wtile", "qst", "q", "bc", "xcsp", "xcld0", "xcld1",
                      "x1st", "win", "wo", "tap", "wres", "wcst", "wci")}
        self.PE = p.sem("PE")
        self.DV = p.sem("DV")
        self.AC = p.sem("AC")
        self.PL = p.sem("PL")
        self.esem = {"scalar": self.AC, "vector": self.DV, "tensor": self.PE, "gpsimd": self.PL}

        self.evac_tok = {0: None, 1: None}
        self.resid_tok = {2: None, 3: None}
        self.dt_done = {4: None, 5: None}
        self.y_tok = {6: None, 7: None}
        self.x1_st = {0: None, 1: None}
        self.scan_tok = {}
        self.tmp_pe_tok = {}
        self.tmp_tok = {}
        self.st_prev = None
        self.wout_used = None
        self.prev_xn = None
        self.prev_rstd = None
        self.q_used = None
        self.bb_used = None
        self.dtin_used = None

        self.stage = self.cfg.get("stage", "full")
        self.prologue()
        if self.stage != "prologue":
            nbl = self.cfg.get("NBL", NB)
            nchl = self.cfg.get("NCHL", NCH)
            for b in range(nbl):
                for k in range(nchl):
                    self.chunk(b, k)

        for sm in self.dsem.values():
            p.wait("sync", (sm, sm.n))
        p.wait("gpsimd", (self.dsem["tap"], self.dsem["tap"].n))

        if env is not None:
            return nc
        block = es.enter_context(nc.Block())
        with es:
            p.finalize(block, strict=self.cfg.get("strict", False))
        return nc

    def dma(self, q, stream, out, in_, **kw):
        sem = self.dsem[stream]
        if q == "gpsimd" and "max_dma_last_dim" not in kw:
            kw = dict(kw, max_dma_last_dim=4096)
        return self.p.op(q, lambda e, out=out, in_=in_, kw=kw: e.dma_start(out=out, in_=in_, **kw), sem, 16)

    def op(self, eng, fn):
        return self.p.op(eng, fn, self.esem[eng], 1)

    def tap(self, name, src_ap, after_toks):
        if name in self.tap_outs:
            self.p.waits("gpsimd", after_toks)
            self.dma("gpsimd", "tap", self.tap_outs[name][:, :], src_ap)

    def prologue(self):
        p, op, dma = self.p, self.op, self.dma
        NDM = self.NDM
        t_io = op("gpsimd", lambda e: e.iota(self.pidx[:, :], [[0, 1]], base=0, channel_multiplier=1,
                                             allow_small_or_imprecise_dtypes=True))
        t_io2 = op("gpsimd", lambda e: e.iota(self.fidx[:, :], [[1, 128]], base=0, channel_multiplier=0,
                                              allow_small_or_imprecise_dtypes=True))
        t_oc = op("gpsimd", lambda e: e.memset(self.ones_col[:, :], 1.0))
        t_or = op("gpsimd", lambda e: e.memset(self.ones_row[:, :], 1.0))
        t_ec = op("gpsimd", lambda e: e.memset(self.epscol[:, :], 1e-5))
        t_on = op("gpsimd", lambda e: e.memset(self.onecol[:, :], 1.0))
        t_hs = op("gpsimd", lambda e: e.memset(self.hstate[:, :], 0.0))
        t_bz = op("gpsimd", lambda e: e.memset(self.big1[:, :], 0.0))
        p.waits("vector", [t_io, t_io2])
        t_id = op("vector", lambda e: e.tensor_scalar(self.identbf[:, :], self.fidx[:, :], self.pidx[:, :],
                                                      None, op0=OP.is_equal))
        t_wx = dma("gpsimd", "wres", self.wxv[:, :, :], self.w_x[:, :].rearrange("(c p) r -> p c r", p=128))
        wcast = self.x0buf[:, :].bitcast(BF16)
        t_pc = None
        for c in range(16):
            if t_pc is not None:
                p.wait("gpsimd", t_pc)
            t_c = dma("gpsimd", "wci", wcast[:, 0:8192], self.w_in[c * 128:(c + 1) * 128, :])
            p.wait("sync", t_c)
            t_pc = dma("sync", "wcst", self.win_bf[c * 128:(c + 1) * 128, :], wcast[:, 0:8192])
        for c in range(32):
            p.wait("gpsimd", t_pc)
            t_c = dma("gpsimd", "wci", wcast[:, 0:2048], self.w_out[c * 128:(c + 1) * 128, :])
            p.wait("sync", t_c)
            t_pc = dma("sync", "wcst", self.wout_bf[c * 128:(c + 1) * 128, :], wcast[:, 0:2048])
        self.wcast_done = t_pc
        p.wait("sync", self.wcast_done)
        t_wdt = dma("gpsimd", "wres", self.wdtbuf[:, :], self.w_dt[:, :])
        t_cw = dma("sync", "res", self.cwbuf[:, :], self.conv_w_p[:, :])
        t_cb = dma("sync", "res", self.cbbuf[:, :], self.conv_b_p[:, :])
        t_bdt = dma("sync", "res", self.bdtbuf[:, :], self.b_dt_p[:, :])
        t_dp = dma("sync", "res", self.Dbuf[:, :], self.d_p[:, :])
        t_l0b = dma("sync", "res", self.l0bbuf[:, :], self.l0b_p[:, :])
        t_al = dma("sync", "res", self.Abuf[:, :], self.a_log_p[:, :])
        p.wait("scalar", t_al)
        t_a1 = op("scalar", lambda e: e.activation(out=self.Abuf[:, :], in_=self.Abuf[:, :], func=AF.Exp))
        p.wait("vector", t_a1)
        t_a2 = op("vector", lambda e: e.tensor_scalar(self.Abuf[:, :], self.Abuf[:, :], -1.0, None, op0=OP.mult))

        p.waits("scalar", [t_bz, t_hs])
        p.waits("vector", [t_bz])
        tok_mm = None
        tok_aw_reader = None
        prev_aw = None
        for i in range(NDM):
            if prev_aw is not None:
                p.wait("sync", prev_aw)
            t_w = dma("sync", "wtile", self.wtile, self.l0w_t[i * 128:(i + 1) * 128, :])
            p.wait("scalar", t_w)
            if tok_aw_reader is not None:
                p.wait("scalar", tok_aw_reader)
            t_aw = op("scalar", lambda e: e.activation(out=self.awtile, in_=self.wtile, func=AF.Abs))
            prev_aw = t_aw
            p.wait("tensor", t_aw)
            if i == 0:
                p.wait("tensor", t_oc)
            for g in range(4):
                tok_mm = op("tensor", lambda e, g=g, i=i: e.matmul(
                    self.pb[g][0:1, :], self.ones_col[:, :], self.awtile[:, g * 512:(g + 1) * 512],
                    start=(i == 0), stop=(i == NDM - 1)))
            tok_aw_reader = tok_mm
        p.wait("vector", tok_mm)
        for g in range(4):
            op("vector", lambda e, g=g: e.tensor_copy(self.throw[:, g * 512:(g + 1) * 512], self.pb[g][0:1, :]))
        op("vector", lambda e: e.tensor_scalar(self.throw, self.throw, 1.0 / self.DM, None, op0=OP.mult))
        t_tr = op("vector", lambda e: e.tensor_scalar(self.throw, self.throw, 0.7, None, op0=OP.mult))
        p.waits("tensor", [t_tr, t_or])
        t_bc = None
        for g in range(4):
            t_bc = op("tensor", lambda e, g=g: e.matmul(
                self.pb[g][:, :], self.ones_row[:, :], self.throw[:, g * 512:(g + 1) * 512], start=True, stop=True))
        p.wait("vector", t_bc)
        t_thbc = None
        for g in range(4):
            t_thbc = op("vector", lambda e, g=g: e.tensor_copy(self.thrbc[:, g * 512:(g + 1) * 512], self.pb[g][:, :]))
        t_qst = None
        p.waits("scalar", [tok_mm, t_bc])
        for i in range(NDM):
            p.wait("sync", prev_aw)
            t_w = dma("sync", "wtile", self.wtile, self.l0w_t[i * 128:(i + 1) * 128, :])
            p.wait("scalar", t_w)
            if t_qst is not None:
                p.wait("scalar", t_qst)
            t_sg = op("scalar", lambda e: e.activation(out=self.sgtile, in_=self.wtile, func=AF.Sign))
            t_aw = op("scalar", lambda e: e.activation(out=self.awtile, in_=self.wtile, func=AF.Abs))
            prev_aw = t_aw
            p.waits("vector", [t_sg, t_aw, t_thbc])
            op("vector", lambda e: e.tensor_tensor(self.awtile, self.awtile, self.thrbc, op=OP.is_gt))
            t_q = op("vector", lambda e: e.tensor_tensor(self.qtile, self.awtile, self.sgtile, op=OP.mult))
            self.p.wait("sync", t_q)
            t_qst = self.dma("sync", "qst", self.qt_dram[i * 128:(i + 1) * 128, :], self.qtile)
        self.q_prep_done = t_qst

        p.waits("vector", [t_wx, t_wdt, t_cw, t_cb, t_bdt, t_dp, t_l0b, t_a2, t_id, t_hs, t_bz])
        p.waits("scalar", [t_a2, t_bdt, t_cb, t_ec, t_on])
        p.waits("tensor", [t_id, t_wx, t_wdt])
        self.evac_tok[0] = self.evac_tok[1] = t_thbc
        self.resid_tok[2] = self.resid_tok[3] = t_thbc

    def chunk(self, b, k):
        p, op, dma = self.p, self.op, self.dma
        NDM, NDI, S, TC = self.NDM, self.NDI, self.S, self.TC
        pb = self.pb
        t0 = k * self.TC
        first, last = (k == 0), (k == self.NCH - 1)

        if self.x1_st[0] is not None:
            p.waits("sync", [self.x1_st[0], self.x1_st[1]])
        p.wait("sync", self.wcast_done)
        p.wait("scalar", self.prev_xn)
        p.wait("vector", self.prev_rstd)
        p.wait("vector", (self.dsem["qst"], self.dsem["qst"].n))
        if first or last:
            p.wait("gpsimd", (self.DV, self.DV.n))
        if first:
            t_ms = op("gpsimd", lambda e: e.memset(self.x0v[:, :, 0:1], 0.0))
            t_x0 = dma("sync", "x0", self.x0v[:, :, 1:258],
                       self.x_bdt[b, :, 0:257].rearrange("(c p) t -> p c t", p=128))
            x0toks = [t_ms, t_x0]
        elif last:
            t_ms = op("gpsimd", lambda e: e.memset(self.x0v[:, :, 257:258], 0.0))
            t_x0 = dma("sync", "x0", self.x0v[:, :, 0:257],
                       self.x_bdt[b, :, t0 - 1:t0 + 256].rearrange("(c p) t -> p c t", p=128))
            x0toks = [t_ms, t_x0]
        else:
            t_x0 = dma("sync", "x0", self.x0v[:, :, :],
                       self.x_bdt[b, :, t0 - 1:t0 + 257].rearrange("(c p) t -> p c t", p=128))
            x0toks = [t_x0]

        p.waits("scalar", x0toks)
        p.waits("tensor", x0toks + [self.resid_tok[2], self.resid_tok[3]])
        sq_tok = {}
        mm_x = mm_sq = None
        for c in range(NDM):
            sl = self.sqslot[:, (c % 2) * 258:(c % 2) * 258 + 258]
            if c >= 2:
                p.wait("scalar", sq_tok[c - 2])
            t_sq = op("scalar", lambda e, c=c, sl=sl: e.activation(out=sl, in_=self.x0v[:, c, :], func=AF.Square))
            p.wait("tensor", t_sq)
            mm_x = op("tensor", lambda e, c=c: e.matmul(pb[2][0:1, 0:258], self.ones_col[:, :], self.x0v[:, c, :],
                                                        start=(c == 0), stop=(c == NDM - 1)))
            mm_sq = op("tensor", lambda e, sl=sl, c=c: e.matmul(pb[3][0:1, 0:258], self.ones_col[:, :], sl,
                                                               start=(c == 0), stop=(c == NDM - 1)))
            sq_tok[c] = mm_sq
        p.waits("vector", [mm_x, mm_sq])
        mrow = self.statrow[0:1, 0:258]
        exrow = self.statrow[0:1, 258:516]
        varrow = self.statrow[0:1, 516:774]
        rrow = self.statrow[0:1, 774:1032]
        op("vector", lambda e: e.tensor_scalar(mrow, pb[2][0:1, 0:258], 1.0 / self.DM, None, op0=OP.mult))
        op("vector", lambda e: e.tensor_scalar(exrow, pb[3][0:1, 0:258], 1.0 / self.DM, None, op0=OP.mult))
        op("vector", lambda e: e.tensor_tensor(varrow, mrow, mrow, op=OP.mult))
        t_var = op("vector", lambda e: e.tensor_tensor(varrow, exrow, varrow, op=OP.subtract))
        p.wait("scalar", t_var)
        t_ln = op("scalar", lambda e: e.activation(out=rrow, in_=varrow, func=AF.Ln, bias=self.epscol[0:1, 0:1]))
        p.wait("scalar", t_ln)
        t_rstd = op("scalar", lambda e: e.activation(out=rrow, in_=rrow, func=AF.Exp, scale=-0.5))
        p.waits("tensor", [t_var, t_rstd])
        t_bm = op("tensor", lambda e: e.matmul(pb[2][:, 0:258], self.ones_row[:, :], mrow, start=True, stop=True))
        t_br = op("tensor", lambda e: e.matmul(pb[3][:, 0:258], self.ones_row[:, :], rrow, start=True, stop=True))
        p.waits("vector", [t_bm, t_br])
        xn_tok = None
        for c in range(NDM):
            sl = self.sqslot[:, (c % 2) * 258:(c % 2) * 258 + 258]
            op("vector", lambda e, c=c, sl=sl: e.tensor_tensor(sl, self.x0v[:, c, :], pb[2][:, 0:258], op=OP.subtract))
            xn_tok = op("vector", lambda e, c=c, sl=sl: e.tensor_tensor(self.xnv[:, c, :], sl, pb[3][:, 0:258], op=OP.mult))
        self.prev_xn = xn_tok
        self.prev_rstd = t_rstd
        self.tap("xn_b%dk%d" % (b, k), self.xnbuf[:, :], [xn_tok])

        if self.stage == "ln":
            return
        win_used = None
        conv_last = None
        for og in range(16):
            if win_used is not None:
                p.wait("sync", win_used)
            t_win = dma("sync", "win", self.winv[:, :, :],
                        self.win_bf[:, og * 256:(og + 1) * 256].rearrange("(c p) o -> p c o", p=128))
            p.waits("tensor", [t_win, xn_tok])
            for oc in range(2):
                j = og * 2 + oc
                bk = j % 2
                p.wait("tensor", self.evac_tok[bk])
                mm = None
                for c in range(NDM):
                    mm = op("tensor", lambda e, c=c, oc=oc, bk=bk: e.matmul(
                        pb[bk][:, 0:258], self.winv[:, c, oc * 128:(oc + 1) * 128], self.xnv[:, c, :],
                        start=(c == 0), stop=(c == NDM - 1)))
                win_used = mm
                p.wait("scalar", mm)
                sl = self.xpslot[:, (j % 2) * 258:(j % 2) * 258 + 258]
                t_xpev = op("scalar", lambda e, sl=sl, bk=bk: e.activation(out=sl, in_=pb[bk][:, 0:258], func=AF.Copy))
                p.wait("vector", t_xpev)
                op("vector", lambda e, sl=sl, j=j: e.tensor_scalar(
                    self.xcv[:, j, 0:256], sl[:, 0:256], self.cwv[:, j, 0:1], self.cbbuf[:, j:j + 1],
                    op0=OP.mult, op1=OP.add))
                op("vector", lambda e, sl=sl, j=j: e.scalar_tensor_tensor(
                    self.xcv[:, j, 0:256], sl[:, 1:257], self.cwv[:, j, 1:2], self.xcv[:, j, 0:256],
                    op0=OP.mult, op1=OP.add))
                t_cv = op("vector", lambda e, sl=sl, j=j: e.scalar_tensor_tensor(
                    self.xcv[:, j, 0:256], sl[:, 2:258], self.cwv[:, j, 2:3], self.xcv[:, j, 0:256],
                    op0=OP.mult, op1=OP.add))
                self.evac_tok[bk] = t_cv
                conv_last = t_cv
        z_last = None
        for og in range(16, 32):
            p.wait("sync", win_used)
            t_win = dma("sync", "win", self.winv[:, :, :],
                        self.win_bf[:, og * 256:(og + 1) * 256].rearrange("(c p) o -> p c o", p=128))
            p.wait("tensor", t_win)
            for oc in range(2):
                j = (og - 16) * 2 + oc
                bk = j % 2
                p.wait("tensor", self.evac_tok[bk])
                mm = None
                for c in range(NDM):
                    mm = op("tensor", lambda e, c=c, oc=oc, bk=bk: e.matmul(
                        pb[bk][:, 0:256], self.winv[:, c, oc * 128:(oc + 1) * 128], self.xnv[:, c, 1:257],
                        start=(c == 0), stop=(c == NDM - 1)))
                win_used = mm
                p.wait("scalar", mm)
                t_z = op("scalar", lambda e, j=j, bk=bk: e.activation(out=self.zv[:, j, :], in_=pb[bk][:, 0:256], func=AF.Copy))
                self.evac_tok[bk] = t_z
                z_last = t_z
        if self.stage == "xz":
            self.x1_st = {0: None, 1: None}
            return
        p.waits("scalar", [conv_last, z_last])
        t_silu_xc = op("scalar", lambda e: e.activation(out=self.xcraw, in_=self.xcraw, func=AF.Silu))
        t_silu_z = op("scalar", lambda e: e.activation(out=self.zraw[:, :], in_=self.zraw[:, :], func=AF.Silu))
        self.tap("xc_b%dk%d" % (b, k), self.xcraw, [t_silu_xc])
        self.tap("z_b%dk%d" % (b, k), self.zraw[:, :], [t_silu_z])

        if self.stage == "silu":
            return
        p.wait("sync", t_silu_xc)
        t_xcsp = dma("sync", "xcsp", self.xc_dram[:, :].rearrange("(c p) t -> p c t", p=128),
                     self.xcv[:, :, 0:256])
        p.wait("sync", t_xcsp)
        xdbl_pe = {0: None, 1: None}
        mm2 = None
        xc_rd = {0: None, 1: None}
        if self.dtin_used is not None:
            p.waits("tensor", self.dtin_used)
        if self.dt_done[5] is not None:
            p.waits("tensor", self.dt_done[5])
        for j in range(NDI):
            csl = self.xcbf[:, (j % 2) * TC:(j % 2) * TC + TC]
            rsl = self.xcring[:, (j % 2) * 256:(j % 2) * 256 + 256]
            p.wait("sync", xc_rd[j % 2])
            t_ld = dma("sync", "xcld%d" % (j % 2), rsl, self.xc_dram[j * 128:(j + 1) * 128, :])
            p.wait("vector", xdbl_pe[j % 2])
            p.wait("vector", t_ld)
            t_c16 = op("vector", lambda e, csl=csl, rsl=rsl: e.tensor_copy(csl, rsl))
            xc_rd[j % 2] = t_c16
            p.wait("tensor", t_c16)
            op("tensor", lambda e, j=j, csl=csl: e.matmul(
                pb[4][:, 0:256], self.wxv[:, j, 0:128], csl, start=(j == 0), stop=(j == NDI - 1)))
            mm2 = op("tensor", lambda e, j=j, csl=csl: e.matmul(
                pb[5][0:32, 0:256], self.wxv[:, j, 128:160], csl, start=(j == 0), stop=(j == NDI - 1)))
            xdbl_pe[j % 2] = mm2
        p.wait("vector", mm2)
        t_dtin = op("vector", lambda e: e.tensor_copy(self.dtinbf[:, :], pb[4][:, 0:256]))
        t_bcs = op("vector", lambda e: e.tensor_copy(self.bcsb[:, :], pb[5][0:32, 0:256]))
        self.tap("bc_b%dk%d" % (b, k), self.bcsb[:, :], [t_bcs])
        p.waits("sync", [t_bcs, (self.DV, self.DV.n)])
        t_bcd = dma("sync", "bc", self.bc_dram[:, :], self.bcsb[:, :])
        p.wait("sync", t_bcd)
        tBC = None
        for s in range(S):
            dma("sync", "bc", self.Bv[:, s, :], self.bc_dram[s:s + 1, :].broadcast_to((128, TC)))
            tBC = dma("sync", "bc", self.Cv[:, s, :], self.bc_dram[S + s:S + s + 1, :].broadcast_to((128, TC)))

        if self.stage == "xdbl":
            return
        yg_tok = None
        xc_scan_rd = {0: xc_rd[0], 1: xc_rd[1]}
        for j in range(self.cfg.get("NJ", NDI)):
            dsl = self.dtslot[:, (j % 2) * TC:(j % 2) * TC + TC]
            ssl = self.spslot[:, (j % 2) * TC:(j % 2) * TC + TC]
            usl = self.dtuslot[:, (j % 2) * TC:(j % 2) * TC + TC]
            dbk = 5 if j % 2 == 0 else 4
            if j < 2:
                p.wait("tensor", t_dtin)
                p.wait("tensor", t_bcs)
            if self.dt_done[dbk] is not None:
                p.waits("tensor", self.dt_done[dbk])
            t_mmdt = op("tensor", lambda e, j=j, dbk=dbk: e.matmul(
                pb[dbk][:, 0:256], self.wdtbuf[:, j * 128:(j + 1) * 128], self.dtinbf[:, :], start=True, stop=True))
            njop = self.cfg.get("NJOP", "all")
            if njop == "mm":
                self.dt_done[dbk] = [t_mmdt]
                continue
            p.wait("scalar", t_mmdt)
            t_av = op("scalar", lambda e, j=j, dbk=dbk, ssl=ssl: e.activation(
                out=ssl, in_=pb[dbk][:, 0:256], func=AF.Abs, bias=self.bdtbuf[:, j:j + 1]))
            if njop == "av":
                self.dt_done[dbk] = [t_av]
                continue
            p.wait("scalar", t_av)
            t_e1 = op("scalar", lambda e, ssl=ssl: e.activation(out=ssl, in_=ssl, func=AF.Exp, scale=-1.0))
            if njop == "e1":
                self.dt_done[dbk] = [t_e1]
                continue
            p.wait("scalar", t_e1)
            t_l = op("scalar", lambda e, ssl=ssl: e.activation(out=ssl, in_=ssl, func=AF.Ln, bias=self.onecol[:, 0:1]))
            if njop == "l":
                self.dt_done[dbk] = [t_l]
                continue
            p.wait("vector", t_mmdt)
            p.wait("vector", t_av)
            t_m = op("vector", lambda e, j=j, dbk=dbk, dsl=dsl: e.tensor_scalar(
                dsl, pb[dbk][:, 0:256], self.bdtbuf[:, j:j + 1], 0.0, op0=OP.add, op1=OP.max))
            if njop == "m":
                self.dt_done[dbk] = [t_av, t_m]
                continue
            p.wait("vector", t_l)
            t_dt = op("vector", lambda e, dsl=dsl, ssl=ssl: e.tensor_tensor(dsl, dsl, ssl, op=OP.add))
            self.dt_done[dbk] = [t_av, t_m]
            if njop == "dt":
                continue
            rsl = self.xcring[:, (j % 2) * 256:(j % 2) * 256 + 256]
            if self.stage != "scan_sp":
                p.wait("sync", xc_scan_rd[j % 2])
                t_ldxc = dma("sync", "xcld%d" % (j % 2), rsl, self.xc_dram[j * 128:(j + 1) * 128, :])
                p.wait("vector", t_ldxc)
                t_dtu = op("vector", lambda e, usl=usl, rsl=rsl, dsl=dsl: e.tensor_tensor(
                    usl, dsl, rsl, op=OP.mult))
            if j == 0:
                p.wait("scalar", tBC)
                p.wait("vector", tBC)
            if j == 0 and b == 0 and k == 0:
                self.tap("dt_b0k0", dsl, [t_dt])
            if self.stage in ("scan_dt", "scan_sp"):
                continue
            ybk = 6 if j % 2 == 0 else 7
            p.wait("tensor", self.y_tok[ybk])
            t_y = None
            for s in range(S):
                asl = self.dAslot[:, (s % 4) * TC:(s % 4) * TC + TC]
                xsl = self.xsslot[:, (s % 4) * TC:(s % 4) * TC + TC]
                tsl = self.tmpslot[:, (s % 4) * TC:(s % 4) * TC + TC]
                prev = (s - 4) % 16
                if prev in self.scan_tok:
                    p.wait("scalar", self.scan_tok[prev])
                    p.wait("vector", self.tmp_pe_tok.get(prev))
                p.wait("scalar", t_dt)
                t_dA = op("scalar", lambda e, asl=asl, dsl=dsl, j=j, s=s: e.activation(
                    out=asl, in_=dsl, func=AF.Exp, scale=self.Abuf[:, j * S + s:j * S + s + 1]))
                p.waits("vector", [t_dA, t_dtu])
                t_xs = op("vector", lambda e, xsl=xsl, usl=usl, s=s: e.tensor_tensor(
                    xsl, usl, self.Bv[:, s, :], op=OP.mult))
                p.wait("vector", self.st_prev)
                if self.stage == "scan_noh":
                    continue
                p.wait("vector", t_dA)
                t_h = op("vector", lambda e, s=s, asl=asl, xsl=xsl, j=j, b=b: e.tensor_tensor_scan(
                    self.hvv[:, s, :], asl, xsl,
                    self.hstate[:, b * 512 + j * 16 + s:b * 512 + j * 16 + s + 1],
                    op0=OP.mult, op1=OP.add))
                self.scan_tok[s] = t_h
                if self.stage == "scan_noy":
                    continue
                t_tmp = op("vector", lambda e, s=s, tsl=tsl: e.tensor_tensor(
                    tsl, self.hvv[:, s, :], self.Cv[:, s, :], op=OP.mult))
                p.wait("tensor", t_tmp)
                t_y = op("tensor", lambda e, s=s, tsl=tsl, ybk=ybk: e.matmul(
                    pb[ybk][:, 0:256], self.identbf[:, :], tsl, start=(s == 0), stop=(s == S - 1)))
                self.tmp_pe_tok[s] = t_y
            if self.stage in ("scan_noh", "scan_noy"):
                continue
            t_st = op("vector", lambda e, j=j, b=b: e.tensor_copy(
                self.hstate[:, b * 512 + j * 16:b * 512 + j * 16 + 16], self.hvv[:, :, 255:256]))
            self.st_prev = t_st
            p.waits("vector", [t_y, t_silu_z])
            gsl = self.yg1slot[:, (j % 2) * TC:(j % 2) * TC + TC]
            t_g1 = op("vector", lambda e, gsl=gsl, rsl=rsl, j=j, ybk=ybk: e.scalar_tensor_tensor(
                gsl, rsl, self.Dbuf[:, j:j + 1], pb[ybk][:, 0:256], op0=OP.mult, op1=OP.add))
            xc_scan_rd[j % 2] = t_g1
            yg_tok = op("vector", lambda e, gsl=gsl, j=j: e.tensor_tensor(
                self.ygv[:, j, :], gsl, self.zv[:, j, :], op=OP.mult))
            self.y_tok[ybk] = t_g1
            if j == 0 and b == 0 and k == 0:
                self.tap("y_b0k0", gsl, [t_g1])
        self.scan_done = yg_tok
        if self.stage in ("scan_dt", "scan_sp", "scan_noh", "scan_noy"):
            return

        if self.stage == "scan":
            return
        for dm in range(16):
            if self.wout_used is not None:
                p.wait("sync", self.wout_used)
            t_wo = dma("sync", "wo", self.wov[:, :, :],
                       self.wout_bf[:, dm * 128:(dm + 1) * 128].rearrange("(c p) o -> p c o", p=128))
            t_q = dma("sync", "q", self.qv[:, :, :],
                      self.qt_dram[:, dm * 128:(dm + 1) * 128].rearrange("(c p) o -> p c o", p=128))
            if dm == 0:
                p.wait("sync", self.q_prep_done)
            p.waits("tensor", [t_wo, t_q, yg_tok, xn_tok])
            rbk = 2 if dm % 2 == 0 else 3
            p.wait("tensor", self.resid_tok[rbk])
            mm = None
            for j in range(NDI):
                mm = op("tensor", lambda e, j=j, rbk=rbk: e.matmul(
                    pb[rbk][:, 0:256], self.wov[:, j, :], self.ygv[:, j, :],
                    start=(j == 0), stop=False))
            for i in range(NDM):
                mm = op("tensor", lambda e, i=i, rbk=rbk: e.matmul(
                    pb[rbk][:, 0:256], self.qv[:, i, :], self.xnv[:, i, 1:257],
                    start=False, stop=(i == NDM - 1)))
            p.wait("vector", mm)
            xsl = self.x1slot[:, (dm % 2) * 256:(dm % 2) * 256 + 256]
            p.wait("vector", self.x1_st[dm % 2])
            t_x1 = op("vector", lambda e, dm=dm, rbk=rbk, xsl=xsl: e.scalar_tensor_tensor(
                xsl, pb[rbk][:, 0:256], self.l0bbuf[:, dm:dm + 1], self.x0v[:, dm, 1:257],
                op0=OP.add, op1=OP.add))
            self.resid_tok[rbk] = t_x1
            p.wait("sync", t_x1)
            self.x1_st[dm % 2] = dma("sync", "x1st", self.x1_bdt[b, dm * 128:(dm + 1) * 128, t0:t0 + TC], xsl)
            self.wout_used = mm
        self.dtin_used = self.dt_done[4]


class L1Builder:
    def __init__(self, cfg):
        self.cfg = cfg
        self.NB = cfg.get("NB", 2)
        self.L = cfg.get("L", 2048)
        self.NT = self.L // 512
        self.NG = self.L // 512
        self.DM = 2048
        self.NDM = 16
        self.NH = 16
        self.NI = 64

    def build(self, env=None):
        if env is None:
            nc = bass.Bass("TRN2", target_bir_lowering=False)
            es = ExitStack()
            p = Prog(nc, es)
        else:
            nc, es, p = env["nc"], env["es"], env["p"]
        self.env = env
        self.nc, self.es, self.p = nc, es, p
        NB, L, NT, NDM, NH, NI = self.NB, self.L, self.NT, self.NDM, self.NH, self.NI

        if env is None:
            self.x1_bdt = nc.dram_tensor("x1_bdt", [NB, 2048, L], F32, kind="ExternalInput").ap()
        else:
            self.x1_bdt = env["x1_ap"]
        self.wq_t = nc.dram_tensor("wq_t", [NH, 2048, 128], F32, kind="ExternalInput").ap()
        self.wk_t = nc.dram_tensor("wk_t", [NH, 2048, 128], F32, kind="ExternalInput").ap()
        self.wv_t = nc.dram_tensor("wv_t", [NH, 2048, 128], F32, kind="ExternalInput").ap()
        self.wo_t = nc.dram_tensor("wo_t", [NH, 128, 2048], F32, kind="ExternalInput").ap()
        self.gate_wt = nc.dram_tensor("gate_wt", [2048, 8192], I32, kind="ExternalInput").ap()
        self.up_wt = nc.dram_tensor("up_wt", [2048, 8192], I32, kind="ExternalInput").ap()
        self.down_wt = nc.dram_tensor("down_wt", [8192, 2048], I32, kind="ExternalInput").ap()
        self.gsc_p = nc.dram_tensor("gsc_p", [128, NI], F32, kind="ExternalInput").ap()
        self.usc_p = nc.dram_tensor("usc_p", [128, NI], F32, kind="ExternalInput").ap()
        self.dsc_p = nc.dram_tensor("dsc_p", [128, NDM], F32, kind="ExternalInput").ap()
        self.out_bdt = nc.dram_tensor("out_bdt", [NB, 2048, L], F32, kind="ExternalOutput").ap()
        self.zt_dram = nc.dram_tensor("zt_scratch", [NH, 128, L], BF16).ap()
        if self.cfg.get("debug"):
            self.dbg_qt = nc.dram_tensor("dbg_qt", [128, L], F32, kind="ExternalOutput").ap()
            self.dbg_kt = nc.dram_tensor("dbg_kt", [128, L], F32, kind="ExternalOutput").ap()
            self.dbg_vt = nc.dram_tensor("dbg_vt", [128, 16 * 128], F32, kind="ExternalOutput").ap()
            self.dbg_p = nc.dram_tensor("dbg_p", [128, L], F32, kind="ExternalOutput").ap()
            self.dbg_pt = nc.dram_tensor("dbg_pt", [128, 16 * 512], F32, kind="ExternalOutput").ap()
        self.h_dram = nc.dram_tensor("h_scratch", [8192, L], BF16).ap()

        if env is None:
            def sb(name, shape, dt):
                return es.enter_context(nc.sbuf_tensor(name, shape, dt))
        else:
            sb = env["alloc"]

        self.R0 = sb("R0", [128, 16 * 2048], BF16)
        self.xn1 = self.R0[:, :]
        self.xn1v = self.xn1.rearrange("p (c t) -> p c t", t=2048)
        self.hbf = self.R0[:, :]
        self.hbv = self.hbf.rearrange("p (c t) -> p c t", t=2048)
        self.hbv2 = self.R0[:, :].rearrange("p (c t) -> p c t", t=512)
        self.R1 = sb("R1", [128, 16384], F32)
        r1b = self.R1[:, :].bitcast(BF16)
        self.x1ch = self.R1[:, 0:8192].rearrange("p (c t) -> p c t", t=512)
        self.sqsl = self.R1[:, 8192:9216]
        self.Sstrip = self.R1[:, 0:2048]
        self.VTf = self.R1[:, 2048:4096]
        self.QT = r1b[:, 8192:10240]
        self.KT = r1b[:, 10240:12288]
        self.VT = r1b[:, 12288:14336]
        self.Vtok = r1b[:, 14336:16384]
        self.Vtokv = self.Vtok.rearrange("p (c e) -> p c e", e=128)
        self.PT = r1b[:, 16384:24576]
        self.PTv = self.PT.rearrange("p (c t) -> p c t", t=512)
        self.zTtile = r1b[:, 24576:28672]
        self.zTl = r1b[:, 0:16384]
        self.zTlv = r1b[:, 0:8192].rearrange("p (c t) -> p c t", t=512)
        self.x1tile = self.R1[:, 8192:16384].rearrange("p (c t) -> p c t", t=512)

        self.R2 = sb("R2", [128, 8192], F32)
        r2b = self.R2[:, :].bitcast(BF16)
        self.wqb = r2b[:, 0:2048].rearrange("p (c e) -> p c e", e=128)
        self.wkb = r2b[:, 2048:4096].rearrange("p (c e) -> p c e", e=128)
        self.wvb = r2b[:, 4096:6144].rearrange("p (c e) -> p c e", e=128)
        self.gwb = r2b[:, 0:2048].rearrange("p (c e) -> p c e", e=128)
        self.uwb = r2b[:, 2048:4096].rearrange("p (c e) -> p c e", e=128)
        self.dwb = r2b[:, 4096:6144]
        self.wob = r2b[:, 6144:16384 + 0]
        self.wobv = r2b[:, 6144:10240].rearrange("p (c e) -> p c e", e=1024)

        self.gs = sb("gs", [128, 2048], BF16)
        self.up = sb("up", [128, 2048], BF16)
        self.outsl = sb("outsl", [128, 2 * 512], F32)
        self.statrow = sb("statrow1", [1, 4 * 512], F32)
        self.masks = sb("masks", [128, 4 * 512], F32)
        self.identbf = sb("identbf1", [128, 128], BF16)
        self.identf = sb("identf1", [128, 128], F32)
        self.pidx = sb("pidx1", [128, 1], F32)
        self.fidx = sb("fidx1", [128, 512], F32)
        self.vcol = sb("vcol1", [128, 4], F32)
        self.ones_col = sb("ones_col1", [128, 1], F32)
        self.ones_row = sb("ones_row1", [1, 128], F32)
        self.epscol = sb("epscol1", [128, 1], F32)
        self.nmax = sb("nmax1", [128, 1], F32)
        self.sums = sb("sums1", [128, 1], F32)
        self.recip = sb("recip1", [128, 1], F32)
        self.gscb = sb("gscb", [128, NI], F32)
        self.uscb = sb("uscb", [128, NI], F32)
        self.dscb = sb("dscb", [128, NDM], F32)

        self.pb = env["pb"] if env is not None else [
            es.enter_context(nc.psum_tensor("ps1_%d" % i, [128, 512], F32)) for i in range(8)]

        self.dsem = {n: p.sem("E_" + n, dma=True) for n in
                     ("x1", "wqkv", "wo", "gw", "dw0", "dw1", "zt", "h", "st", "res", "zl", "hl", "tapd")}
        self.PE = p.sem("F_PE")
        self.DV = p.sem("F_DV")
        self.AC = p.sem("F_AC")
        self.PL = p.sem("F_PL")
        self.esem = {"scalar": self.AC, "vector": self.DV, "tensor": self.PE, "gpsimd": self.PL}

        self.prologue()
        for b in range(NB):
            self.layer1_batch(b)

        for sm in self.dsem.values():
            p.wait("sync", (sm, sm.n))

        if env is not None:
            return nc
        block = es.enter_context(nc.Block())
        with es:
            p.finalize(block, strict=self.cfg.get("strict", False))
        return nc

    def dma(self, q, stream, out, in_, **kw):
        sem = self.dsem[stream]
        if q == "gpsimd" and "max_dma_last_dim" not in kw:
            kw = dict(kw, max_dma_last_dim=4096)
        return self.p.op(q, lambda e, out=out, in_=in_, kw=kw: e.dma_start(out=out, in_=in_, **kw), sem, 16)

    def op(self, eng, fn):
        return self.p.op(eng, fn, self.esem[eng], 1)

    def barrier(self, engs=("scalar", "vector", "tensor", "gpsimd", "sync")):
        toks = [(s, s.n) for s in (self.PE, self.DV, self.AC, self.PL)] + \
               [(s, s.n) for s in self.dsem.values()]
        for e in engs:
            self.p.waits(e, toks)

    def prologue(self):
        p, op, dma = self.p, self.op, self.dma
        t1 = op("gpsimd", lambda e: e.iota(self.pidx[:, :], [[0, 1]], base=0, channel_multiplier=1,
                                           allow_small_or_imprecise_dtypes=True))
        t2 = op("gpsimd", lambda e: e.iota(self.fidx[:, :], [[1, 512]], base=0, channel_multiplier=0,
                                           allow_small_or_imprecise_dtypes=True))
        op("gpsimd", lambda e: e.memset(self.ones_col[:, :], 1.0))
        op("gpsimd", lambda e: e.memset(self.ones_row[:, :], 1.0))
        t5 = op("gpsimd", lambda e: e.memset(self.epscol[:, :], 1e-5))
        p.waits("vector", [t1, t2, t5])
        op("vector", lambda e: e.tensor_scalar(self.identbf[:, :], self.fidx[:, 0:128], self.pidx[:, :],
                                               None, op0=OP.is_equal))
        op("vector", lambda e: e.tensor_scalar(self.identf[:, :], self.fidx[:, 0:128], self.pidx[:, :],
                                               None, op0=OP.is_equal))
        for v in range(4):
            t_vc = op("vector", lambda e, v=v: e.tensor_scalar(self.vcol[:, v:v + 1], self.pidx[:, :], float(v * 128),
                                                               None, op0=OP.add))
            p.wait("vector", t_vc)
            op("vector", lambda e, v=v: e.tensor_scalar(self.masks[:, v * 512:(v + 1) * 512], self.fidx[:, :],
                                                        self.vcol[:, v:v + 1], -1e9, op0=OP.is_gt, op1=OP.mult))
        dma("sync", "res", self.gscb[:, :], self.gsc_p[:, :])
        dma("sync", "res", self.uscb[:, :], self.usc_p[:, :])
        t = dma("sync", "res", self.dscb[:, :], self.dsc_p[:, :])
        p.waits("scalar", [t])
        p.waits("vector", [t])
        self.barrier()

    def ln(self, b):
        p, op, dma = self.p, self.op, self.dma
        pb = self.pb
        lnrd = None
        for ch in range(self.NT):
            t0 = ch * 512
            if lnrd is not None:
                p.wait("sync", lnrd)
            t_x = dma("sync", "x1", self.x1ch[:, :, :],
                      self.x1_bdt[b, :, t0:t0 + 512].rearrange("(c p) t -> p c t", p=128))
            p.waits("scalar", [t_x])
            p.waits("tensor", [t_x])
            if lnrd is not None:
                p.wait("tensor", lnrd)
            mm_x = mm_sq = None
            sq_tok = {}
            for c in range(16):
                sl = self.sqsl[:, (c % 2) * 512:(c % 2) * 512 + 512]
                if c >= 2:
                    p.wait("scalar", sq_tok[c - 2])
                t_sq = op("scalar", lambda e, c=c, sl=sl: e.activation(out=sl, in_=self.x1ch[:, c, :], func=AF.Square))
                p.wait("tensor", t_sq)
                mm_x = op("tensor", lambda e, c=c: e.matmul(pb[0][0:1, :], self.ones_col[:, :], self.x1ch[:, c, :],
                                                            start=(c == 0), stop=(c == 15)))
                mm_sq = op("tensor", lambda e, sl=sl, c=c: e.matmul(pb[1][0:1, :], self.ones_col[:, :], sl,
                                                                   start=(c == 0), stop=(c == 15)))
                sq_tok[c] = mm_sq
            p.waits("vector", [mm_x, mm_sq])
            mrow = self.statrow[0:1, 0:512]
            exrow = self.statrow[0:1, 512:1024]
            varrow = self.statrow[0:1, 1024:1536]
            rrow = self.statrow[0:1, 1536:2048]
            op("vector", lambda e: e.tensor_scalar(mrow, pb[0][0:1, :], 1.0 / 2048, None, op0=OP.mult))
            op("vector", lambda e: e.tensor_scalar(exrow, pb[1][0:1, :], 1.0 / 2048, None, op0=OP.mult))
            op("vector", lambda e: e.tensor_tensor(varrow, mrow, mrow, op=OP.mult))
            t_var = op("vector", lambda e: e.tensor_tensor(varrow, exrow, varrow, op=OP.subtract))
            p.wait("scalar", t_var)
            t_ln = op("scalar", lambda e: e.activation(out=rrow, in_=varrow, func=AF.Ln, bias=self.epscol[0:1, 0:1]))
            p.wait("scalar", t_ln)
            t_rstd = op("scalar", lambda e: e.activation(out=rrow, in_=rrow, func=AF.Exp, scale=-0.5))
            p.waits("tensor", [t_var, t_rstd])
            t_bm = op("tensor", lambda e: e.matmul(pb[0][:, :], self.ones_row[:, :], mrow, start=True, stop=True))
            t_br = op("tensor", lambda e: e.matmul(pb[1][:, :], self.ones_row[:, :], rrow, start=True, stop=True))
            p.waits("vector", [t_bm, t_br])
            for c in range(16):
                sl = self.sqsl[:, (c % 2) * 512:(c % 2) * 512 + 512]
                op("vector", lambda e, c=c, sl=sl: e.tensor_tensor(sl, self.x1ch[:, c, :], pb[0][:, :], op=OP.subtract))
                lnrd = op("vector", lambda e, c=c, sl=sl, t0=t0: e.tensor_tensor(
                    self.xn1v[:, c, t0:t0 + 512], sl, pb[1][:, :], op=OP.mult))
        self.barrier()

    def attn_head(self, b, h):
        p, op, dma = self.p, self.op, self.dma
        pb = self.pb
        self.strip_users = []
        NT, NG = self.NT, self.NG
        t_wq = dma("gpsimd", "wqkv", self.wqb[:, :, :],
                   self.wq_t[h].rearrange("(c p) e -> p c e", p=128))
        t_wk = dma("gpsimd", "wqkv", self.wkb[:, :, :],
                   self.wk_t[h].rearrange("(c p) e -> p c e", p=128))
        t_wv = dma("gpsimd", "wqkv", self.wvb[:, :, :],
                   self.wv_t[h].rearrange("(c p) e -> p c e", p=128))
        p.waits("tensor", [t_wq, t_wk, t_wv])
        for which, wbuf, dst in (("q", self.wqb, self.QT), ("k", self.wkb, self.KT)):
            for tt in range(NT):
                bk = tt % 2
                if self.pbk_tok[bk] is not None:
                    p.wait("tensor", self.pbk_tok[bk])
                mm = None
                for c in range(16):
                    mm = op("tensor", lambda e, c=c, tt=tt, bk=bk, wbuf=wbuf: e.matmul(
                        pb[bk][:, :], wbuf[:, c, :], self.xn1v[:, c, tt * 512:(tt + 1) * 512],
                        start=(c == 0), stop=(c == 15)))
                p.wait("vector", mm)
                t_ev = op("vector", lambda e, tt=tt, bk=bk, dst=dst: e.tensor_copy(
                    dst[:, tt * 512:(tt + 1) * 512], pb[bk][:, :]))
                self.pbk_tok[bk] = t_ev
        for tt in range(NT):
            bk = tt % 2
            p.wait("tensor", self.pbk_tok[bk])
            mm = None
            for c in range(16):
                mm = op("tensor", lambda e, c=c, tt=tt, bk=bk: e.matmul(
                    pb[bk][:, :], self.wvb[:, c, :], self.xn1v[:, c, tt * 512:(tt + 1) * 512],
                    start=(c == 0), stop=(c == 15)))
            p.wait("vector", mm)
            t_ev = op("vector", lambda e, tt=tt, bk=bk: e.tensor_copy(
                self.VTf[:, tt * 512:(tt + 1) * 512], pb[bk][:, :]))
            self.pbk_tok[bk] = t_ev
        vtf_done = (self.DV, self.DV.n)
        for g4 in range(NT):
            bk = 2 + (g4 % 2)
            if self.pbk_tok[bk] is not None:
                p.wait("tensor", self.pbk_tok[bk])
            p.wait("tensor", vtf_done)
            tr = None
            for q in range(4):
                ti = g4 * 4 + q
                tr = op("tensor", lambda e, ti=ti, q=q, bk=bk: e.transpose(
                    pb[bk][:, q * 128:(q + 1) * 128], self.VTf[:, ti * 128:(ti + 1) * 128], self.identf[:, :]))
            p.wait("vector", tr)
            t_ev = op("vector", lambda e, g4=g4, bk=bk: e.tensor_copy(
                self.Vtok[:, g4 * 512:(g4 + 1) * 512], pb[bk][:, :]))
            self.pbk_tok[bk] = t_ev
        for g in range(NG):
            zbk = 4 + (g % 2)
            if self.pbk_tok[zbk] is not None:
                p.wait("tensor", self.pbk_tok[zbk])
            nseg = g + 1
            for t1 in range(4):
                ti = g * 4 + t1
                p.waits("vector", self.strip_users)
                for seg in range(nseg):
                    bk = seg % 2
                    p.wait("tensor", self.pbk_tok[bk])
                    mm = op("tensor", lambda e, ti=ti, seg=seg, bk=bk: e.matmul(
                        pb[bk][:, :], self.QT[:, ti * 128:(ti + 1) * 128], self.KT[:, seg * 512:(seg + 1) * 512],
                        start=True, stop=True))
                    p.wait("vector", mm)
                    if seg == nseg - 1:
                        t_ev = op("vector", lambda e, seg=seg, bk=bk, t1=t1: e.scalar_tensor_tensor(
                            self.Sstrip[:, seg * 512:(seg + 1) * 512], pb[bk][:, :], SCALE,
                            self.masks[:, t1 * 512:(t1 + 1) * 512], op0=OP.mult, op1=OP.add))
                    else:
                        t_ev = op("vector", lambda e, seg=seg, bk=bk: e.tensor_scalar(
                            self.Sstrip[:, seg * 512:(seg + 1) * 512], pb[bk][:, :], SCALE, None, op0=OP.mult))
                    self.pbk_tok[bk] = t_ev
                Lq = nseg * 512
                t_mx = op("vector", lambda e, Lq=Lq: e.tensor_reduce(
                    self.nmax[:, :], self.Sstrip[:, 0:Lq], axis=AX.X, op=OP.max, negate=True))
                p.wait("scalar", t_mx)
                t_exp = op("scalar", lambda e, Lq=Lq: e.activation(
                    out=self.Sstrip[:, 0:Lq], in_=self.Sstrip[:, 0:Lq], func=AF.Exp,
                    bias=self.nmax[:, :], accum_out=self.sums[:, :]))
                p.wait("vector", t_exp)
                t_rc = op("vector", lambda e: e.reciprocal(self.recip[:, :], self.sums[:, :]))
                p.wait("vector", t_rc)
                t_p = op("vector", lambda e, Lq=Lq: e.tensor_scalar(
                    self.Sstrip[:, 0:Lq], self.Sstrip[:, 0:Lq], self.recip[:, :], None, op0=OP.mult))
                self.strip_users = [t_exp]
                for g4 in range(nseg):
                    bk = 2 + (g4 % 2)
                    p.wait("tensor", self.pbk_tok[bk])
                    p.wait("tensor", t_p)
                    tr = None
                    for q in range(4):
                        sg2 = g4 * 4 + q
                        tr = op("tensor", lambda e, sg2=sg2, q=q, bk=bk: e.transpose(
                            pb[bk][:, q * 128:(q + 1) * 128], self.Sstrip[:, sg2 * 128:(sg2 + 1) * 128],
                            self.identf[:, :]))
                    p.wait("vector", tr)
                    self.strip_users.append(tr)
                    t_ev = op("vector", lambda e, g4=g4, t1=t1, bk=bk: e.tensor_copy(
                        self.PTv[:, g4 * 4:g4 * 4 + 4, t1 * 128:(t1 + 1) * 128],
                        pb[bk][:, :].rearrange("p (c t) -> p c t", t=128)))
                    self.pbk_tok[bk] = t_ev
            p.wait("tensor", self.pbk_tok[2])
            p.wait("tensor", self.pbk_tok[3])
            mm = None
            for sg2 in range(nseg * 4):
                mm = op("tensor", lambda e, sg2=sg2, zbk=zbk, nseg=nseg: e.matmul(
                    pb[zbk][:, :], self.Vtokv[:, sg2, :], self.PTv[:, sg2, :],
                    start=(sg2 == 0), stop=(sg2 == nseg * 4 - 1)))
            p.wait("vector", mm)
            t_z = op("vector", lambda e, g=g, zbk=zbk: e.tensor_copy(
                self.zTtile[:, g * 512:(g + 1) * 512], pb[zbk][:, :]))
            self.pbk_tok[zbk] = t_z
        p.wait("sync", self.pbk_tok[4])
        p.wait("sync", self.pbk_tok[5])
        t_st = dma("sync", "zt", self.zt_dram[h][:, 0:self.L], self.zTtile[:, 0:self.L])
        if self.cfg.get("debug") and b == 0 and h == 0:
            self.barrier()
            self.dma("gpsimd", "tapd", self.dbg_qt[:, :], self.QT[:, :])
            self.dma("gpsimd", "tapd", self.dbg_kt[:, :], self.KT[:, :])
            self.dma("gpsimd", "tapd", self.dbg_vt[:, :], self.Vtok[:, :])
            self.dma("sync", "zt", self.dbg_p[:, :], self.Sstrip[:, :])
            self.dma("gpsimd", "tapd", self.dbg_pt[:, :], self.PT[:, :])
        return t_st

    def ffn_gateup(self, b):
        p, op, dma = self.p, self.op, self.dma
        pb = self.pb
        for i in range(self.NI):
            if self.gw_used is not None:
                p.wait("gpsimd", self.gw_used)
            t_g = dma("gpsimd", "gw", self.gwb[:, :, :],
                      self.gate_wt[:, i * 128:(i + 1) * 128].rearrange("(c p) e -> p c e", p=128))
            t_u = dma("gpsimd", "gw", self.uwb[:, :, :],
                      self.up_wt[:, i * 128:(i + 1) * 128].rearrange("(c p) e -> p c e", p=128))
            p.waits("vector", [t_g, t_u])
            t_sg = op("vector", lambda e: e.tensor_scalar(
                self.gwb[:, :, :], self.gwb[:, :, :], -128.0, None, op0=OP.add))
            t_su = op("vector", lambda e: e.tensor_scalar(
                self.uwb[:, :, :], self.uwb[:, :, :], -128.0, None, op0=OP.add))
            p.waits("tensor", [t_sg, t_su])
            for bk in range(8):
                if self.pbk_tok[bk] is not None:
                    p.wait("tensor", self.pbk_tok[bk])
            mmg = mmu = None
            for c in range(16):
                for tt in range(self.NT):
                    mmg = op("tensor", lambda e, c=c, tt=tt: e.matmul(
                        pb[tt][:, :], self.gwb[:, c, :], self.xn1v[:, c, tt * 512:(tt + 1) * 512],
                        start=(c == 0), stop=(c == 15)))
            for c in range(16):
                for tt in range(self.NT):
                    mmu = op("tensor", lambda e, c=c, tt=tt: e.matmul(
                        pb[4 + tt][:, :], self.uwb[:, c, :], self.xn1v[:, c, tt * 512:(tt + 1) * 512],
                        start=(c == 0), stop=(c == 15)))
            self.gw_used = mmu
            p.wait("scalar", mmg)
            if self.h_store is not None:
                p.wait("scalar", self.h_store)
            t_gs = None
            for tt in range(self.NT):
                t_gs = op("scalar", lambda e, tt=tt, i=i: e.activation(
                    out=self.gs[:, tt * 512:(tt + 1) * 512], in_=pb[tt][:, :], func=AF.Silu,
                    scale=self.gscb[:, i:i + 1]))
                self.pbk_tok[tt] = t_gs
            p.wait("vector", mmu)
            t_up = None
            for tt in range(self.NT):
                t_up = op("vector", lambda e, tt=tt, i=i: e.tensor_scalar(
                    self.up[:, tt * 512:(tt + 1) * 512], pb[4 + tt][:, :], self.uscb[:, i:i + 1],
                    None, op0=OP.mult))
                self.pbk_tok[4 + tt] = t_up
            p.waits("vector", [t_gs, t_up])
            if self.h_store is not None:
                p.wait("vector", self.h_store)
            t_h = op("vector", lambda e: e.tensor_tensor(self.gs[:, 0:self.L], self.gs[:, 0:self.L],
                                                         self.up[:, 0:self.L], op=OP.mult))
            p.wait("sync", t_h)
            self.h_store = dma("sync", "h", self.h_dram[i * 128:(i + 1) * 128, 0:self.L], self.gs[:, 0:self.L])
        self.barrier()

    def down_out(self, b):
        p, op, dma = self.p, self.op, self.dma
        pb = self.pb
        prev_mm = None
        prev_ev = None
        for tt in range(self.NT):
            if prev_mm is not None:
                p.wait("sync", prev_mm)
                p.wait("sync", prev_ev)
            t_hl = dma("sync", "hl", self.hbv2[:, :, :],
                       self.h_dram[:, tt * 512:(tt + 1) * 512].rearrange("(c p) t -> p c t", p=128))
            t_x1 = dma("sync", "x1", self.x1tile[:, :, :],
                       self.x1_bdt[b, :, tt * 512:(tt + 1) * 512].rearrange("(c p) t -> p c t", p=128))
            t_zl = dma("sync", "zl", self.zTlv[:, :, :],
                       self.zt_dram[:, :, tt * 512:(tt + 1) * 512].rearrange("c p t -> p c t"))
            p.waits("tensor", [t_hl, t_x1, t_zl])
            for half in range(2):
                dw_used = None
                for bk in range(8):
                    if self.pbk_tok[bk] is not None:
                        p.wait("tensor", self.pbk_tok[bk])
                mm = None
                for i in range(self.NI):
                    sl = self.dwb[:, (i % 2) * 1024:(i % 2) * 1024 + 1024]
                    if self.dw_used[i % 2] is not None:
                        p.wait("gpsimd", self.dw_used[i % 2])
                    t_dw = dma("gpsimd", "dw%d" % (i % 2), sl,
                               self.down_wt[i * 128:(i + 1) * 128, half * 1024:(half + 1) * 1024])
                    p.wait("vector", t_dw)
                    if self.dw_sub[i % 2] is not None:
                        p.wait("vector", self.dw_sub[i % 2])
                    t_sb = op("vector", lambda e, sl=sl: e.tensor_scalar(sl, sl, -128.0, None, op0=OP.add))
                    p.wait("tensor", t_sb)
                    for dmx in range(8):
                        mm = op("tensor", lambda e, i=i, dmx=dmx, sl=sl, tt=tt: e.matmul(
                            pb[dmx][:, :], sl[:, dmx * 128:(dmx + 1) * 128], self.hbv2[:, i, :],
                            start=(i == 0), stop=False))
                    self.dw_sub[i % 2] = mm
                    self.dw_used[i % 2] = mm
                for hb in range(4):
                    if self.wo_used is not None:
                        p.wait("gpsimd", self.wo_used)
                    t_wo = dma("gpsimd", "wo", self.wobv[:, :, :],
                               self.wo_t[hb * 4:(hb + 1) * 4, :, half * 1024:(half + 1) * 1024]
                               .rearrange("c p e -> p c e"))
                    p.wait("tensor", t_wo)
                    for hq in range(4):
                        h = hb * 4 + hq
                        for dmx in range(8):
                            last = (hb == 3 and hq == 3)
                            mm = op("tensor", lambda e, h=h, hq=hq, dmx=dmx, last=last: e.matmul(
                                pb[dmx][:, :], self.wobv[:, hq, dmx * 128:(dmx + 1) * 128],
                                self.zTlv[:, h, :], start=False, stop=last))
                    self.wo_used = mm
                p.wait("vector", mm)
                for dmx in range(8):
                    dm = half * 8 + dmx
                    osl = self.outsl[:, (dmx % 2) * 512:(dmx % 2) * 512 + 512]
                    p.wait("vector", self.out_st[dmx % 2])
                    t_o = op("vector", lambda e, dm=dm, dmx=dmx, osl=osl: e.scalar_tensor_tensor(
                        osl, pb[dmx][:, :], self.dscb[:, dm:dm + 1], self.x1tile[:, dm, :],
                        op0=OP.mult, op1=OP.add))
                    self.pbk_tok[dmx] = t_o
                    p.wait("sync", t_o)
                    self.out_st[dmx % 2] = dma("sync", "st",
                                               self.out_bdt[b, dm * 128:(dm + 1) * 128, tt * 512:(tt + 1) * 512],
                                               osl)
                prev_mm = mm
                prev_ev = t_o
        self.barrier()

    def layer1_batch(self, b):
        self.pbk_tok = {k: None for k in range(8)}
        self.gw_used = None
        self.h_store = None
        self.dw_used = {0: None, 1: None}
        self.dw_sub = {0: None, 1: None}
        self.wo_used = None
        self.out_st = {0: None, 1: None}
        self.ln(b)
        if self.cfg.get("stage") == "ln":
            return
        for h in range(self.NH):
            self.attn_head(b, h)
            self.barrier()
        if self.cfg.get("stage") == "attn":
            return
        self.ffn_gateup(b)
        if self.cfg.get("stage") == "gateup":
            return
        self.down_out(b)



def _vec_p(v, nch):
    return np.ascontiguousarray(np.asarray(v, np.float32).reshape(nch, 128).T)


_CACHE = {}

ARENA_F32 = 53200


def build_fused():
    nc = bass.Bass("TRN2", target_bir_lowering=False)
    es = ExitStack()
    p = Prog(nc, es)
    pb = [es.enter_context(nc.psum_tensor("pbk%d" % i, [128, 512], F32)) for i in range(8)]
    arena = es.enter_context(nc.sbuf_tensor("arena", [128, ARENA_F32], F32))
    arena_ap = arena[:, :]
    arena_bf = arena_ap.bitcast(BF16)
    x1_ap = nc.dram_tensor("x1_bdt", [2, 2048, 2048], F32).ap()
    off = [0]

    def alloc(name, shape, dt):
        nelem = shape[1]
        nf32 = (nelem + 1) // 2 if dt == BF16 else nelem
        a = off[0]
        off[0] = a + nf32 + ((-(a + nf32)) % 16)
        assert off[0] <= ARENA_F32, "arena overflow at %s (%d)" % (name, off[0])
        if dt == BF16:
            v = arena_bf[:, 2 * a:2 * a + nelem]
        else:
            v = arena_ap[:, a:a + nelem]
        if shape[0] != 128:
            v = v[0:shape[0], :]
        return v

    env = {"nc": nc, "es": es, "p": p, "pb": pb, "alloc": alloc, "x1_ap": x1_ap}
    b0 = L0Builder({"NB": 2, "L": 2048})
    b0.build(env)
    l0_toks = [(sm, sm.n) for sm in b0.dsem.values()] +               [(sm, sm.n) for sm in (b0.PE, b0.DV, b0.AC, b0.PL)]
    for e in ENGS:
        p.waits(e, l0_toks)
    off[0] = 0
    b1 = L1Builder({"NB": 2, "L": 2048})
    b1.build(env)
    block = es.enter_context(nc.Block())
    with es:
        p.finalize(block)
    return nc


def _get_prog():
    if "fused" not in _CACHE:
        _CACHE["fused"] = build_fused()
    return _CACHE["fused"]


def kernel(L0_attn_x, L0_attn_W_in, L0_attn_conv_w, L0_attn_conv_b, L0_attn_W_x,
           L0_attn_W_dt, L0_attn_b_dt, L0_attn_A_log, L0_attn_D, L0_attn_W_out,
           L0_ffn_weight, L0_ffn_bias, L1_attn_W_Q, L1_attn_W_K, L1_attn_W_V,
           L1_attn_W_O, L1_ffn_gate_w, L1_ffn_gate_scale, L1_ffn_up_w,
           L1_ffn_up_scale, L1_ffn_down_w, L1_ffn_down_scale):
    x = np.asarray(L0_attn_x, np.float32)
    NBC = 2
    n_cores = 8
    f32 = lambda a: np.ascontiguousarray(np.asarray(a, np.float32))
    i32 = lambda a: np.ascontiguousarray(np.asarray(a, np.int32))

    w_l0 = {
        "w_in": f32(L0_attn_W_in),
        "conv_w_p": np.ascontiguousarray(np.asarray(L0_attn_conv_w, np.float32)
                                         .reshape(32, 128, 3).transpose(1, 0, 2).reshape(128, 96)),
        "conv_b_p": _vec_p(L0_attn_conv_b, 32),
        "w_x": f32(L0_attn_W_x),
        "w_dt": f32(L0_attn_W_dt),
        "b_dt_p": _vec_p(L0_attn_b_dt, 32),
        "a_log_p": np.ascontiguousarray(np.asarray(L0_attn_A_log, np.float32)
                                        .reshape(32, 128, 16).transpose(1, 0, 2).reshape(128, 512)),
        "d_p": _vec_p(L0_attn_D, 32),
        "w_out": f32(L0_attn_W_out),
        "l0w_t": np.ascontiguousarray(np.asarray(L0_ffn_weight, np.float32).T),
        "l0b_p": _vec_p(L0_ffn_bias, 16),
    }
    w_l1 = {
        "wq_t": np.ascontiguousarray(np.asarray(L1_attn_W_Q, np.float32).transpose(0, 2, 1)),
        "wk_t": np.ascontiguousarray(np.asarray(L1_attn_W_K, np.float32).transpose(0, 2, 1)),
        "wv_t": np.ascontiguousarray(np.asarray(L1_attn_W_V, np.float32).transpose(0, 2, 1)),
        "wo_t": np.ascontiguousarray(np.asarray(L1_attn_W_O, np.float32).transpose(0, 2, 1)),
        "gate_wt": np.ascontiguousarray(np.asarray(L1_ffn_gate_w, np.int32).T),
        "up_wt": np.ascontiguousarray(np.asarray(L1_ffn_up_w, np.int32).T),
        "down_wt": np.ascontiguousarray(np.asarray(L1_ffn_down_w, np.int32).T),
        "gsc_p": _vec_p(L1_ffn_gate_scale, 64),
        "usc_p": _vec_p(L1_ffn_up_scale, 64),
        "dsc_p": _vec_p(L1_ffn_down_scale, 16),
    }

    ncf = _get_prog()
    in_maps = []
    for c in range(n_cores):
        xs = np.ascontiguousarray(x[c * NBC:(c + 1) * NBC].transpose(0, 2, 1))
        m = dict(w_l0, x_bdt=xs)
        m.update(w_l1)
        in_maps.append(m)
    res = run_bass_kernel_spmd(ncf, in_maps, core_ids=list(range(n_cores)))
    out = np.empty((16, 2048, 2048), np.float32)
    for c in range(n_cores):
        out[c * NBC:(c + 1) * NBC] = res.results[c]["out_bdt"].transpose(0, 2, 1)
    return out
